# version 15
# baseline (speedup 1.0000x reference)
from contextlib import ExitStack
import numpy as np
import ml_dtypes
import concourse.bass as bass
import concourse.mybir as mybir
from concourse.bass_utils import run_bass_kernel_spmd

F32 = mybir.dt.float32
BF16 = mybir.dt.bfloat16
I32 = mybir.dt.int32
ALU = mybir.AluOpType
AF = mybir.ActivationFunctionType
AX = mybir.AxisListType

NCORES = 8
R = 2048
NT = 16
D = 1024
KC = 8
NH = 16
HD = 64
EPS = 1e-5


class Sem:
    __slots__ = ("h", "n")

    def __init__(self, h):
        self.h = h
        self.n = 0


class Prog:
    ENGINES = ("sync", "scalar", "vector", "tensor", "gpsimd")

    def __init__(self, nc):
        self.nc = nc
        self.ops = {e: [] for e in self.ENGINES}

    def emit(self, ek, make, inc=None, inck=1):
        if inc is not None:
            def fn(eng, make=make, s=inc, k=inck):
                make(eng).then_inc(s.h, k)
            inc.n += inck
        else:
            fn = make
        self.ops[ek].append(fn)

    def dma(self, ek, out, in_, inc, **kw):
        self.emit(ek, lambda eng, o=out, i=in_, kw=kw: eng.dma_start(out=o, in_=i, **kw),
                  inc=inc, inck=16)

    def cc(self, kind, op, in_ap, out_ap, sem):
        def fn(eng, k=kind, o=op, i=in_ap, u=out_ap, s=sem):
            eng.collective_compute(k, o, replica_groups=[list(range(NCORES))],
                                   ins=[i], outs=[u]).then_inc(s.h)
        sem.n += 1
        self.ops["gpsimd"].append(fn)

    def wait(self, ek, sem, at=None):
        v = sem.n if at is None else at
        self.ops[ek].append(lambda eng, s=sem, vv=v: eng.wait_ge(s.h, vv))

    def drain(self, ek):
        self.ops[ek].append(lambda eng: eng.drain())

    def build(self):
        with self.nc.Block() as block:
            for ek in self.ENGINES:
                def runner(eng, ek=ek):
                    for fn in self.ops[ek]:
                        fn(eng)
                getattr(block, ek)(runner)


def _ln_tile(p, xs, xn_out, sa, stats, extra_absmax=None):
    cp = p.sems["lncp"]
    v, s = "vector", "scalar"
    m, var, sd, rstd, eps_col = stats["m"], stats["var"], stats["sd"], stats["rstd"], stats["eps_col"]
    p.emit(v, lambda e, a=xs, o=m: e.reduce_sum(out=o, in_=a, axis=AX.X))
    p.drain(v)
    p.emit(v, lambda e, o=m, a=m: e.tensor_scalar(out=o, in0=a, scalar1=1.0 / D, scalar2=None, op0=ALU.mult))
    p.drain(v)
    p.emit(v, lambda e, o=sa, a=xs, mm=m: e.tensor_scalar(out=o, in0=a, scalar1=mm, scalar2=None, op0=ALU.subtract), inc=cp)
    p.wait(s, cp)
    p.emit(s, lambda e, o=xn_out, a=sa: e.activation(out=o, in_=a, func=AF.Square), inc=cp)
    p.wait(v, cp)
    p.emit(v, lambda e, o=var, a=xn_out: e.reduce_sum(out=o, in_=a, axis=AX.X))
    p.drain(v)
    p.emit(v, lambda e, o=var, a=var: e.tensor_scalar(out=o, in0=a, scalar1=1.0 / D, scalar2=None, op0=ALU.mult), inc=cp)
    p.wait(s, cp)
    p.emit(s, lambda e, o=sd, a=var, b=eps_col: e.activation(out=o, in_=a, func=AF.Sqrt, bias=b), inc=cp)
    p.wait(v, cp)
    p.emit(v, lambda e, o=rstd, a=sd: e.reciprocal(out=o, in_=a))
    p.drain(v)
    p.emit(v, lambda e, o=xn_out, a=sa, rr=rstd: e.tensor_scalar(out=o, in0=a, scalar1=rr, scalar2=None, op0=ALU.mult), inc=cp)
    if extra_absmax is not None:
        p.drain(v)
        p.emit(v, lambda e, o=extra_absmax, a=xn_out: e.tensor_reduce(
            out=o, in_=a, axis=AX.X, op=ALU.max, apply_absolute_value=True))
        p.drain(v)


def emit_phase1(nc, p, octx, T, G):
    xin, w1t = T["xin"], T["w1t"]
    xn_o = T["xn_d"]
    phiq_fm_o, phik_fm_o = T["phiq_fm_d"], T["phik_fm_d"]
    v_tm_o = T["v_tm_d"]
    ssum_o, sc0_o = T["cc_st_in"], T["sc0_d"]
    maxp1_o = T["cc_m1_in"]

    with ExitStack() as ctx:
        sbuf = lambda n, sh, dt: ctx.enter_context(nc.sbuf_tensor("p1" + n, sh, dt))[:, :]
        psum = lambda n, sh, dt: ctx.enter_context(nc.psum_tensor("p1" + n, sh, dt))[:, :]
        sem = lambda n: octx.enter_context(nc.semaphore("p1" + n))
        xw = sbuf("xw", [128, NT * D], F32)
        sa = sbuf("sa", [128, D], F32)
        sb = sbuf("sb", [128, D], F32)
        stat = sbuf("stat", [128, 8], F32)
        pmx = sbuf("pmx", [128, NT], F32)
        mxc = sbuf("mxc", [128, 1], F32)
        mx1 = sbuf("mx1", [1, 128], F32)
        xnn_bf = sbuf("xnn_bf", [128, D], BF16)
        xnn_fm = sbuf("xnn_fm", [128, KC * R], BF16)
        w1b = sbuf("w1b", [128, KC * 3 * D], BF16)
        idb, idf = G["idb"], G["idf"]
        phr = sbuf("phr", [128, D], F32)
        phe = sbuf("phe", [128, D], F32)
        phq_bf = sbuf("phq_bf", [128, D], BF16)
        phk_bf = sbuf("phk_bf", [128, D], BF16)
        v_bf = sbuf("v_bf", [128, NH * (HD + 1)], BF16)
        tstage = sbuf("tstage", [128, D], BF16)
        scst = sbuf("scst", [64, 2 * NH * 65], F32)
        ps_qkv = psum("ps_qkv", [128, D], F32)
        ps_t = psum("ps_t", [128, D], BF16)
        ps_scd0 = psum("ps_scd0", [64, 512], F32)
        ps_scd1 = psum("ps_scd1", [64, 512], F32)
        ps_scz = psum("ps_scz", [64, 16], F32)
        ps_mx = psum("ps_mx", [1, 128], F32)
        s_ld = sem("s_ld"); s_mm = sem("s_mm")
        s_ev = sem("s_ev"); s_st = sem("s_st"); s_aux = sem("s_aux")

        ld = Sem(s_ld); mm = Sem(s_mm); ev = Sem(s_ev); st = Sem(s_st); aux = Sem(s_aux)
        stats = {"m": stat[:, 0:1], "var": stat[:, 1:2], "sd": stat[:, 2:3],
                 "rstd": stat[:, 3:4], "eps_col": stat[:, 4:5]}

        p.emit("vector", lambda e: e.memset(stat[:, 4:5], EPS), inc=p.sems["lncp"])
        for t in range(NT):
            p.dma("sync", xw[:, t * D:(t + 1) * D], xin[t * 128:(t + 1) * 128, :], ld)
        for c in range(KC):
            p.dma("gpsimd", w1b[:, c * 3 * D:(c + 1) * 3 * D], w1t[c * 128:(c + 1) * 128, :], ld)
        ld_all = ld.n

        p.wait("vector", ld, ld_all)
        p.wait("scalar", ld, ld_all)
        for t in range(NT):
            xt = xw[:, t * D:(t + 1) * D]
            _ln_tile(p, xt, sb, sa, stats, extra_absmax=pmx[:, t:t + 1])
            p.emit("vector", lambda e, o=xt, a=sb: e.tensor_copy(out=o, in_=a), inc=p.sems["lncp"])
            _ln_tile(p, xt, sb, sa, stats)
            p.emit("vector", lambda e, o=xnn_bf, a=sb: e.tensor_copy(out=o, in_=a), inc=p.sems["lncp"])
            cpv = p.sems["lncp"].n
            p.wait("sync", p.sems["lncp"], cpv)
            p.dma("sync", xn_o[t * 128:(t + 1) * 128, :], xt, st)
            p.wait("tensor", p.sems["lncp"], cpv)
            if t > 0:
                p.wait("tensor", ev)
            for c in range(KC):
                p.emit("tensor", lambda e, c=c: e.transpose(
                    ps_t[:, c * 128:(c + 1) * 128], xnn_bf[:, c * 128:(c + 1) * 128], idb[:, :]),
                    inc=(mm if c == KC - 1 else None))
            p.wait("scalar", mm)
            p.emit("scalar", lambda e, t=t: e.activation(
                out=xnn_fm.rearrange("p (c j) -> p c j", j=R)[:, :, t * 128:(t + 1) * 128],
                in_=ps_t.rearrange("p (c j) -> p c j", j=128), func=AF.Copy), inc=ev)

        for t in range(NT):
            b, ch = t // 2, t % 2
            for part in range(3):
                p.wait("tensor", ev)
                for c in range(KC):
                    lhs = xnn_fm.rearrange("p (c j) -> p c j", j=R)[:, c, t * 128:(t + 1) * 128]
                    for jn in range(2):
                        rhs = w1b[:, c * 3 * D + part * D + jn * 512: c * 3 * D + part * D + (jn + 1) * 512]
                        p.emit("tensor", lambda e, l=lhs, r=rhs, jn=jn, c=c: e.matmul(
                            ps_qkv[:, jn * 512:(jn + 1) * 512], l, r,
                            start=(c == 0), stop=(c == KC - 1)),
                            inc=(mm if c == KC - 1 and jn == 1 else None))
                p.wait("scalar", mm)
                p.wait("vector", mm)
                if part < 2:
                    p.emit("scalar", lambda e: e.activation(out=phr[:, :], in_=ps_qkv[:, :], func=AF.Relu), inc=ev)
                    p.wait("vector", ev)
                    p.emit("vector", lambda e: e.tensor_tensor(out=phe[:, :], in0=ps_qkv[:, :], in1=phr[:, :], op=ALU.subtract), inc=ev)
                    p.wait("scalar", ev)
                    p.emit("scalar", lambda e: e.activation(out=phe[:, :], in_=phe[:, :], func=AF.Exp), inc=ev)
                    p.wait("vector", ev)
                    dst = phq_bf if part == 0 else phk_bf
                    p.emit("vector", lambda e, d=dst: e.tensor_tensor(out=d[:, :], in0=phe[:, :], in1=phr[:, :], op=ALU.add), inc=ev)
                else:
                    p.emit("vector", lambda e: e.memset(
                        v_bf.rearrange("p (h e) -> p h e", e=HD + 1)[:, :, HD:HD + 1], 1.0))
                    p.emit("vector", lambda e: e.tensor_copy(
                        out=v_bf.rearrange("p (h e) -> p h e", e=HD + 1)[:, :, 0:HD],
                        in_=ps_qkv.rearrange("p (h d) -> p h d", d=HD)), inc=ev)
                p.wait("tensor", ev)

            for src, dst_fm in ((phq_bf, phiq_fm_o), (phk_bf, phik_fm_o)):
                p.wait("tensor", ev)
                for c in range(KC):
                    p.emit("tensor", lambda e, s=src, c=c: e.transpose(
                        ps_t[:, c * 128:(c + 1) * 128], s[:, c * 128:(c + 1) * 128], idb[:, :]),
                        inc=(mm if c == KC - 1 else None))
                p.wait("scalar", mm)
                p.wait("scalar", st)
                p.emit("scalar", lambda e: e.activation(out=tstage[:, :], in_=ps_t[:, :], func=AF.Copy), inc=ev)
                p.wait("sync", ev)
                p.dma("sync", dst_fm.rearrange("p (b c j) -> p b c j", b=8, j=256)[
                          :, t // 2, :, (t % 2) * 128:(t % 2) * 128 + 128],
                      tstage.rearrange("p (c j) -> p c j", j=128), st)

            p.wait("tensor", ev)
            for hh in range(NH):
                bank = ps_scd0 if hh < 8 else ps_scd1
                col = (hh % 8) * 64
                lhs = phk_bf[:, hh * 64:(hh + 1) * 64]
                rhs = v_bf[:, hh * 65:hh * 65 + 64]
                p.emit("tensor", lambda e, bk=bank, co=col, l=lhs, r=rhs: e.matmul(
                    bk[:, co:co + 64], l, r, start=True, stop=True))
                rhs1 = v_bf[:, hh * 65 + 64:hh * 65 + 65]
                p.emit("tensor", lambda e, hh=hh, l=lhs, r=rhs1: e.matmul(
                    ps_scz[:, hh:hh + 1], l, r, start=True, stop=True),
                    inc=(mm if hh == NH - 1 else None))
            p.wait("vector", mm)
            scv = scst.rearrange("p (c x e) -> p c x e", c=2, e=65)
            p.emit("vector", lambda e, ch=ch, scv=scv: e.tensor_copy(
                out=scv[0:64, ch, 0:8, 0:64],
                in_=ps_scd0.rearrange("p (x d) -> p x d", d=64)))
            p.emit("vector", lambda e, ch=ch, scv=scv: e.tensor_copy(
                out=scv[0:64, ch, 8:16, 0:64],
                in_=ps_scd1.rearrange("p (x d) -> p x d", d=64)))
            p.emit("vector", lambda e, ch=ch, scv=scv: e.tensor_copy(
                out=scv[0:64, ch, 0:16, 64:65],
                in_=ps_scz.rearrange("p (x o) -> p x o", o=1)), inc=ev)
            p.wait("tensor", ev)
            if ch == 1:
                blk = NH * 65
                p.wait("sync", ev)
                p.dma("sync", sc0_o[0:64, b * blk:(b + 1) * blk], scst[0:64, 0:blk], st)
                p.wait("vector", st)
                p.emit("vector", lambda e, blk=blk: e.tensor_tensor(
                    out=scst[0:64, 0:blk], in0=scst[0:64, 0:blk],
                    in1=scst[0:64, blk:2 * blk], op=ALU.add), inc=ev)
                p.wait("sync", ev)
                p.dma("sync", ssum_o[0:64, b * blk:(b + 1) * blk], scst[0:64, 0:blk], st)

            p.wait("sync", ev)
            p.dma("sync", v_tm_o[t * 128:(t + 1) * 128, :], v_bf[:, :], st)
            p.wait("scalar", st)
            p.wait("vector", st)

        p.emit("vector", lambda e: e.reduce_max(out=mxc[:, :], in_=pmx[:, :], axis=AX.X), inc=ev)
        p.wait("tensor", ev)
        p.emit("tensor", lambda e: e.transpose(ps_mx[0:1, 0:128], mxc[:, 0:1], idf[:, :]), inc=mm)
        p.wait("vector", mm)
        p.emit("vector", lambda e: e.tensor_copy(out=mx1[:, :], in_=ps_mx[0:1, :]))
        p.drain("vector")
        p.emit("vector", lambda e: e.reduce_max(out=stat[0:1, 5:6], in_=mx1[0:1, :], axis=AX.X), inc=ev)
        p.wait("sync", ev)
        p.dma("sync", maxp1_o, stat[0:1, 5:6], st)

        p.wait("gpsimd", st)
        p.cc("AllReduce", ALU.max, T["cc_m1_in"], T["cc_m1_out"], G["ccs"])
        p.cc("AllGather", ALU.bypass, T["cc_st_in"], T["cc_st_out"], G["ccs"])
        return [ld, mm, ev, st, aux]


def _prep_host(inputs):
    h = {}
    x = np.ascontiguousarray(inputs["L0_attn_x"], dtype=np.float32)
    h["x_slices"] = [np.ascontiguousarray(x[:, c * 256:(c + 1) * 256, :].reshape(R, D))
                     for c in range(NCORES)]
    h["w1t"] = np.ascontiguousarray(inputs["L0_attn_W1"].T)
    h["identb"] = np.eye(128).astype(ml_dtypes.bfloat16)
    h["identf"] = np.eye(128, dtype=np.float32)
    h["triu"] = np.triu(np.ones((128, 128), np.float32))
    h["w2t"] = np.ascontiguousarray(inputs["L0_attn_W2"].T)
    h["wup"] = np.ascontiguousarray(inputs["L0_ffn_w_up"])
    h["wgate"] = np.ascontiguousarray(inputs["L0_ffn_w_gate"])
    h["wupt"] = np.ascontiguousarray(inputs["L0_ffn_w_up"].T)
    h["wgatet"] = np.ascontiguousarray(inputs["L0_ffn_w_gate"].T)
    h["wdown"] = np.ascontiguousarray(inputs["L0_ffn_w_down"])
    perm = np.zeros(3072, np.int64)
    for j in range(3):
        for hh in range(16):
            for d in range(64):
                perm[j * 1024 + hh * 64 + d] = hh * 192 + j * 64 + d
    def _swz(xT, K, OC, J):
        return np.ascontiguousarray(
            xT.reshape(K, 128, OC, J).transpose(2, 1, 0, 3).reshape(OC * 128, K * J))
    h["ipw_t"] = _swz(inputs["L1_attn_in_proj_weight"][perm, :].T, 8, 6, 512)
    h["ipb_row"] = np.ascontiguousarray(inputs["L1_attn_in_proj_bias"][perm][None, :])
    h["opw_t"] = _swz(inputs["L1_attn_out_proj_weight"].T, 8, 8, 128)
    h["opb_row"] = np.ascontiguousarray(inputs["L1_attn_out_proj_bias"][None, :])
    h["rt_t"] = _swz(inputs["L1_ffn_router_w"].T, 8, 1, 32)
    h["rb_row"] = np.ascontiguousarray(inputs["L1_ffn_router_bias"][None, :])
    h["gat"] = _swz(inputs["L1_ffn_gate_w"].reshape(4096, 1024).T, 8, 32, 128)
    h["upt"] = _swz(inputs["L1_ffn_up_w"].reshape(4096, 1024).T, 8, 32, 128)
    h["dat"] = _swz(inputs["L1_ffn_down_w"].transpose(0, 2, 1).reshape(4096, 1024), 32, 8, 128)
    h["sgt"] = _swz(inputs["L1_ffn_sgate_w"].T, 8, 4, 128)
    h["sut"] = _swz(inputs["L1_ffn_sup_w"].T, 8, 4, 128)
    h["sdt"] = _swz(inputs["L1_ffn_sdown_w"].T, 4, 8, 128)
    pm = np.zeros((NCORES, 64, 8), np.float32)
    for c in range(NCORES):
        pm[c, :, :c] = 1.0
    h["pmask"] = pm
    return h


def build_fused():
    nc = bass.Bass("TRN2", target_bir_lowering=False)
    ap = lambda n, sh, dt, kind: nc.dram_tensor(n, sh, dt, kind=kind).ap()
    T = {}
    T["xin"] = ap("xin", [R, D], F32, "ExternalInput")
    T["w1t"] = ap("w1t", [D, 3 * D], F32, "ExternalInput")
    T["w2t"] = ap("w2t", [D, D], F32, "ExternalInput")
    T["wup"] = ap("wup", [3 * D, D], F32, "ExternalInput")
    T["wgate"] = ap("wgate", [3 * D, D], F32, "ExternalInput")
    T["wdown"] = ap("wdown", [D, 3 * D], F32, "ExternalInput")
    T["ipw_t"] = ap("ipw_t", [768, 4096], F32, "ExternalInput")
    T["ipb_row"] = ap("ipb_row", [1, 3 * D], F32, "ExternalInput")
    T["opw_t"] = ap("opw_t", [1024, 1024], F32, "ExternalInput")
    T["opb_row"] = ap("opb_row", [1, D], F32, "ExternalInput")
    T["rt_t"] = ap("rt_t", [128, 256], F32, "ExternalInput")
    T["rb_row"] = ap("rb_row", [1, 32], F32, "ExternalInput")
    T["gat"] = ap("gat", [4096, 1024], F32, "ExternalInput")
    T["upt"] = ap("upt", [4096, 1024], F32, "ExternalInput")
    T["dat"] = ap("dat", [1024, 4096], F32, "ExternalInput")
    T["sgt"] = ap("sgt", [512, 1024], F32, "ExternalInput")
    T["sut"] = ap("sut", [512, 1024], F32, "ExternalInput")
    T["sdt"] = ap("sdt", [1024, 512], F32, "ExternalInput")
    T["identb"] = ap("identb", [128, 128], BF16, "ExternalInput")
    T["identf"] = ap("identf", [128, 128], F32, "ExternalInput")
    T["triu"] = ap("triu", [128, 128], F32, "ExternalInput")
    T["pmask_i"] = ap("pmask_i", [64, 8], F32, "ExternalInput")
    T["out_o"] = ap("out_o", [R, D], F32, "ExternalOutput")
    T["xn_d"] = ap("xn_d", [R, D], F32, "Internal")
    T["phiq_fm_d"] = ap("phiq_fm_d", [128, KC * R], BF16, "Internal")
    T["phik_fm_d"] = ap("phik_fm_d", [128, KC * R], BF16, "Internal")
    T["v_tm_d"] = ap("v_tm_d", [R, NH * (HD + 1)], BF16, "Internal")
    T["sc0_d"] = ap("sc0_d", [64, 128 * 65], F32, "Internal")
    T["xa_d"] = ap("xa_d", [R, D], F32, "Internal")
    T["up_raw_d"] = ap("up_raw_d", [128, 24 * R], BF16, "Internal")
    T["gate_raw_d"] = ap("gate_raw_d", [128, 24 * R], BF16, "Internal")
    T["upq_d"] = ap("upq_d", [128, 24 * R], BF16, "Internal")
    T["t_d"] = ap("t_d", [128, 24 * R], BF16, "Internal")
    T["oraw_d"] = ap("oraw_d", [128, 24 * R], BF16, "Internal")
    T["cc_st_in"] = ap("cc_st_in", [64, 128 * 65], F32, "Internal")
    T["cc_st_out"] = nc.dram_tensor("cc_st_out", [512, 128 * 65], F32,
                                    addr_space="Shared").ap()
    T["cc_m1_in"] = ap("cc_m1_in", [1, 1], F32, "Internal")
    T["cc_m1_out"] = nc.dram_tensor("cc_m1_out", [1, 1], F32, addr_space="Shared").ap()
    T["cc_m23_in"] = ap("cc_m23_in", [1, 2], F32, "Internal")
    T["cc_m23_out"] = nc.dram_tensor("cc_m23_out", [1, 2], F32, addr_space="Shared").ap()
    T["cc_m4_in"] = ap("cc_m4_in", [1, 1], F32, "Internal")
    T["cc_m4_out"] = nc.dram_tensor("cc_m4_out", [1, 1], F32, addr_space="Shared").ap()
    T["cc_m5_in"] = ap("cc_m5_in", [1, 1], F32, "Internal")
    T["cc_m5_out"] = nc.dram_tensor("cc_m5_out", [1, 1], F32, addr_space="Shared").ap()
    T["scr_t"] = nc.dram_tensor("scr", [1, 16], F32)
    T["wt_dram"] = nc.dram_tensor("wt_dram", [32, R], F32)

    with ExitStack() as octx:
        sbuf_o = lambda n, sh, dt: octx.enter_context(nc.sbuf_tensor(n, sh, dt))[:, :]
        G = {}
        G["idb"] = sbuf_o("g_idb", [128, 128], BF16)
        G["idf"] = sbuf_o("g_idf", [128, 128], F32)
        G["bardum"] = sbuf_o("g_bardum", [128, 8], F32)
        p = Prog(nc)
        p.sems = {"lncp": Sem(octx.enter_context(nc.semaphore("s_lncp")))}
        G["ccs"] = Sem(octx.enter_context(nc.semaphore("s_cc")))
        gbar = Sem(octx.enter_context(nc.semaphore("s_gbar")))
        g_ld = Sem(octx.enter_context(nc.semaphore("s_gld")))

        p.dma("sync", G["idb"], T["identb"], g_ld)
        p.dma("sync", G["idf"], T["identf"], g_ld)
        for ek in Prog.ENGINES:
            p.wait(ek, g_ld)

        def gbarrier(sems):
            _phase_barrier(p, sems + [p.sems["lncp"]],
                           G["bardum"][0:1, 0:1], G["bardum"][0:1, 1:2],
                           G["bardum"][0:1, 2:3], gbar)

        sems1 = emit_phase1(nc, p, octx, T, G)
        gbarrier(sems1)
        sems2 = emit_phase2(nc, p, octx, T, G)
        gbarrier(sems2)
        sems3 = emit_phase3(nc, p, octx, T, G)
        gbarrier(sems3)
        sems4 = emit_phase4(nc, p, octx, T, G)
        gbarrier(sems4)
        emit_phase5(nc, p, octx, T, G)
        p.build()
    return nc


def kernel(**inputs):
    h = _prep_host(inputs)
    cores = list(range(NCORES))
    nc = build_fused()
    im = [{
        "xin": h["x_slices"][c], "w1t": h["w1t"], "w2t": h["w2t"],
        "wup": h["wup"], "wgate": h["wgate"], "wdown": h["wdown"],
        "ipw_t": h["ipw_t"], "ipb_row": h["ipb_row"],
        "opw_t": h["opw_t"], "opb_row": h["opb_row"],
        "rt_t": h["rt_t"], "rb_row": h["rb_row"],
        "gat": h["gat"], "upt": h["upt"], "dat": h["dat"],
        "sgt": h["sgt"], "sut": h["sut"], "sdt": h["sdt"],
        "identb": h["identb"], "identf": h["identf"], "triu": h["triu"],
        "pmask_i": h["pmask"][c],
    } for c in cores]
    r = run_bass_kernel_spmd(nc, im, core_ids=cores).results
    out = np.zeros((8, 2048, 1024), np.float32)
    for c in cores:
        out[:, c * 256:(c + 1) * 256, :] = r[c]["out_o"].reshape(8, 256, 1024)
    return out


def emit_phase2(nc, p, octx, T, G):
    xin = T["xin"]
    xn_i = T["xn_d"]
    phiq_fm, phik_fm = T["phiq_fm_d"], T["phik_fm_d"]
    v_tm = T["v_tm_d"]
    sc0_i = T["sc0_d"]
    st_ag = T["cc_st_out"]
    w2t, wup, wgate = T["w2t"], T["wup"], T["wgate"]
    xa_o = T["xa_d"]
    up_raw_o, gate_raw_o = T["up_raw_d"], T["gate_raw_d"]
    maxp2_o = T["cc_m23_in"][0:1, 0:1]
    maxp3_o = T["cc_m23_in"][0:1, 1:2]
    scr = T["scr_t"]

    BLK = NH * 65

    with ExitStack() as ctx:
        sbuf = lambda n, sh, dt: ctx.enter_context(nc.sbuf_tensor("p2" + n, sh, dt))[:, :]
        psum = lambda n, sh, dt: ctx.enter_context(nc.psum_tensor("p2" + n, sh, dt))[:, :]
        sem = lambda n: octx.enter_context(nc.semaphore("p2" + n))
        sinf = sbuf("sinf", [64, BLK], F32)
        smidf = sbuf("smidf", [64, BLK], F32)
        stg = sbuf("stg", [64, BLK], F32)
        sin_all = sbuf("sin_all", [128, 8 * 8 * 65], BF16)
        smid_all = sbuf("smid_all", [128, 8 * 8 * 65], BF16)
        pq_b = sbuf("pq_b", [128, 2 * KC * 256], BF16)
        pk_b = sbuf("pk_b", [128, 2 * KC * 256], BF16)
        v_b = sbuf("v_b", [128, 2 * 2 * BLK], BF16)
        att_bf = sbuf("att_bf", [128, 2 * 512], BF16)
        rec4 = sbuf("rec4", [128, 8], F32)
        ctxt = sbuf("ctxt", [128, D], F32)
        ctx_bf = sbuf("ctx_bf", [128, D], BF16)
        ctx_fm = sbuf("ctx_fm", [128, NT * D], BF16)
        w2b = sbuf("w2b", [128, KC * D], BF16)
        w2o = sbuf("w2o", [128, 512], F32)
        xa_st = sbuf("xa_st", [128, 4 * D], F32)
        xnt = sbuf("xnt", [128, 2 * D], F32)
        xq_i32 = sbuf("xq_i32", [128, D], I32)
        dm_s = sbuf("dm_s", [128, D], F32)
        rr_s = sbuf("rr_s", [128, D], F32)
        xq_bf = sbuf("xq_bf", [128, D], BF16)
        xq_fm = sbuf("xq_fm", [128, KC * R], BF16)
        wtile = sbuf("wtile", [128, 2 * D], F32)
        bmx = sbuf("bmx", [128, 2 * 32], F32)
        sblk = sbuf("sblk", [128, 2 * 32], F32)
        rsblk = sbuf("rsblk", [128, 2 * 32], F32)
        yw = sbuf("yw", [128, 2 * D], F32)
        runm = sbuf("runm", [128, 2], F32)
        cols = sbuf("cols", [128, 4], F32)
        m1sc = sbuf("m1sc", [1, 8], F32)
        mx1 = sbuf("mx1", [1, 128], F32)
        trm = sbuf("trm", [128, 128], F32)
        pmaskb = sbuf("pmask", [64, 8], F32)
        idb, idf = G["idb"], G["idf"]
        pb0 = psum("pb0", [128, 512], F32)
        pb1 = psum("pb1", [128, 512], F32)
        pb2 = psum("pb2", [128, 512], F32)
        pb3 = psum("pb3", [128, 512], F32)
        ps_t = psum("ps_t", [128, D], BF16)
        ps_tf = psum("ps_tf", [128, 512], F32)
        ps_mx = psum("ps_mx", [1, 128], F32)
        s_ld = sem("s_ld"); s_mm = sem("s_mm"); s_ev = sem("s_ev")
        s_st = sem("s_st"); s_aux = sem("s_aux"); s_cp = sem("s_cp")
        slot_sems = [sem(f"s2_sl{i}") for i in range(6)]

        ld = Sem(s_ld); mm = Sem(s_mm); ev = Sem(s_ev); st = Sem(s_st)
        aux = Sem(s_aux); cp = Sem(s_cp)
        sA = [Sem(slot_sems[0]), Sem(slot_sems[1])]
        sW = [Sem(slot_sems[2]), Sem(slot_sems[3])]
        sX = [Sem(slot_sems[4]), Sem(slot_sems[5])]

        p.dma("sync", trm, T["triu"], ld)
        p.dma("sync", pmaskb, T["pmask_i"], ld)

        p.wait("sync", G["ccs"], at=1)
        p.dma("sync", m1sc[0:1, 0:1], T["cc_m1_out"], ld)
        ld0 = ld.n
        p.wait("vector", ld, ld0)
        p.emit("vector", lambda e: e.tensor_scalar(out=m1sc[0:1, 1:2], in0=m1sc[0:1, 0:1],
                                                   scalar1=1.0 / 32767.0, scalar2=1e-12, op0=ALU.mult, op1=ALU.max))
        p.drain("vector")
        p.emit("vector", lambda e: e.reciprocal(out=m1sc[0:1, 2:3], in_=m1sc[0:1, 1:2]), inc=cp)
        p.wait("sync", cp)
        p.dma("sync", scr.ap()[0:1, 0:2], m1sc[0:1, 1:3], aux)
        p.wait("gpsimd", aux)
        p.dma("gpsimd", cols[:, 0:1], scr.ap()[0:1, 0:1].broadcast_to([128, 1]), ld)
        p.dma("gpsimd", cols[:, 1:2], scr.ap()[0:1, 1:2].broadcast_to([128, 1]), ld)
        p.emit("vector", lambda e: e.memset(cols[:, 2:3], 32768.0))
        p.emit("vector", lambda e: e.memset(runm, 0.0), inc=cp)

        p.wait("scalar", ld, ld.n)
        p.wait("scalar", cp, cp.n)
        p.wait("vector", ld, ld.n)
        xn_free = [0, 0]
        for t in range(NT):
            sl = t % 2
            p.wait("sync", cp, at=xn_free[sl])
            p.dma("sync", xnt[:, sl * D:(sl + 1) * D], xn_i[t * 128:(t + 1) * 128, :], sX[sl])
            p.wait("scalar", sX[sl])
            xsl = xnt[:, sl * D:(sl + 1) * D]
            p.emit("scalar", lambda e, xsl=xsl: e.activation(out=xq_i32, in_=xsl, func=AF.Identity,
                                                             bias=cols[:, 2:3], scale=cols[:, 1:2]), inc=cp)
            p.wait("vector", cp, at=cp.n)
            p.wait("vector", sX[sl])
            p.emit("vector", lambda e, xsl=xsl: e.scalar_tensor_tensor(out=dm_s, in0=xsl, scalar=cols[:, 1:2],
                                                                       in1=xq_i32, op0=ALU.mult, op1=ALU.subtract))
            p.drain("vector")
            p.emit("vector", lambda e: e.scalar_tensor_tensor(out=rr_s, in0=dm_s, scalar=-32767.5,
                                                              in1=xq_i32, op0=ALU.is_ge, op1=ALU.add))
            p.drain("vector")
            p.emit("vector", lambda e: e.tensor_scalar(out=xq_bf, in0=rr_s, scalar1=32768.0,
                                                       scalar2=cols[:, 0:1], op0=ALU.subtract, op1=ALU.mult), inc=cp)
            xn_free[sl] = cp.n
            p.wait("tensor", cp, at=cp.n)
            p.wait("tensor", ev)
            for c in range(KC):
                p.emit("tensor", lambda e, c=c: e.transpose(
                    ps_t[:, c * 128:(c + 1) * 128], xq_bf[:, c * 128:(c + 1) * 128], idb),
                    inc=(mm if c == KC - 1 else None))
            p.wait("scalar", mm, at=mm.n)
            p.emit("scalar", lambda e, t=t: e.activation(
                out=xq_fm.rearrange("p (c j) -> p c j", j=R)[:, :, t * 128:(t + 1) * 128],
                in_=ps_t.rearrange("p (c j) -> p c j", j=128), func=AF.Copy), inc=ev)
            p.wait("vector", ev)

        p.wait("sync", G["ccs"], at=2)
        for b in range(8):
            p.wait("vector", ld)
            p.emit("vector", lambda e: e.memset(sinf, 0.0), inc=cp)
            for j in range(8):
                p.wait("sync", cp)
                p.dma("sync", stg, st_ag[64 * j:64 * (j + 1), b * BLK:(b + 1) * BLK], ld)
                p.wait("vector", ld)
                p.emit("vector", lambda e, j=j: e.scalar_tensor_tensor(
                    out=sinf, in0=stg, scalar=pmaskb[0:64, j:j + 1],
                    in1=sinf, op0=ALU.mult, op1=ALU.add), inc=cp)
            p.wait("sync", cp)
            p.dma("sync", stg, sc0_i[0:64, b * BLK:(b + 1) * BLK], ld)
            p.wait("vector", ld)
            p.emit("vector", lambda e: e.tensor_tensor(out=smidf, in0=sinf, in1=stg, op=ALU.add), inc=cp)
            p.wait("gpsimd", cp)
            for src, dst in ((sinf, sin_all), (smidf, smid_all)):
                sv = src.rearrange("p (h2 two e) -> p h2 two e", two=2, e=65)
                dv = dst.rearrange("p (b h2 e) -> p b h2 e", b=8, e=65)
                p.dma("gpsimd", dv[0:64, b, :, :], sv[0:64, :, 0, :], ld)
                p.dma("gpsimd", dv[64:128, b, :, :], sv[0:64, :, 1, :], ld)
        states_ld = ld.n

        p.wait("vector", ld, states_ld)
        batch_free = [0, 0]
        att_free = [0, 0]
        y_free = [0, 0]
        for b in range(8):
            sl = b % 2
            p.wait("gpsimd", mm, at=batch_free[sl])
            p.dma("gpsimd", pq_b[:, sl * KC * 256:(sl + 1) * KC * 256],
                  phiq_fm[:, b * KC * 256:(b + 1) * KC * 256], sA[sl])
            p.dma("gpsimd", pk_b[:, sl * KC * 256:(sl + 1) * KC * 256],
                  phik_fm[:, b * KC * 256:(b + 1) * KC * 256], sA[sl])
            p.dma("gpsimd", v_b.rearrange("p (s c j) -> p s c j", s=2, j=BLK)[:, sl, :, :],
                  v_tm[b * 256:(b + 1) * 256, :].rearrange("(c p) j -> p c j", p=128), sA[sl])
            p.wait("tensor", sA[sl])
            pqs = pq_b[:, sl * KC * 256:(sl + 1) * KC * 256]
            pks = pk_b[:, sl * KC * 256:(sl + 1) * KC * 256]
            vbs = v_b[:, sl * 2 * BLK:(sl + 1) * 2 * BLK]
            for chn in range(2):
                for g in range(4):
                    gp = g % 2
                    par = g // 2
                    base4 = (g % 2) * 4
                    heads = [2 * (base4 + i) + par for i in range(4)]
                    patt = pb0 if gp == 0 else pb1
                    py = pb2 if gp == 0 else pb3
                    absl = att_bf[:, gp * 512:(gp + 1) * 512]
                    p.wait("tensor", ev, at=att_free[gp])
                    for hi in range(4):
                        hh = heads[hi]
                        cc, po = hh // 2, (hh % 2) * 64
                        lk = pks[po:po + 64, cc * 256 + chn * 128: cc * 256 + (chn + 1) * 128]
                        lq = pqs[po:po + 64, cc * 256 + chn * 128: cc * 256 + (chn + 1) * 128]
                        p.emit("tensor", lambda e, patt=patt, hi=hi, lk=lk, lq=lq: e.matmul(
                            patt[:, hi * 128:(hi + 1) * 128], lk, lq, start=True, stop=True),
                            inc=(mm if hi == 3 else None))
                    p.wait("vector", mm, at=mm.n)
                    p.emit("vector", lambda e, patt=patt, absl=absl: e.tensor_tensor(
                        out=absl.rearrange("p (x j) -> p x j", j=128),
                        in0=patt.rearrange("p (x j) -> p x j", j=128),
                        in1=trm.rearrange("p (o j) -> p o j", o=1).broadcast_to([128, 4, 128]),
                        op=ALU.mult), inc=ev)
                    att_free[gp] = ev.n
                    p.wait("tensor", ev, at=ev.n)
                    p.wait("tensor", ev, at=y_free[gp])
                    sall = sin_all if chn == 0 else smid_all
                    for hi in range(4):
                        hh = heads[hi]
                        cc, po = hh // 2, (hh % 2) * 64
                        lq = pqs[po:po + 64, cc * 256 + chn * 128: cc * 256 + (chn + 1) * 128]
                        p.emit("tensor", lambda e, py=py, hi=hi, absl=absl, vbs=vbs, chn=chn, hh=hh: e.matmul(
                            py[:, hi * 65:hi * 65 + 65], absl[:, hi * 128:(hi + 1) * 128],
                            vbs[:, chn * BLK + hh * 65: chn * BLK + (hh + 1) * 65],
                            start=True, stop=False))
                        p.emit("tensor", lambda e, py=py, hi=hi, lq=lq, sall=sall, b=b, hh=hh, po=po: e.matmul(
                            py[:, hi * 65:hi * 65 + 65], lq,
                            sall[po:po + 64, (b * 8 + hh // 2) * 65:(b * 8 + hh // 2 + 1) * 65],
                            start=False, stop=True), inc=(mm if hi == 3 else None))
                    p.wait("vector", mm, at=mm.n)
                    rsl = rec4[:, gp * 4:(gp + 1) * 4]
                    p.emit("vector", lambda e, py=py, rsl=rsl: e.reciprocal(
                        out=rsl, in_=py[:, 0:260].rearrange("p (x e) -> p x e", e=65)[:, :, 64:65]))
                    p.drain("vector")
                    p.emit("vector", lambda e, py=py, par=par, base4=base4, rsl=rsl: e.tensor_tensor(
                        out=ctxt.rearrange("p (h2 par d) -> p h2 par d", par=2, d=64)[:, base4:base4 + 4, par, :],
                        in0=py[:, 0:260].rearrange("p (x e) -> p x e", e=65)[:, :, 0:64],
                        in1=rsl.rearrange("p (x o) -> p x o", o=1).broadcast_to([128, 4, 64]),
                        op=ALU.mult), inc=ev)
                    y_free[gp] = ev.n
                p.wait("scalar", ev)
                p.emit("scalar", lambda e: e.activation(out=ctx_bf, in_=ctxt, func=AF.Copy), inc=cp)
                p.wait("tensor", cp, at=cp.n)
                p.wait("tensor", ev)
                for c in range(KC):
                    p.emit("tensor", lambda e, c=c: e.transpose(
                        ps_t[:, c * 128:(c + 1) * 128], ctx_bf[:, c * 128:(c + 1) * 128], idb),
                        inc=(mm if c == KC - 1 else None))
                p.wait("scalar", mm, at=mm.n)
                t_idx = b * 2 + chn
                p.emit("scalar", lambda e, t_idx=t_idx: e.activation(
                    out=ctx_fm.rearrange("p (c j) -> p c j", j=R)[:, :, t_idx * 128:(t_idx + 1) * 128],
                    in_=ps_t.rearrange("p (c j) -> p c j", j=128), func=AF.Copy), inc=ev)
                p.wait("vector", ev)
            batch_free[sl] = mm.n

        for c in range(KC):
            p.dma("gpsimd", w2b[:, c * D:(c + 1) * D], w2t[c * 128:(c + 1) * 128, :], ld)
        w2_ld = ld.n
        p.wait("tensor", ld, w2_ld)
        p.wait("tensor", ev)
        for tokc in range(4):
            p.wait("sync", ev if tokc == 0 else st)
            for tb in range(4):
                p.dma("sync", xa_st[:, tb * D:(tb + 1) * D],
                      xin[(tokc * 4 + tb) * 128:(tokc * 4 + tb + 1) * 128, :], ld)
            xa_ld = ld.n
            p.wait("vector", ld, xa_ld)
            for oc in range(KC):
                pp = pb0 if oc % 2 == 0 else pb1
                for c in range(KC):
                    p.emit("tensor", lambda e, pp=pp, oc=oc, c=c, tokc=tokc: e.matmul(
                        pp, w2b[:, c * D + oc * 128: c * D + (oc + 1) * 128],
                        ctx_fm[:, c * R + tokc * 512: c * R + (tokc + 1) * 512],
                        start=(c == 0), stop=(c == KC - 1)), inc=(mm if c == KC - 1 else None))
                p.wait("scalar", mm, at=mm.n)
                p.wait("scalar", ev)
                p.emit("scalar", lambda e, pp=pp: e.activation(out=w2o, in_=pp, func=AF.Copy), inc=ev)
                p.wait("tensor", ev, at=ev.n)
                for tb in range(4):
                    p.emit("tensor", lambda e, tb=tb: e.transpose(
                        ps_tf[:, tb * 128:(tb + 1) * 128], w2o[:, tb * 128:(tb + 1) * 128], idf),
                        inc=(mm if tb == 3 else None))
                p.wait("vector", mm, at=mm.n)
                p.emit("vector", lambda e, oc=oc: e.tensor_tensor(
                    out=xa_st.rearrange("p (t j) -> p t j", j=D)[:, :, oc * 128:(oc + 1) * 128],
                    in0=xa_st.rearrange("p (t j) -> p t j", j=D)[:, :, oc * 128:(oc + 1) * 128],
                    in1=ps_tf.rearrange("p (t j) -> p t j", j=128), op=ALU.add), inc=ev)
                p.wait("tensor", ev)
            p.wait("sync", ev)
            for tb in range(4):
                p.dma("sync", xa_o[(tokc * 4 + tb) * 128:(tokc * 4 + tb + 1) * 128, :],
                      xa_st[:, tb * D:(tb + 1) * D], st)

        iw0 = dm_s.bitcast(I32); iw1 = rr_s.bitcast(I32)
        dw = xnt
        wq_bf = ctxt.bitcast(BF16)
        wlhs = xa_st[:, 0:1024].bitcast(BF16)
        upst = xa_st[:, 2048:3072]
        p.emit("vector", lambda e: e.memset(m1sc[0:1, 4:5], 16.0))
        p.emit("vector", lambda e: e.memset(cols[:, 3:4], 16.0), inc=cp)
        p.wait("scalar", cp)
        w_free = [0, 0]
        lhs_free = [0, 0]
        pp_ev = [0, 0]
        for wi, (wsrc, dst) in enumerate(((wup, up_raw_o), (wgate, gate_raw_o))):
            for oci in range(24):
                oc = oci
                sl = (wi * 24 + oci) % 2
                wts = wtile[:, sl * D:(sl + 1) * D]
                yws = yw[:, sl * D:(sl + 1) * D]
                iws = iw0 if sl == 0 else iw1
                dws = dw[:, sl * D:(sl + 1) * D]
                wqs = wq_bf[:, sl * D:(sl + 1) * D]
                wls = wlhs[:, sl * D:(sl + 1) * D]
                bms = bmx[:, sl * 32:(sl + 1) * 32]
                sbs = sblk[:, sl * 32:(sl + 1) * 32]
                rss = rsblk[:, sl * 32:(sl + 1) * 32]
                p.wait("sync", cp, at=w_free[sl])
                p.dma("sync", wts, wsrc[oc * 128:(oc + 1) * 128, :], sW[sl])
                p.wait("vector", sW[sl])
                p.emit("vector", lambda e, wts=wts, bms=bms: e.tensor_reduce(
                    out=bms, in_=wts.rearrange("p (g x) -> p g x", x=32), axis=AX.X,
                    op=ALU.max, apply_absolute_value=True))
                p.drain("vector")
                p.emit("vector", lambda e, bms=bms, sbs=sbs: e.tensor_scalar(
                    out=sbs, in0=bms, scalar1=1.0 / 7.0, scalar2=1e-12, op0=ALU.mult, op1=ALU.max))
                p.drain("vector")
                p.emit("vector", lambda e, sbs=sbs, rss=rss: e.reciprocal(out=rss, in_=sbs))
                p.drain("vector")
                p.emit("vector", lambda e, wts=wts, yws=yws, rss=rss: e.tensor_tensor(
                    out=yws.rearrange("p (g x) -> p g x", x=32), in0=wts.rearrange("p (g x) -> p g x", x=32),
                    in1=rss.rearrange("p (g o) -> p g o", o=1).broadcast_to([128, 32, 32]), op=ALU.mult), inc=cp)
                w_free[sl] = cp.n
                p.wait("scalar", cp, at=cp.n)
                p.emit("scalar", lambda e, yws=yws, iws=iws: e.activation(
                    out=iws, in_=yws, func=AF.Identity, bias=cols[:, 3:4], scale=1.0), inc=cp)
                p.wait("vector", cp, at=cp.n)
                p.emit("vector", lambda e, yws=yws, iws=iws, dws=dws: e.scalar_tensor_tensor(
                    out=dws, in0=yws, scalar=16.0, in1=iws, op0=ALU.add, op1=ALU.subtract))
                p.drain("vector")
                p.emit("vector", lambda e, dws=dws, iws=iws: e.scalar_tensor_tensor(
                    out=dws, in0=dws, scalar=0.5, in1=iws, op0=ALU.is_ge, op1=ALU.add))
                p.drain("vector")
                p.emit("vector", lambda e, dws=dws: e.tensor_scalar(
                    out=dws, in0=dws, scalar1=16.0, scalar2=None, op0=ALU.subtract))
                p.drain("vector")
                p.emit("vector", lambda e, dws=dws, wqs=wqs, sbs=sbs: e.tensor_tensor(
                    out=wqs.rearrange("p (g x) -> p g x", x=32), in0=dws.rearrange("p (g x) -> p g x", x=32),
                    in1=sbs.rearrange("p (g o) -> p g o", o=1).broadcast_to([128, 32, 32]), op=ALU.mult), inc=cp)
                p.wait("tensor", cp, at=cp.n)
                p.wait("tensor", mm, at=lhs_free[sl])
                p.wait("tensor", ev)
                for c in range(KC):
                    p.emit("tensor", lambda e, c=c, wqs=wqs: e.transpose(
                        ps_t[:, c * 128:(c + 1) * 128], wqs[:, c * 128:(c + 1) * 128], idb),
                        inc=(mm if c == KC - 1 else None))
                p.wait("scalar", mm, at=mm.n)
                p.emit("scalar", lambda e, wls=wls: e.activation(out=wls, in_=ps_t, func=AF.Copy), inc=ev)
                p.wait("tensor", ev, at=ev.n)
                for tokc in range(4):
                    pp = pb0 if tokc % 2 == 0 else pb1
                    usl = upst[:, (tokc % 2) * 512:(tokc % 2 + 1) * 512]
                    p.wait("tensor", ev, at=pp_ev[tokc % 2])
                    for c in range(KC):
                        p.emit("tensor", lambda e, pp=pp, c=c, tokc=tokc, wls=wls: e.matmul(
                            pp, wls[:, c * 128:(c + 1) * 128],
                            xq_fm[:, c * R + tokc * 512: c * R + (tokc + 1) * 512],
                            start=(c == 0), stop=(c == KC - 1)), inc=(mm if c == KC - 1 else None))
                    if tokc == 3:
                        lhs_free[sl] = mm.n
                    p.wait("scalar", mm, at=mm.n)
                    p.wait("scalar", st)
                    p.wait("scalar", cp)
                    p.emit("scalar", lambda e, usl=usl, pp=pp: e.activation(out=usl, in_=pp, func=AF.Copy), inc=ev)
                    pp_ev[tokc % 2] = ev.n
                    p.wait("gpsimd", ev, at=ev.n)
                    p.dma("gpsimd", dst.rearrange("p (o j) -> p o j", j=R)[:, oc, tokc * 512:(tokc + 1) * 512], usl, st)
                    p.wait("vector", ev, at=ev.n)
                    p.emit("vector", lambda e, usl=usl: e.tensor_reduce(
                        out=rec4[:, 0:1], in_=usl, axis=AX.X, op=ALU.max, apply_absolute_value=True))
                    p.drain("vector")
                    p.emit("vector", lambda e, wi=wi: e.tensor_tensor(
                        out=runm[:, wi:wi + 1], in0=runm[:, wi:wi + 1], in1=rec4[:, 0:1], op=ALU.max), inc=cp)
        for wi, dst in ((0, maxp2_o), (1, maxp3_o)):
            p.wait("tensor", cp)
            if wi == 1:
                p.wait("tensor", ev)
            p.emit("tensor", lambda e, wi=wi: e.transpose(ps_mx[0:1, 0:128], runm[:, wi:wi + 1], idf), inc=mm)
            p.wait("vector", mm, at=mm.n)
            p.emit("vector", lambda e: e.tensor_copy(out=mx1, in_=ps_mx[0:1, :]), inc=ev)
            p.drain("vector")
            p.emit("vector", lambda e: e.reduce_max(out=m1sc[0:1, 3:4], in_=mx1[0:1, :], axis=AX.X), inc=ev)
            p.wait("sync", ev)
            p.dma("sync", dst, m1sc[0:1, 3:4], st)
            p.wait("vector", st)

        p.wait("gpsimd", st)
        p.cc("AllReduce", ALU.max, T["cc_m23_in"], T["cc_m23_out"], G["ccs"])
        return [ld, mm, ev, st, aux, cp] + sA + sW + sX


def _phase_barrier(p, sems, dum_v, dum_s, dum_g, bar):
    p.emit("vector", lambda e, d=dum_v: e.memset(d, 0.0), inc=bar)
    p.emit("scalar", lambda e, d=dum_s: e.activation(out=d, in_=d, func=AF.Copy), inc=bar)
    p.emit("gpsimd", lambda e, d=dum_g: e.memset(d, 0.0), inc=bar)
    tgt = bar.n
    for ek in ("sync", "scalar", "vector", "tensor", "gpsimd"):
        p.wait(ek, bar, at=tgt)
        for sm in sems:
            p.wait(ek, sm)


def _emit_scales(p, nc_scr, mrow, msc, cols_s, cols_rs, ld, cp, aux, idx):
    p.emit("vector", lambda e: e.reduce_max(out=msc[0:1, 0:1], in_=mrow, axis=AX.X))
    p.drain("vector")
    p.emit("vector", lambda e: e.tensor_scalar(out=msc[0:1, 1:2], in0=msc[0:1, 0:1],
                                               scalar1=1.0 / 32767.0, scalar2=1e-12, op0=ALU.mult, op1=ALU.max))
    p.drain("vector")
    p.emit("vector", lambda e: e.reciprocal(out=msc[0:1, 2:3], in_=msc[0:1, 1:2]), inc=cp)
    p.wait("sync", cp)
    p.dma("sync", nc_scr.ap()[0:1, idx:idx + 2], msc[0:1, 1:3], aux)
    p.wait("gpsimd", aux)
    p.dma("gpsimd", cols_s, nc_scr.ap()[0:1, idx:idx + 1].broadcast_to([128, 1]), ld)
    p.dma("gpsimd", cols_rs, nc_scr.ap()[0:1, idx + 1:idx + 2].broadcast_to([128, 1]), ld)


def _emit_qdq(p, src, dst, i32b, dmb, s_col, rs_col, b_col, cp):
    p.wait("scalar", cp)
    p.emit("scalar", lambda e: e.activation(out=i32b, in_=src, func=AF.Identity,
                                            bias=b_col, scale=rs_col), inc=cp)
    p.wait("vector", cp)
    p.emit("vector", lambda e: e.scalar_tensor_tensor(out=dmb, in0=src, scalar=rs_col,
                                                      in1=i32b, op0=ALU.mult, op1=ALU.subtract))
    p.drain("vector")
    p.emit("vector", lambda e: e.scalar_tensor_tensor(out=dmb, in0=dmb, scalar=-32767.5,
                                                      in1=i32b, op0=ALU.is_ge, op1=ALU.add))
    p.drain("vector")
    p.emit("vector", lambda e: e.tensor_scalar(out=dst, in0=dmb, scalar1=32768.0,
                                               scalar2=s_col, op0=ALU.subtract, op1=ALU.mult), inc=cp)


def _emit_colmax_out(p, runcol, mx1, msc_sl, ps_mx, idf, dst, mm, ev, st):
    p.emit("tensor", lambda e: e.transpose(ps_mx[0:1, 0:128], runcol, idf), inc=mm)
    p.wait("vector", mm)
    p.emit("vector", lambda e: e.tensor_copy(out=mx1, in_=ps_mx[0:1, :]), inc=ev)
    p.drain("vector")
    p.emit("vector", lambda e: e.reduce_max(out=msc_sl, in_=mx1[0:1, :], axis=AX.X), inc=ev)
    p.wait("sync", ev)
    p.dma("sync", dst, msc_sl, st)
    p.wait("vector", st)


def emit_phase3(nc, p, octx, T, G):
    up_raw_i, gate_raw_i = T["up_raw_d"], T["gate_raw_d"]
    upq_o, t_o = T["upq_d"], T["t_d"]
    maxp4_o = T["cc_m4_in"]
    scr = T["scr_t"]

    with ExitStack() as ctx:
        sbuf = lambda n, sh, dt: ctx.enter_context(nc.sbuf_tensor("p3" + n, sh, dt))[:, :]
        psum = lambda n, sh, dt: ctx.enter_context(nc.psum_tensor("p3" + n, sh, dt))[:, :]
        sem = lambda n: octx.enter_context(nc.semaphore("p3" + n))
        upc = sbuf("upc", [128, 2 * R], F32)
        gac = sbuf("gac", [128, 2 * R], F32)
        i32u = sbuf("i32u", [128, 2 * R], I32)
        dmu = sbuf("dmu", [128, 2 * R], F32)
        i32g = sbuf("i32g", [128, 2 * R], I32)
        dmg = sbuf("dmg", [128, 2 * R], F32)
        gq = sbuf("gq", [128, 2 * R], F32)
        sg = sbuf("sg", [128, 2 * R], F32)
        outU = sbuf("outU", [128, 2 * R], F32)
        outT = sbuf("outT", [128, 2 * R], F32)
        m2row = sbuf("m2row", [1, 8], F32)
        m3row = sbuf("m3row", [1, 8], F32)
        msc = sbuf("msc", [1, 8], F32)
        cols = sbuf("cols", [128, 8], F32)
        runm = sbuf("runm", [128, 1], F32)
        mx1 = sbuf("mx1", [1, 128], F32)
        idf = G["idf"]
        ps_mx = psum("ps_mx", [1, 128], F32)
        s_ld = sem("s_ld"); s_cp = sem("s_cp"); s_ev = sem("s_ev")
        s_st = sem("s_st"); s_aux = sem("s_aux"); s_mm = sem("s_mm")
        ld = Sem(s_ld); cp = Sem(s_cp); ev = Sem(s_ev); st = Sem(s_st)
        aux = Sem(s_aux); mm = Sem(s_mm)

        p.wait("sync", G["ccs"], at=3)
        p.dma("sync", m2row[0:1, 0:1], T["cc_m23_out"][0:1, 0:1], ld)
        p.dma("sync", m3row[0:1, 0:1], T["cc_m23_out"][0:1, 1:2], ld)
        p.emit("vector", lambda e: e.memset(cols[:, 4:5], 32768.0))
        p.emit("vector", lambda e: e.memset(runm, 0.0), inc=cp)
        p.wait("vector", ld)
        _emit_scales(p, scr, m2row[0:1, 0:1], msc, cols[:, 0:1], cols[:, 1:2], ld, cp, aux, 4)
        _emit_scales(p, scr, m3row[0:1, 0:1], msc, cols[:, 2:3], cols[:, 3:4], ld, cp, aux, 6)
        p.wait("scalar", ld)
        p.wait("scalar", cp)
        p.wait("vector", ld)
        cpS = Sem(sem("s_cs")); cpV = Sem(sem("s_cv")); cpG = Sem(sem("s_cg"))
        sl_ld = [Sem(sem("s_l0")), Sem(sem("s_l1"))]
        v_free = [0, 0]; g_free = [0, 0]; s_free = [0, 0]
        upst_done = [0, 0]
        tst_done = [0, 0]

        def gchain(ops):
            for i, fn in enumerate(ops):
                p.emit("vector", fn, inc=cpG)
                if i < len(ops) - 1:
                    p.drain("vector")

        for oc in range(24):
            sl = oc % 2
            c2 = lambda buf, sl=sl: buf[:, sl * R:(sl + 1) * R]
            U, Gc = c2(upc), c2(gac)
            Iu, Du = c2(i32u), c2(dmu)
            Ig, Dg = c2(i32g), c2(dmg)
            Q, S_, Ou, Ot = c2(gq), c2(sg), c2(outU), c2(outT)
            p.wait("gpsimd", cpV, at=v_free[sl])
            p.wait("gpsimd", cpG, at=g_free[sl])
            p.wait("gpsimd", cpS, at=s_free[sl])
            p.dma("gpsimd", U, up_raw_i[:, oc * R:(oc + 1) * R], sl_ld[sl])
            p.dma("gpsimd", Gc, gate_raw_i[:, oc * R:(oc + 1) * R], sl_ld[sl])
            ldv = sl_ld[sl].n
            p.wait("scalar", sl_ld[sl], at=ldv)
            p.emit("scalar", lambda e, U=U, Iu=Iu: e.activation(
                out=Iu, in_=U, func=AF.Identity, bias=cols[:, 4:5], scale=cols[:, 1:2]), inc=cpS)
            cv_u = cpS.n
            p.emit("scalar", lambda e, Gc=Gc, Ig=Ig: e.activation(
                out=Ig, in_=Gc, func=AF.Identity, bias=cols[:, 4:5], scale=cols[:, 3:4]), inc=cpS)
            cv_g = cpS.n
            p.wait("vector", cpS, at=cv_u)
            p.wait("vector", st, at=upst_done[sl])
            gchain([
                lambda e, U=U, Iu=Iu, Du=Du: e.scalar_tensor_tensor(
                    out=Du, in0=U, scalar=cols[:, 1:2], in1=Iu, op0=ALU.mult, op1=ALU.subtract),
                lambda e, Iu=Iu, Du=Du: e.scalar_tensor_tensor(
                    out=Du, in0=Du, scalar=-32767.5, in1=Iu, op0=ALU.is_ge, op1=ALU.add),
                lambda e, Du=Du, Ou=Ou: e.tensor_scalar(
                    out=Ou, in0=Du, scalar1=32768.0, scalar2=cols[:, 0:1],
                    op0=ALU.subtract, op1=ALU.mult),
            ])
            upq_done = cpG.n
            p.wait("vector", cpS, at=cv_g)
            p.emit("vector", lambda e, Gc=Gc, Ig=Ig, Dg=Dg: e.scalar_tensor_tensor(
                out=Dg, in0=Gc, scalar=cols[:, 3:4], in1=Ig, op0=ALU.mult, op1=ALU.subtract))
            p.drain("vector")
            p.emit("vector", lambda e, Ig=Ig, Dg=Dg: e.scalar_tensor_tensor(
                out=Dg, in0=Dg, scalar=-32767.5, in1=Ig, op0=ALU.is_ge, op1=ALU.add))
            p.drain("vector")
            p.emit("vector", lambda e, Dg=Dg, Q=Q: e.tensor_scalar(
                out=Q, in0=Dg, scalar1=32768.0, scalar2=cols[:, 2:3],
                op0=ALU.subtract, op1=ALU.mult), inc=cpV)
            gq_done = cpV.n
            p.wait("scalar", cpV, at=gq_done)
            p.emit("scalar", lambda e, Q=Q, S_=S_: e.activation(out=S_, in_=Q, func=AF.Sigmoid), inc=cpS)
            p.emit("scalar", lambda e, S_=S_, Ig=Ig: e.activation(
                out=Ig, in_=S_, func=AF.Identity, scale=65536.0), inc=cpS)
            sgc = cpS.n
            p.wait("vector", cpS, at=sgc)
            gchain([
                lambda e, S_=S_, Ig=Ig, Du=Du: e.scalar_tensor_tensor(
                    out=Du, in0=S_, scalar=65536.0, in1=Ig, op0=ALU.mult, op1=ALU.subtract),
                lambda e, Ig=Ig, Du=Du: e.scalar_tensor_tensor(
                    out=Du, in0=Du, scalar=0.5, in1=Ig, op0=ALU.is_ge, op1=ALU.add),
                lambda e, Du=Du, S_=S_: e.tensor_scalar(
                    out=S_, in0=Du, scalar1=65535.0, scalar2=1.0 / 65536.0,
                    op0=ALU.min, op1=ALU.mult),
            ])
            fq_done = cpG.n
            p.wait("vector", cpG, at=fq_done)
            p.wait("vector", st, at=tst_done[sl])
            p.emit("vector", lambda e, Q=Q, S_=S_, Ot=Ot: e.tensor_tensor(
                out=Ot, in0=Q, in1=S_, op=ALU.mult), inc=cpV)
            t_done = cpV.n
            p.drain("vector")
            p.emit("vector", lambda e, Ot=Ot, Dg=Dg: e.tensor_reduce(
                out=Dg[:, 0:1], in_=Ot, axis=AX.X, op=ALU.max, apply_absolute_value=True))
            p.drain("vector")
            p.emit("vector", lambda e, Dg=Dg: e.tensor_tensor(
                out=runm, in0=runm, in1=Dg[:, 0:1], op=ALU.max), inc=cpV)
            v_free[sl] = cpV.n
            g_free[sl] = cpG.n
            s_free[sl] = cpS.n
            p.wait("gpsimd", cpG, at=upq_done)
            p.dma("gpsimd", upq_o[:, oc * R:(oc + 1) * R], Ou, st)
            upst_done[sl] = st.n
            p.wait("gpsimd", cpV, at=t_done)
            p.dma("gpsimd", t_o[:, oc * R:(oc + 1) * R], Ot, st)
            tst_done[sl] = st.n
        p.wait("tensor", cpV)
        _emit_colmax_out(p, runm[:, 0:1], mx1, msc[0:1, 7:8], ps_mx, idf, maxp4_o, mm, ev, st)
        p.wait("gpsimd", st)
        p.cc("AllReduce", ALU.max, T["cc_m4_in"], T["cc_m4_out"], G["ccs"])
        return [ld, cp, ev, st, aux, mm, cpS, cpV, cpG] + sl_ld


def emit_phase4(nc, p, octx, T, G):
    t_i, upq_i = T["t_d"], T["upq_d"]
    oraw_o = T["oraw_d"]
    maxp5_o = T["cc_m5_in"]
    scr = T["scr_t"]

    with ExitStack() as ctx:
        sbuf = lambda n, sh, dt: ctx.enter_context(nc.sbuf_tensor("p4" + n, sh, dt))[:, :]
        psum = lambda n, sh, dt: ctx.enter_context(nc.psum_tensor("p4" + n, sh, dt))[:, :]
        sem = lambda n: octx.enter_context(nc.semaphore("p4" + n))
        tc = sbuf("tc", [128, 2 * R], F32)
        uc = sbuf("uc", [128, 2 * R], F32)
        i32b = sbuf("i32b", [128, 2 * R], I32)
        dmb = sbuf("dmb", [128, 2 * R], F32)
        g2 = sbuf("g2", [128, 2 * R], F32)
        outb = sbuf("outb", [128, 2 * R], F32)
        m4row = sbuf("m4row", [1, 8], F32)
        msc = sbuf("msc", [1, 8], F32)
        cols = sbuf("cols", [128, 4], F32)
        runm = sbuf("runm", [128, 1], F32)
        mx1 = sbuf("mx1", [1, 128], F32)
        idf = G["idf"]
        ps_mx = psum("ps_mx", [1, 128], F32)
        s_ld = sem("s_ld"); s_cp = sem("s_cp"); s_ev = sem("s_ev")
        s_st = sem("s_st"); s_aux = sem("s_aux"); s_mm = sem("s_mm")
        ld = Sem(s_ld); cp = Sem(s_cp); ev = Sem(s_ev); st = Sem(s_st)
        aux = Sem(s_aux); mm = Sem(s_mm)

        p.wait("sync", G["ccs"], at=4)
        p.dma("sync", m4row[0:1, 0:1], T["cc_m4_out"], ld)
        p.emit("vector", lambda e: e.memset(cols[:, 2:3], 32768.0))
        p.emit("vector", lambda e: e.memset(runm, 0.0), inc=cp)
        p.wait("vector", ld)
        _emit_scales(p, scr, m4row[0:1, 0:1], msc, cols[:, 0:1], cols[:, 1:2], ld, cp, aux, 8)
        p.wait("scalar", ld)
        p.wait("scalar", cp)
        cpS = Sem(sem("s_cs")); cpV = Sem(sem("s_cv")); cpG = Sem(sem("s_cg"))
        sl_ld = [Sem(sem("s_l0")), Sem(sem("s_l1"))]
        v_free = [0, 0]; g_free = [0, 0]; s_free = [0, 0]
        ost_done = [0, 0]
        for oc in range(24):
            sl = oc % 2
            c2 = lambda buf, sl=sl: buf[:, sl * R:(sl + 1) * R]
            Tc, Uc, I, Dm, G2, Ob = c2(tc), c2(uc), c2(i32b), c2(dmb), c2(g2), c2(outb)
            p.wait("gpsimd", cpV, at=v_free[sl])
            p.wait("gpsimd", cpG, at=g_free[sl])
            p.wait("gpsimd", cpS, at=s_free[sl])
            p.dma("gpsimd", Tc, t_i[:, oc * R:(oc + 1) * R], sl_ld[sl])
            p.dma("gpsimd", Uc, upq_i[:, oc * R:(oc + 1) * R], sl_ld[sl])
            ldv = sl_ld[sl].n
            p.wait("scalar", sl_ld[sl], at=ldv)
            p.emit("scalar", lambda e, Tc=Tc, I=I: e.activation(
                out=I, in_=Tc, func=AF.Identity, bias=cols[:, 2:3], scale=cols[:, 1:2]), inc=cpS)
            cv = cpS.n
            p.wait("vector", cpS, at=cv)
            p.emit("vector", lambda e, Tc=Tc, I=I, Dm=Dm: e.scalar_tensor_tensor(
                out=Dm, in0=Tc, scalar=cols[:, 1:2], in1=I, op0=ALU.mult, op1=ALU.subtract))
            p.drain("vector")
            p.emit("vector", lambda e, I=I, Dm=Dm: e.scalar_tensor_tensor(
                out=Dm, in0=Dm, scalar=-32767.5, in1=I, op0=ALU.is_ge, op1=ALU.add))
            p.drain("vector")
            p.emit("vector", lambda e, Dm=Dm, G2=G2: e.tensor_scalar(
                out=G2, in0=Dm, scalar1=32768.0, scalar2=cols[:, 0:1],
                op0=ALU.subtract, op1=ALU.mult), inc=cpV)
            q_done = cpV.n
            p.wait("vector", st, at=ost_done[sl])
            p.emit("vector", lambda e, G2=G2, Uc=Uc, Ob=Ob: e.tensor_tensor(
                out=Ob, in0=G2, in1=Uc, op=ALU.mult), inc=cpG)
            o_done = cpG.n
            p.wait("gpsimd", cpG, at=o_done)
            p.dma("gpsimd", oraw_o[:, oc * R:(oc + 1) * R], Ob, st)
            ost_done[sl] = st.n
            p.wait("vector", cpG, at=o_done)
            p.emit("vector", lambda e, Ob=Ob, Dm=Dm: e.tensor_reduce(
                out=Dm[:, 0:1], in_=Ob, axis=AX.X, op=ALU.max, apply_absolute_value=True))
            p.drain("vector")
            p.emit("vector", lambda e, Dm=Dm: e.tensor_tensor(
                out=runm, in0=runm, in1=Dm[:, 0:1], op=ALU.max), inc=cpV)
            v_free[sl] = cpV.n
            g_free[sl] = cpG.n
            s_free[sl] = cpS.n
        p.wait("tensor", cpV)
        _emit_colmax_out(p, runm[:, 0:1], mx1, msc[0:1, 7:8], ps_mx, idf, maxp5_o, mm, ev, st)
        p.wait("gpsimd", st)
        p.cc("AllReduce", ALU.max, T["cc_m5_in"], T["cc_m5_out"], G["ccs"])
        return [ld, cp, ev, st, aux, mm, cpS, cpV, cpG] + sl_ld


def emit_phase5(nc, p, octx, T, G):
    oraw_i = T["oraw_d"]
    xa_i = T["xa_d"]
    wdown, ipw_t, ipb_row = T["wdown"], T["ipw_t"], T["ipb_row"]
    opw_t, opb_row, rt_t, rb_row = T["opw_t"], T["opb_row"], T["rt_t"], T["rb_row"]
    gat, upt, dat = T["gat"], T["upt"], T["dat"]
    sgt, sut, sdt = T["sgt"], T["sut"], T["sdt"]
    out_o = T["out_o"]
    scr = T["scr_t"]
    wt_dram = T["wt_dram"]
    debug = 0

    with ExitStack() as ctx:
        sbuf = lambda n, sh, dt: ctx.enter_context(nc.sbuf_tensor("p5" + n, sh, dt))[:, :]
        psum = lambda n, sh, dt: ctx.enter_context(nc.psum_tensor("p5" + n, sh, dt))[:, :]
        sem = lambda n: octx.enter_context(nc.semaphore("p5" + n))
        x1 = sbuf("x1", [128, NT * D], F32)
        xn1_fm = sbuf("xn1_fm", [128, KC * R], BF16)
        big1 = sbuf("big1", [128, 12288], F32)
        big2 = sbuf("big2", [128, 8192], F32)
        big3 = sbuf("big3", [128, 4096], F32)
        q512b = sbuf("q512b", [128, 512], F32)
        q512c = sbuf("q512c", [128, 512], F32)
        ev512 = sbuf("ev512", [128, 512], F32)
        bias5 = q512b
        wpool = sbuf("wpool", [128, 4096], BF16)
        dlhs = wpool
        gulhs = wpool[:, 0:2048]
        oplhs = wpool[:, 0:1024]
        rtb = wpool[:, 0:256]
        sglhs = wpool[:, 0:1024]
        sulhs = wpool[:, 1024:2048]
        sdlhs = wpool[:, 0:512]
        wbc = big1[:, 0:512]
        wtstg = big3[0:32, 1024:1152]
        wall = sbuf("wall", [128, 32], F32)
        opb = sbuf("opb", [128, 8], F32)
        rbb = sbuf("rbb", [128, 32], F32)
        bmx3 = sbuf("bmx3", [128, 96], F32)
        stat = sbuf("stat", [128, 8], F32)
        cols = sbuf("cols", [128, 4], F32)
        m5row = sbuf("m5row", [1, 8], F32)
        msc = sbuf("msc", [1, 8], F32)
        idb, idf = G["idb"], G["idf"]
        ps_a = psum("ps_a", [128, 512], F32)
        ps_b = psum("ps_b", [128, 512], F32)
        ps_c = psum("ps_c", [128, 512], F32)
        ps_d = psum("ps_d", [128, 512], F32)
        ps_t = psum("ps_t", [128, D], BF16)
        ps_tf = psum("ps_tf", [128, 512], F32)
        s_ld = sem("s_ld"); s_cp = sem("s_cp"); s_ev = sem("s_ev")
        s_st = sem("s_st"); s_aux = sem("s_aux"); s_mm = sem("s_mm")
        s_bar = sem("s_bar")
        s_vc = sem("s_vc")
        slot_sems = [sem(f"s_sl{i}") for i in range(6)]
        ld = Sem(s_ld); cp = Sem(s_cp); ev = Sem(s_ev); st = Sem(s_st)
        aux = Sem(s_aux); mm = Sem(s_mm); cpvc = Sem(s_vc)
        sg = [Sem(slot_sems[0]), Sem(slot_sems[1])]
        sw = [Sem(slot_sems[2]), Sem(slot_sems[3])]
        sd = [Sem(slot_sems[4]), Sem(slot_sems[5])]
        bar = Sem(s_bar)
        allsems = [ld, cp, ev, st, aux, mm, cpvc, p.sems["lncp"]] + sg + sw + sd
        barrier = lambda: _phase_barrier(p, allsems, stat[0:1, 6:7], stat[0:1, 7:8], msc[0:1, 6:7], bar)

        p.wait("sync", G["ccs"], at=5)
        p.dma("sync", m5row[0:1, 0:1], T["cc_m5_out"], ld)
        p.emit("vector", lambda e: e.memset(cols[:, 2:3], 32768.0), inc=cp)
        p.wait("vector", ld)
        _emit_scales(p, scr, m5row[0:1, 0:1], msc, cols[:, 0:1], cols[:, 1:2], ld, cp, aux, 10)
        p.dma("gpsimd", opb, opb_row.rearrange("a (c p) -> p a c", p=128)[:, 0, :], ld,
              allow_slow_non_contiguous=True)
        p.dma("gpsimd", rbb, rb_row.broadcast_to([128, 32]), ld)
        for t in range(NT):
            p.dma("sync", x1[:, t * D:(t + 1) * D], xa_i[t * 128:(t + 1) * 128, :], ld)
        xa_ld = ld.n

        wdlhs = big1.bitcast(BF16)
        wtile = big3[:, 0:1024]
        yw = big3[:, 1024:2048]
        iw = big3[:, 2048:3072].bitcast(I32)[:, 0:1024]
        dw = big3[:, 3072:4096]
        wq_bf = ev512.bitcast(BF16)
        p.emit("vector", lambda e: e.memset(stat[:, 0:1], 16.0), inc=cp)
        p.wait("scalar", ld, xa_ld)
        for oc in range(8):
            for pc in range(3):
                p.wait("sync", cp)
                p.dma("sync", wtile, wdown[oc * 128:(oc + 1) * 128, pc * 1024:(pc + 1) * 1024], ld)
                p.wait("vector", ld)
                p.emit("vector", lambda e: e.tensor_reduce(
                    out=bmx3[:, 0:32], in_=wtile.rearrange("p (g x) -> p g x", x=32), axis=AX.X,
                    op=ALU.max, apply_absolute_value=True))
                p.drain("vector")
                p.emit("vector", lambda e: e.tensor_scalar(out=bmx3[:, 32:64], in0=bmx3[:, 0:32],
                                                           scalar1=1.0 / 7.0, scalar2=1e-12, op0=ALU.mult, op1=ALU.max))
                p.drain("vector")
                p.emit("vector", lambda e: e.reciprocal(out=bmx3[:, 64:96], in_=bmx3[:, 32:64]))
                p.drain("vector")
                p.emit("vector", lambda e: e.tensor_tensor(
                    out=yw.rearrange("p (g x) -> p g x", x=32), in0=wtile.rearrange("p (g x) -> p g x", x=32),
                    in1=bmx3[:, 64:96].rearrange("p (g o) -> p g o", o=1).broadcast_to([128, 32, 32]),
                    op=ALU.mult), inc=cp)
                p.wait("scalar", cp)
                p.emit("scalar", lambda e: e.activation(out=iw, in_=yw, func=AF.Identity,
                                                        bias=stat[:, 0:1], scale=1.0), inc=cp)
                p.wait("vector", cp)
                p.emit("vector", lambda e: e.scalar_tensor_tensor(out=dw, in0=yw, scalar=16.0,
                                                                  in1=iw, op0=ALU.add, op1=ALU.subtract))
                p.drain("vector")
                p.emit("vector", lambda e: e.scalar_tensor_tensor(out=dw, in0=dw, scalar=0.5,
                                                                  in1=iw, op0=ALU.is_ge, op1=ALU.add))
                p.drain("vector")
                p.emit("vector", lambda e: e.tensor_scalar(out=dw, in0=dw, scalar1=16.0, scalar2=None,
                                                           op0=ALU.subtract))
                p.drain("vector")
                p.emit("vector", lambda e: e.tensor_tensor(
                    out=wq_bf.rearrange("p (g x) -> p g x", x=32), in0=dw.rearrange("p (g x) -> p g x", x=32),
                    in1=bmx3[:, 32:64].rearrange("p (g o) -> p g o", o=1).broadcast_to([128, 32, 32]),
                    op=ALU.mult), inc=cp)
                p.wait("tensor", cp)
                p.wait("tensor", ev)
                for c in range(8):
                    p.emit("tensor", lambda e, c=c: e.transpose(
                        ps_t[:, c * 128:(c + 1) * 128], wq_bf[:, c * 128:(c + 1) * 128], idb),
                        inc=(mm if c == 7 else None))
                p.wait("scalar", mm)
                p.emit("scalar", lambda e, oc=oc, pc=pc: e.activation(
                    out=wdlhs.rearrange("p (o k j) -> p o k j", o=8, j=128)[:, oc, pc * 8:(pc + 1) * 8, :],
                    in_=ps_t.rearrange("p (k j) -> p k j", j=128), func=AF.Copy), inc=ev)
        o_slab = big2.bitcast(BF16)[:, 0:24 * 512]
        qstage = big3[:, 0:2048]
        qi32 = big3[:, 2048:4096].bitcast(I32)[:, 0:2048]
        p.wait("scalar", cp)
        p.wait("scalar", ev)
        for tq in range(4):
            for g in range(6):
                p.wait("gpsimd", cp)
                p.dma("gpsimd", qstage.rearrange("p (k j) -> p k j", j=512),
                      oraw_i.rearrange("p (k r) -> p k r", r=R)[:, 4 * g:4 * g + 4,
                                                               tq * 512:(tq + 1) * 512], ld)
                p.wait("scalar", ld)
                _emit_qdq(p, qstage, o_slab[:, g * 2048:(g + 1) * 2048],
                          qi32, qstage, cols[:, 0:1], cols[:, 1:2], cols[:, 2:3], cp)
            p.wait("tensor", cp)
            for oc in range(8):
                pp = ps_a if oc % 2 == 0 else ps_b
                for kc in range(24):
                    p.emit("tensor", lambda e, pp=pp, oc=oc, kc=kc: e.matmul(
                        pp, wdlhs.rearrange("p (o k j) -> p o k j", o=8, j=128)[:, oc, kc, :],
                        o_slab[:, kc * 512:(kc + 1) * 512],
                        start=(kc == 0), stop=(kc == 23)), inc=(mm if kc == 23 else None))
                p.wait("scalar", mm)
                p.wait("scalar", st)
                p.emit("scalar", lambda e, pp=pp: e.activation(out=ev512, in_=pp, func=AF.Copy), inc=ev)
                p.wait("tensor", ev)
                for tb in range(4):
                    p.emit("tensor", lambda e, tb=tb: e.transpose(
                        ps_tf[:, tb * 128:(tb + 1) * 128], ev512[:, tb * 128:(tb + 1) * 128], idf),
                        inc=(mm if tb == 3 else None))
                p.wait("vector", mm)
                p.wait("vector", ld, xa_ld)
                p.emit("vector", lambda e, tq=tq, oc=oc: e.tensor_tensor(
                    out=x1.rearrange("p (t j) -> p t j", j=D)[:, tq * 4:(tq + 1) * 4, oc * 128:(oc + 1) * 128],
                    in0=x1.rearrange("p (t j) -> p t j", j=D)[:, tq * 4:(tq + 1) * 4, oc * 128:(oc + 1) * 128],
                    in1=ps_tf.rearrange("p (t j) -> p t j", j=128), op=ALU.add), inc=ev)
                p.wait("tensor", ev)
            p.wait("scalar", ev)

        if debug & 1:
            p.wait("sync", ev)
            for t in range(NT):
                p.dma("sync", dbg_x1a[t * 128:(t + 1) * 128, :], x1[:, t * D:(t + 1) * D], st)
            p.wait("vector", st)
        barrier()
        sa = big3[:, 0:1024]
        sb = big3[:, 1024:2048]
        xb = big3[:, 2048:2560].bitcast(BF16)[:, 0:1024]
        stats = {"m": stat[:, 1:2], "var": stat[:, 2:3], "sd": stat[:, 3:4],
                 "rstd": stat[:, 4:5], "eps_col": stat[:, 5:6]}
        p.emit("vector", lambda e: e.memset(stat[:, 5:6], EPS))
        p.wait("vector", ev)
        for t in range(NT):
            xt = x1[:, t * D:(t + 1) * D]
            _ln_tile(p, xt, sb, sa, stats)
            p.emit("vector", lambda e: e.tensor_copy(out=xb, in_=sb), inc=p.sems["lncp"])
            cpv = p.sems["lncp"].n
            p.wait("tensor", p.sems["lncp"], cpv)
            p.wait("tensor", ev)
            for c in range(KC):
                p.emit("tensor", lambda e, c=c: e.transpose(
                    ps_t[:, c * 128:(c + 1) * 128], xb[:, c * 128:(c + 1) * 128], idb),
                    inc=(mm if c == KC - 1 else None))
            p.wait("scalar", mm)
            p.emit("scalar", lambda e, t=t: e.activation(
                out=xn1_fm.rearrange("p (c j) -> p c j", j=R)[:, :, t * 128:(t + 1) * 128],
                in_=ps_t.rearrange("p (c j) -> p c j", j=128), func=AF.Copy), inc=ev)
            p.wait("vector", ev)

        if debug & 2:
            p.wait("gpsimd", ev)
            p.dma("gpsimd", dbg_xn1, xn1_fm, st)
            p.wait("vector", st)
        barrier()
        qkv1 = big1.bitcast(BF16)
        wblks = [big3[:, 0:2048].bitcast(BF16), wpool[:, 0:4096]]
        bias5s = [q512b, q512c]
        prodv = big3[:, 0:2048]
        prodg = big3[:, 2048:3072]
        ctxsf = big3[:, 3072:4096]
        xbC = q512b.bitcast(BF16)[:, 0:1024]
        scoresb = q512c[:, 0:128]
        smx = q512c[:, 128:144]
        ssm = q512c[:, 144:160]
        src_ = q512c[:, 160:176]
        ctx_fm = big2.bitcast(BF16)[:, 0:16384]
        wb_used = [0, 0]; wb_ld = [0, 0]; b5_used = [0, 0]
        pp_free = [0, 0]
        p.wait("gpsimd", p.sems["lncp"])
        p.wait("gpsimd", ev)
        for pb in range(2):
            if pb == 1:
                p.wait("gpsimd", cp)
            for oc in range(6):
                sl = oc % 2
                p.wait("gpsimd", mm, at=wb_used[sl])
                p.dma("gpsimd", wblks[sl], ipw_t[oc * 128:(oc + 1) * 128, :], sg[sl])
                p.wait("gpsimd", ev, at=b5_used[sl])
                p.dma("gpsimd", bias5s[sl].rearrange("p (o j) -> p o j", o=1),
                      ipb_row[0:1, oc * 512:(oc + 1) * 512].rearrange(
                          "a (o j) -> a o j", o=1).broadcast_to([128, 1, 512]), sw[sl])
                p.wait("tensor", sg[sl])
                for si in range(8):
                    t_idx = si * 2 + pb
                    psl = si % 2
                    pp = ps_a if psl == 0 else ps_b
                    p.wait("tensor", ev, at=pp_free[psl])
                    for kc in range(KC):
                        p.emit("tensor", lambda e, pp=pp, kc=kc, t_idx=t_idx, sl=sl: e.matmul(
                            pp, xn1_fm.rearrange("p (c j) -> p c j", j=R)[:, kc, t_idx * 128:(t_idx + 1) * 128],
                            wblks[sl][:, kc * 512:(kc + 1) * 512],
                            start=(kc == 0), stop=(kc == KC - 1)), inc=(mm if kc == KC - 1 else None))
                    if si == 7:
                        wb_used[sl] = mm.n
                    p.wait("vector", mm, at=mm.n)
                    p.wait("vector", sw[sl])
                    p.emit("vector", lambda e, pp=pp, si=si, oc=oc, sl=sl: e.tensor_tensor(
                        out=qkv1.rearrange("p (s f) -> p s f", f=3072)[:, si, oc * 512:(oc + 1) * 512],
                        in0=pp, in1=bias5s[sl], op=ALU.add), inc=ev)
                    pp_free[psl] = ev.n
                    if si == 7:
                        b5_used[sl] = ev.n
            if debug & 4:
                p.wait("gpsimd", ev)
                if pb == 1:
                    p.dma("gpsimd", dbg_qkv1, qkv1, st)
                    p.wait("vector", st)
            qv = qkv1.rearrange("p (s th d) -> p s th d", th=3 * NH, d=64)
            p.wait("vector", ev)
            p.wait("gpsimd", ev)
            sc_rdy = 0
            for si in range(8):
                for tp in range(4):
                    p.emit("vector", lambda e, si=si, tp=tp, qv=qv: e.tensor_tensor(
                        out=prodv.rearrange("p (t2 h d) -> p t2 h d", t2=2, d=64),
                        in0=qv[:, si:si + 1, 0:NH, :].broadcast_to([128, 2, NH, 64]),
                        in1=qv[:, 2 * tp:2 * tp + 2, NH:2 * NH, :], op=ALU.mult))
                    p.emit("vector", lambda e, tp=tp: e.tensor_reduce(
                        out=scoresb.rearrange("p (h t) -> p h t", t=8)[:, :, 2 * tp:2 * tp + 2],
                        in_=prodv.rearrange("p (t2 h d) -> p h t2 d", t2=2, d=64), axis=AX.X, op=ALU.add))
                p.drain("vector")
                p.emit("vector", lambda e: e.tensor_reduce(
                    out=smx, in_=scoresb.rearrange("p (h t) -> p h t", t=8), axis=AX.X, op=ALU.max))
                p.drain("vector")
                p.emit("vector", lambda e: e.tensor_tensor(
                    out=scoresb.rearrange("p (h t) -> p h t", t=8), in0=scoresb.rearrange("p (h t) -> p h t", t=8),
                    in1=smx.rearrange("p (h o) -> p h o", o=1).broadcast_to([128, 16, 8]), op=ALU.subtract), inc=cp)
                p.wait("scalar", cp, at=cp.n)
                p.emit("scalar", lambda e: e.activation(out=scoresb, in_=scoresb, func=AF.Exp, scale=0.125), inc=cp)
                p.wait("vector", cp, at=cp.n)
                p.emit("vector", lambda e: e.tensor_reduce(
                    out=ssm, in_=scoresb.rearrange("p (h t) -> p h t", t=8), axis=AX.X, op=ALU.add))
                p.drain("vector")
                p.emit("vector", lambda e: e.reciprocal(out=src_, in_=ssm))
                p.drain("vector")
                p.emit("vector", lambda e: e.tensor_tensor(
                    out=scoresb.rearrange("p (h t) -> p h t", t=8), in0=scoresb.rearrange("p (h t) -> p h t", t=8),
                    in1=src_.rearrange("p (h o) -> p h o", o=1).broadcast_to([128, 16, 8]), op=ALU.mult), inc=cp)
                sc_rdy = cp.n
                ctx2 = prodv[:, 0:1024]
                prodv2 = prodv[:, 1024:2048]
                p.wait("gpsimd", cp, at=sc_rdy)
                for t in range(4):
                    p.emit("gpsimd", lambda e, t=t, qv=qv: e.tensor_tensor(
                        out=prodg[:, 0:1024].rearrange("p (h d) -> p h d", d=64),
                        in0=qv[:, t, 2 * NH:3 * NH, :],
                        in1=scoresb.rearrange("p (h t) -> p h t", t=8)[:, :, t:t + 1].broadcast_to([128, 16, 64]),
                        op=ALU.mult))
                    if t == 0:
                        p.emit("gpsimd", lambda e: e.tensor_copy(out=ctxsf, in_=prodg[:, 0:1024]))
                    elif t < 3:
                        p.emit("gpsimd", lambda e: e.tensor_tensor(out=ctxsf, in0=ctxsf, in1=prodg[:, 0:1024], op=ALU.add))
                    else:
                        p.emit("gpsimd", lambda e: e.tensor_tensor(out=ctxsf, in0=ctxsf, in1=prodg[:, 0:1024], op=ALU.add), inc=cp)
                g_done = cp.n
                for t in range(4, 8):
                    p.emit("vector", lambda e, t=t, qv=qv, prodv2=prodv2: e.tensor_tensor(
                        out=prodv2.rearrange("p (h d) -> p h d", d=64),
                        in0=qv[:, t, 2 * NH:3 * NH, :],
                        in1=scoresb.rearrange("p (h t) -> p h t", t=8)[:, :, t:t + 1].broadcast_to([128, 16, 64]),
                        op=ALU.mult))
                    p.drain("vector")
                    if t == 4:
                        p.emit("vector", lambda e, ctx2=ctx2, prodv2=prodv2: e.tensor_copy(out=ctx2, in_=prodv2))
                    else:
                        p.emit("vector", lambda e, ctx2=ctx2, prodv2=prodv2: e.tensor_tensor(
                            out=ctx2, in0=ctx2, in1=prodv2, op=ALU.add))
                    p.drain("vector")
                p.wait("vector", cp, at=g_done)
                p.emit("vector", lambda e, ctx2=ctx2: e.tensor_tensor(out=ctxsf, in0=ctxsf, in1=ctx2, op=ALU.add))
                p.drain("vector")
                p.emit("vector", lambda e: e.tensor_copy(out=xbC, in_=ctxsf), inc=cpvc)
                cast_done = cpvc.n
                p.wait("tensor", cpvc, at=cast_done)
                p.wait("tensor", ev)
                t_idx = si * 2 + pb
                for c in range(KC):
                    p.emit("tensor", lambda e, c=c: e.transpose(
                        ps_t[:, c * 128:(c + 1) * 128], xbC[:, c * 128:(c + 1) * 128], idb),
                        inc=(mm if c == KC - 1 else None))
                p.wait("scalar", mm, at=mm.n)
                p.emit("scalar", lambda e, t_idx=t_idx: e.activation(
                    out=ctx_fm.rearrange("p (c j) -> p c j", j=R)[:, :, t_idx * 128:(t_idx + 1) * 128],
                    in_=ps_t.rearrange("p (c j) -> p c j", j=128), func=AF.Copy), inc=ev)
                p.wait("gpsimd", cpvc, at=cast_done)
                p.wait("gpsimd", ev)
                p.wait("vector", ev)
                p.wait("vector", cp)

        if debug & 8:
            p.wait("gpsimd", ev)
            p.dma("gpsimd", dbg_ctx, ctx_fm, st)
            p.wait("vector", st)
        barrier()
        for oc in range(8):
            p.wait("gpsimd", mm)
            p.dma("gpsimd", oplhs, opw_t[oc * 128:(oc + 1) * 128, :], ld)
            p.wait("tensor", ld)
            for tq in range(4):
                pp = ps_a if tq % 2 == 0 else ps_b
                p.wait("tensor", ev)
                for kc in range(KC):
                    p.emit("tensor", lambda e, pp=pp, kc=kc, oc=oc, tq=tq: e.matmul(
                        pp, oplhs[:, kc * 128:(kc + 1) * 128],
                        ctx_fm[:, kc * R + tq * 512: kc * R + (tq + 1) * 512],
                        start=(kc == 0), stop=(kc == KC - 1)), inc=(mm if kc == KC - 1 else None))
                p.wait("scalar", mm)
                p.emit("scalar", lambda e, pp=pp, oc=oc: e.activation(
                    out=ev512, in_=pp, func=AF.Identity, bias=opb[:, oc:oc + 1], scale=1.0), inc=ev)
                p.wait("tensor", ev)
                for tb in range(4):
                    p.emit("tensor", lambda e, tb=tb: e.transpose(
                        ps_tf[:, tb * 128:(tb + 1) * 128], ev512[:, tb * 128:(tb + 1) * 128], idf),
                        inc=(mm if tb == 3 else None))
                p.wait("vector", mm)
                p.emit("vector", lambda e, tq=tq, oc=oc: e.tensor_tensor(
                    out=x1.rearrange("p (t j) -> p t j", j=D)[:, tq * 4:(tq + 1) * 4, oc * 128:(oc + 1) * 128],
                    in0=x1.rearrange("p (t j) -> p t j", j=D)[:, tq * 4:(tq + 1) * 4, oc * 128:(oc + 1) * 128],
                    in1=ps_tf.rearrange("p (t j) -> p t j", j=128), op=ALU.add), inc=ev)
                p.wait("tensor", ev)

        if debug & 16:
            p.wait("sync", ev)
            for t in range(NT):
                p.dma("sync", dbg_x1d1[t * 128:(t + 1) * 128, :], x1[:, t * D:(t + 1) * D], st)
            p.wait("vector", st)
        barrier()
        prod = prodv
        p.wait("gpsimd", mm)
        p.dma("gpsimd", rtb, rt_t, ld)
        p.wait("tensor", ld)
        for t in range(NT):
            pp = ps_a if t % 2 == 0 else ps_b
            p.wait("tensor", ev)
            for kc in range(KC):
                p.emit("tensor", lambda e, pp=pp, kc=kc, t=t: e.matmul(
                    pp[:, 0:32],
                    xn1_fm.rearrange("p (c j) -> p c j", j=R)[:, kc, t * 128:(t + 1) * 128],
                    rtb[:, kc * 32:(kc + 1) * 32],
                    start=(kc == 0), stop=(kc == KC - 1)), inc=(mm if kc == KC - 1 else None))
            p.wait("vector", mm)
            wsl = wall[:, 0:32]
            p.emit("vector", lambda e, pp=pp, wsl=wsl: e.tensor_tensor(out=prod[:, 0:32], in0=pp[:, 0:32], in1=rbb, op=ALU.add), inc=ev)
            p.drain("vector")
            p.emit("vector", lambda e: e.reduce_max(out=stat[:, 6:7], in_=prod[:, 0:32], axis=AX.X))
            p.drain("vector")
            p.emit("vector", lambda e: e.tensor_scalar(out=prod[:, 0:32], in0=prod[:, 0:32],
                                                       scalar1=stat[:, 6:7], scalar2=None, op0=ALU.subtract), inc=cp)
            p.wait("scalar", cp)
            p.emit("scalar", lambda e: e.activation(out=prod[:, 0:32], in_=prod[:, 0:32], func=AF.Exp), inc=cp)
            p.wait("vector", cp)
            p.emit("vector", lambda e: e.reduce_sum(out=stat[:, 6:7], in_=prod[:, 0:32], axis=AX.X))
            p.drain("vector")
            p.emit("vector", lambda e: e.reciprocal(out=stat[:, 7:8], in_=stat[:, 6:7]))
            p.drain("vector")
            p.emit("vector", lambda e: e.tensor_scalar(out=prod[:, 0:32], in0=prod[:, 0:32],
                                                       scalar1=stat[:, 7:8], scalar2=None, op0=ALU.mult))
            p.drain("vector")
            p.emit("vector", lambda e: e.tensor_copy(out=prod[:, 32:64], in_=prod[:, 0:32]))
            p.emit("vector", lambda e, wsl=wsl: e.memset(wsl, 0.0))
            p.drain("vector")
            for it in range(4):
                p.emit("vector", lambda e: e.reduce_max(out=stat[:, 6:7], in_=prod[:, 32:64], axis=AX.X))
                p.drain("vector")
                p.emit("vector", lambda e: e.tensor_scalar(out=prod[:, 64:96], in0=prod[:, 32:64],
                                                           scalar1=stat[:, 6:7], scalar2=None, op0=ALU.is_ge))
                p.drain("vector")
                p.emit("vector", lambda e: e.tensor_tensor(out=prod[:, 96:128], in0=prod[:, 64:96], in1=prod[:, 0:32], op=ALU.mult))
                p.drain("vector")
                p.emit("vector", lambda e, wsl=wsl: e.tensor_tensor(out=wsl, in0=wsl, in1=prod[:, 96:128], op=ALU.add))
                p.emit("vector", lambda e: e.tensor_scalar(out=prod[:, 96:128], in0=prod[:, 64:96],
                                                           scalar1=-1.0, scalar2=1.0, op0=ALU.mult, op1=ALU.add))
                p.drain("vector")
                p.emit("vector", lambda e: e.tensor_tensor(out=prod[:, 32:64], in0=prod[:, 32:64], in1=prod[:, 96:128], op=ALU.mult))
                p.drain("vector")
                p.emit("vector", lambda e: e.tensor_tensor(out=prod[:, 32:64], in0=prod[:, 32:64], in1=prod[:, 64:96], op=ALU.subtract))
                p.drain("vector")
            p.emit("vector", lambda e, wsl=wsl: e.reduce_sum(out=stat[:, 6:7], in_=wsl, axis=AX.X))
            p.drain("vector")
            p.emit("vector", lambda e: e.tensor_scalar(out=stat[:, 6:7], in0=stat[:, 6:7],
                                                       scalar1=1e-12, scalar2=None, op0=ALU.max))
            p.drain("vector")
            p.emit("vector", lambda e: e.reciprocal(out=stat[:, 7:8], in_=stat[:, 6:7]))
            p.drain("vector")
            p.emit("vector", lambda e, wsl=wsl: e.tensor_scalar(out=wsl, in0=wsl,
                                                                scalar1=stat[:, 7:8], scalar2=None, op0=ALU.mult), inc=cp)
            p.wait("tensor", cp)
            p.wait("tensor", aux)
            p.emit("tensor", lambda e, wsl=wsl: e.transpose(ps_tf[0:32, 0:128], wsl, idf), inc=mm)
            p.wait("vector", mm)
            p.wait("vector", st)
            p.emit("vector", lambda e: e.tensor_copy(out=wtstg, in_=ps_tf[0:32, 0:128]), inc=aux)
            p.wait("sync", aux)
            p.dma("sync", wt_dram.ap()[0:32, t * 128:(t + 1) * 128], wtstg, st)
        p.wait("vector", st)

        barrier()
        h_fm = big2.bitcast(BF16)[:, 0:32 * 512]
        sils = [big3[:, 0:512], big3[:, 512:1024]]
        h1s = [big3[:, 1024:1536], big3[:, 1536:2048]]
        gus = [wpool[:, 0:2048], wpool[:, 2048:4096]]
        wbcs = [big1[:, 0:512], big1[:, 512:1024]]
        dls = [big1[:, 1024:3072].bitcast(BF16), big1[:, 3072:5120].bitcast(BF16)]
        gu_used = [0, 0]; wbc_used = [0, 0]; gu_ld = [0, 0]; wbc_ld = [0, 0]
        pe_free = [0, 0]
        dl_used = [0, 0]; dl_ld = [0, 0]; dn_free = [0, 0]
        if debug & 32:
            p.wait("sync", st)
            p.dma("sync", dbg_wt, wt_dram.ap(), st)
        p.wait("gpsimd", st)
        for tq in range(4):
            p.wait("vector", mm)
            p.wait("gpsimd", mm)
            p.wait("tensor", ev)
            for ei in range(32):
                sl = ei % 2
                pg = ps_a if sl == 0 else ps_c
                pu = ps_b if sl == 0 else ps_d
                p.wait("gpsimd", mm, at=gu_used[sl])
                p.dma("gpsimd", gus[sl][:, 0:1024], gat[ei * 128:(ei + 1) * 128, :], sg[sl])
                p.dma("gpsimd", gus[sl][:, 1024:2048], upt[ei * 128:(ei + 1) * 128, :], sg[sl])
                p.wait("gpsimd", cp, at=wbc_used[sl])
                p.dma("gpsimd", wbcs[sl].rearrange("p (o j) -> p o j", o=1),
                      wt_dram.ap()[ei:ei + 1, tq * 512:(tq + 1) * 512].rearrange(
                          "a (o j) -> a o j", o=1).broadcast_to([128, 1, 512]), sw[sl])
                p.wait("tensor", sg[sl])
                p.wait("tensor", cp, at=pe_free[sl])
                for kc in range(KC):
                    p.emit("tensor", lambda e, pg=pg, kc=kc, tq=tq, sl=sl: e.matmul(
                        pg, gus[sl][:, kc * 128:(kc + 1) * 128],
                        xn1_fm[:, kc * R + tq * 512: kc * R + (tq + 1) * 512],
                        start=(kc == 0), stop=(kc == KC - 1)))
                    p.emit("tensor", lambda e, pu=pu, kc=kc, tq=tq, sl=sl: e.matmul(
                        pu, gus[sl][:, 1024 + kc * 128: 1024 + (kc + 1) * 128],
                        xn1_fm[:, kc * R + tq * 512: kc * R + (tq + 1) * 512],
                        start=(kc == 0), stop=(kc == KC - 1)), inc=(mm if kc == KC - 1 else None))
                gu_used[sl] = mm.n
                p.wait("scalar", mm, at=mm.n)
                p.emit("scalar", lambda e, pg=pg, sl=sl: e.activation(out=sils[sl], in_=pg, func=AF.Silu), inc=cp)
                p.wait("vector", cp, at=cp.n)
                p.wait("vector", sw[sl])
                p.emit("vector", lambda e, pu=pu, sl=sl: e.tensor_tensor(out=h1s[sl], in0=pu, in1=sils[sl], op=ALU.mult))
                p.emit("vector", lambda e, ei=ei, sl=sl: e.tensor_tensor(
                    out=h_fm[:, ei * 512:(ei + 1) * 512], in0=h1s[sl], in1=wbcs[sl], op=ALU.mult), inc=cp)
                wbc_used[sl] = cp.n
                pe_free[sl] = cp.n
            p.wait("tensor", cp)
            if (debug & 64) and tq == 0:
                p.wait("gpsimd", cp)
                p.dma("gpsimd", dbg_hfm, h_fm, st)
                p.wait("vector", st)
            for oc in range(8):
                sl = oc % 2
                pp = ps_a if sl == 0 else ps_b
                p.wait("gpsimd", mm, at=dl_used[sl])
                p.dma("gpsimd", dls[sl], dat[oc * 128:(oc + 1) * 128, :], sd[sl])
                p.wait("tensor", sd[sl])
                p.wait("tensor", ev, at=dn_free[sl])
                for kc in range(32):
                    p.emit("tensor", lambda e, pp=pp, kc=kc, sl=sl: e.matmul(
                        pp, dls[sl][:, kc * 128:(kc + 1) * 128], h_fm[:, kc * 512:(kc + 1) * 512],
                        start=(kc == 0), stop=(kc == 31)), inc=(mm if kc == 31 else None))
                dl_used[sl] = mm.n
                p.wait("scalar", mm, at=mm.n)
                p.wait("scalar", ev)
                p.emit("scalar", lambda e, pp=pp: e.activation(out=ev512, in_=pp, func=AF.Copy), inc=ev)
                p.wait("tensor", ev, at=ev.n)
                for tb in range(4):
                    p.emit("tensor", lambda e, tb=tb: e.transpose(
                        ps_tf[:, tb * 128:(tb + 1) * 128], ev512[:, tb * 128:(tb + 1) * 128], idf),
                        inc=(mm if tb == 3 else None))
                p.wait("vector", mm, at=mm.n)
                p.emit("vector", lambda e, tq=tq, oc=oc: e.tensor_tensor(
                    out=x1.rearrange("p (t j) -> p t j", j=D)[:, tq * 4:(tq + 1) * 4, oc * 128:(oc + 1) * 128],
                    in0=x1.rearrange("p (t j) -> p t j", j=D)[:, tq * 4:(tq + 1) * 4, oc * 128:(oc + 1) * 128],
                    in1=ps_tf.rearrange("p (t j) -> p t j", j=128), op=ALU.add), inc=ev)
                dn_free[sl] = ev.n
                p.wait("tensor", ev)

        if debug & 128:
            p.wait("sync", ev)
            for t in range(NT):
                p.dma("sync", dbg_x1d2[t * 128:(t + 1) * 128, :], x1[:, t * D:(t + 1) * D], st)
            p.wait("vector", st)
        barrier()
        s_h = big3.bitcast(BF16)[:, 0:4 * R]
        for oc in range(4):
            p.wait("gpsimd", mm)
            p.dma("gpsimd", sglhs, sgt[oc * 128:(oc + 1) * 128, :], ld)
            p.dma("gpsimd", sulhs, sut[oc * 128:(oc + 1) * 128, :], ld)
            p.wait("tensor", ld)
            for tq in range(4):
                pg = ps_a if tq % 2 == 0 else ps_c
                pu = ps_b if tq % 2 == 0 else ps_d
                p.wait("tensor", cp)
                for kc in range(KC):
                    p.emit("tensor", lambda e, pg=pg, kc=kc, tq=tq: e.matmul(
                        pg, sglhs[:, kc * 128:(kc + 1) * 128],
                        xn1_fm[:, kc * R + tq * 512: kc * R + (tq + 1) * 512],
                        start=(kc == 0), stop=(kc == KC - 1)))
                    p.emit("tensor", lambda e, pu=pu, kc=kc, tq=tq: e.matmul(
                        pu, sulhs[:, kc * 128:(kc + 1) * 128],
                        xn1_fm[:, kc * R + tq * 512: kc * R + (tq + 1) * 512],
                        start=(kc == 0), stop=(kc == KC - 1)), inc=(mm if kc == KC - 1 else None))
                p.wait("scalar", mm)
                p.emit("scalar", lambda e, pg=pg: e.activation(out=q512b, in_=pg, func=AF.Silu), inc=cp)
                p.wait("vector", cp)
                p.emit("vector", lambda e, pu=pu, oc=oc, tq=tq: e.tensor_tensor(
                    out=s_h[:, oc * R + tq * 512: oc * R + (tq + 1) * 512],
                    in0=pu, in1=q512b, op=ALU.mult), inc=cp)
        p.wait("tensor", cp)
        for oc in range(8):
            p.wait("gpsimd", mm)
            p.dma("gpsimd", sdlhs, sdt[oc * 128:(oc + 1) * 128, :], ld)
            p.wait("tensor", ld)
            for tq in range(4):
                pp = ps_a if tq % 2 == 0 else ps_b
                p.wait("tensor", ev)
                for kc in range(4):
                    p.emit("tensor", lambda e, pp=pp, kc=kc, tq=tq: e.matmul(
                        pp, sdlhs[:, kc * 128:(kc + 1) * 128],
                        s_h[:, kc * R + tq * 512: kc * R + (tq + 1) * 512],
                        start=(kc == 0), stop=(kc == 3)), inc=(mm if kc == 3 else None))
                p.wait("scalar", mm)
                p.emit("scalar", lambda e, pp=pp: e.activation(out=ev512, in_=pp, func=AF.Copy), inc=ev)
                p.wait("tensor", ev)
                for tb in range(4):
                    p.emit("tensor", lambda e, tb=tb: e.transpose(
                        ps_tf[:, tb * 128:(tb + 1) * 128], ev512[:, tb * 128:(tb + 1) * 128], idf),
                        inc=(mm if tb == 3 else None))
                p.wait("vector", mm)
                p.emit("vector", lambda e, tq=tq, oc=oc: e.tensor_tensor(
                    out=x1.rearrange("p (t j) -> p t j", j=D)[:, tq * 4:(tq + 1) * 4, oc * 128:(oc + 1) * 128],
                    in0=x1.rearrange("p (t j) -> p t j", j=D)[:, tq * 4:(tq + 1) * 4, oc * 128:(oc + 1) * 128],
                    in1=ps_tf.rearrange("p (t j) -> p t j", j=128), op=ALU.add), inc=ev)
                p.wait("tensor", ev)

        barrier()
        p.wait("sync", ev)
        for t in range(NT):
            p.dma("sync", out_o[t * 128:(t + 1) * 128, :], x1[:, t * D:(t + 1) * D], st)
        p.wait("gpsimd", st)
        p.wait("sync", st)
        return [ld, cp, ev, st, aux, mm] + sg + sw + sd


# version 16
# speedup vs baseline: 1.0512x; 1.0512x over previous
from contextlib import ExitStack
import numpy as np
import ml_dtypes
import concourse.bass as bass
import concourse.mybir as mybir
from concourse.bass_utils import run_bass_kernel_spmd

F32 = mybir.dt.float32
BF16 = mybir.dt.bfloat16
I32 = mybir.dt.int32
ALU = mybir.AluOpType
AF = mybir.ActivationFunctionType
AX = mybir.AxisListType

NCORES = 8
R = 2048
NT = 16
D = 1024
KC = 8
NH = 16
HD = 64
EPS = 1e-5


class Sem:
    __slots__ = ("h", "n")

    def __init__(self, h):
        self.h = h
        self.n = 0


class Prog:
    ENGINES = ("sync", "scalar", "vector", "tensor", "gpsimd")

    def __init__(self, nc):
        self.nc = nc
        self.ops = {e: [] for e in self.ENGINES}

    def emit(self, ek, make, inc=None, inck=1):
        if inc is not None:
            def fn(eng, make=make, s=inc, k=inck):
                make(eng).then_inc(s.h, k)
            inc.n += inck
        else:
            fn = make
        self.ops[ek].append(fn)

    def dma(self, ek, out, in_, inc, **kw):
        self.emit(ek, lambda eng, o=out, i=in_, kw=kw: eng.dma_start(out=o, in_=i, **kw),
                  inc=inc, inck=16)

    def cc(self, kind, op, in_ap, out_ap, sem):
        def fn(eng, k=kind, o=op, i=in_ap, u=out_ap, s=sem):
            eng.collective_compute(k, o, replica_groups=[list(range(NCORES))],
                                   ins=[i], outs=[u]).then_inc(s.h)
        sem.n += 1
        self.ops["gpsimd"].append(fn)

    def wait(self, ek, sem, at=None):
        v = sem.n if at is None else at
        self.ops[ek].append(lambda eng, s=sem, vv=v: eng.wait_ge(s.h, vv))

    def drain(self, ek):
        self.ops[ek].append(lambda eng: eng.drain())

    def build(self):
        with self.nc.Block() as block:
            for ek in self.ENGINES:
                def runner(eng, ek=ek):
                    for fn in self.ops[ek]:
                        fn(eng)
                getattr(block, ek)(runner)


def _ln_tile(p, xs, xn_out, sa, stats, extra_absmax=None):
    cp = p.sems["lncp"]
    v, s = "vector", "scalar"
    m, var, sd, rstd, eps_col = stats["m"], stats["var"], stats["sd"], stats["rstd"], stats["eps_col"]
    p.emit(v, lambda e, a=xs, o=m: e.reduce_sum(out=o, in_=a, axis=AX.X))
    p.drain(v)
    p.emit(v, lambda e, o=m, a=m: e.tensor_scalar(out=o, in0=a, scalar1=1.0 / D, scalar2=None, op0=ALU.mult))
    p.drain(v)
    p.emit(v, lambda e, o=sa, a=xs, mm=m: e.tensor_scalar(out=o, in0=a, scalar1=mm, scalar2=None, op0=ALU.subtract), inc=cp)
    p.wait(s, cp)
    p.emit(s, lambda e, o=xn_out, a=sa: e.activation(out=o, in_=a, func=AF.Square), inc=cp)
    p.wait(v, cp)
    p.emit(v, lambda e, o=var, a=xn_out: e.reduce_sum(out=o, in_=a, axis=AX.X))
    p.drain(v)
    p.emit(v, lambda e, o=var, a=var: e.tensor_scalar(out=o, in0=a, scalar1=1.0 / D, scalar2=None, op0=ALU.mult), inc=cp)
    p.wait(s, cp)
    p.emit(s, lambda e, o=sd, a=var, b=eps_col: e.activation(out=o, in_=a, func=AF.Sqrt, bias=b), inc=cp)
    p.wait(v, cp)
    p.emit(v, lambda e, o=rstd, a=sd: e.reciprocal(out=o, in_=a))
    p.drain(v)
    p.emit(v, lambda e, o=xn_out, a=sa, rr=rstd: e.tensor_scalar(out=o, in0=a, scalar1=rr, scalar2=None, op0=ALU.mult), inc=cp)
    if extra_absmax is not None:
        p.drain(v)
        p.emit(v, lambda e, o=extra_absmax, a=xn_out: e.tensor_reduce(
            out=o, in_=a, axis=AX.X, op=ALU.max, apply_absolute_value=True))
        p.drain(v)


def emit_phase1(nc, p, octx, T, G):
    xin, w1t = T["xin"], T["w1t"]
    xn_o = T["xn_d"]
    phiq_fm_o, phik_fm_o = T["phiq_fm_d"], T["phik_fm_d"]
    v_tm_o = T["v_tm_d"]
    ssum_o, sc0_o = T["cc_st_in"], T["sc0_d"]
    maxp1_o = T["cc_m1_in"]

    with ExitStack() as ctx:
        sbuf = lambda n, sh, dt: ctx.enter_context(nc.sbuf_tensor("p1" + n, sh, dt))[:, :]
        psum = lambda n, sh, dt: ctx.enter_context(nc.psum_tensor("p1" + n, sh, dt))[:, :]
        sem = lambda n: octx.enter_context(nc.semaphore("p1" + n))
        xw = sbuf("xw", [128, NT * D], F32)
        sa = sbuf("sa", [128, D], F32)
        sb = sbuf("sb", [128, D], F32)
        stat = sbuf("stat", [128, 8], F32)
        pmx = sbuf("pmx", [128, NT], F32)
        mxc = sbuf("mxc", [128, 1], F32)
        mx1 = sbuf("mx1", [1, 128], F32)
        xnn_bf = sbuf("xnn_bf", [128, D], BF16)
        xnn_fm = sbuf("xnn_fm", [128, KC * R], BF16)
        w1b = sbuf("w1b", [128, KC * 3 * D], BF16)
        idb, idf = G["idb"], G["idf"]
        phr = sbuf("phr", [128, D], F32)
        phe = sbuf("phe", [128, D], F32)
        phq_bf = sbuf("phq_bf", [128, D], BF16)
        phk_bf = sbuf("phk_bf", [128, D], BF16)
        v_bf = sbuf("v_bf", [128, NH * (HD + 1)], BF16)
        tstage = sbuf("tstage", [128, D], BF16)
        scst = sbuf("scst", [64, 2 * NH * 65], F32)
        ps_qkv = psum("ps_qkv", [128, D], F32)
        ps_t = psum("ps_t", [128, D], BF16)
        ps_scd0 = psum("ps_scd0", [64, 512], F32)
        ps_scd1 = psum("ps_scd1", [64, 512], F32)
        ps_scz = psum("ps_scz", [64, 16], F32)
        ps_mx = psum("ps_mx", [1, 128], F32)
        s_ld = sem("s_ld"); s_mm = sem("s_mm")
        s_ev = sem("s_ev"); s_st = sem("s_st"); s_aux = sem("s_aux")

        ld = Sem(s_ld); mm = Sem(s_mm); ev = Sem(s_ev); st = Sem(s_st); aux = Sem(s_aux)
        stats = {"m": stat[:, 0:1], "var": stat[:, 1:2], "sd": stat[:, 2:3],
                 "rstd": stat[:, 3:4], "eps_col": stat[:, 4:5]}

        p.emit("vector", lambda e: e.memset(stat[:, 4:5], EPS), inc=p.sems["lncp"])
        for t in range(NT):
            p.dma("sync", xw[:, t * D:(t + 1) * D], xin[t * 128:(t + 1) * 128, :], ld)
        for c in range(KC):
            p.dma("gpsimd", w1b[:, c * 3 * D:(c + 1) * 3 * D], w1t[c * 128:(c + 1) * 128, :], ld)
        ld_all = ld.n

        p.wait("vector", ld, ld_all)
        p.wait("scalar", ld, ld_all)
        for t in range(NT):
            xt = xw[:, t * D:(t + 1) * D]
            _ln_tile(p, xt, sb, sa, stats, extra_absmax=pmx[:, t:t + 1])
            p.emit("vector", lambda e, o=xt, a=sb: e.tensor_copy(out=o, in_=a), inc=p.sems["lncp"])
            _ln_tile(p, xt, sb, sa, stats)
            p.emit("vector", lambda e, o=xnn_bf, a=sb: e.tensor_copy(out=o, in_=a), inc=p.sems["lncp"])
            cpv = p.sems["lncp"].n
            p.wait("sync", p.sems["lncp"], cpv)
            p.dma("sync", xn_o[t * 128:(t + 1) * 128, :], xt, st)
            p.wait("tensor", p.sems["lncp"], cpv)
            if t > 0:
                p.wait("tensor", ev)
            for c in range(KC):
                p.emit("tensor", lambda e, c=c: e.transpose(
                    ps_t[:, c * 128:(c + 1) * 128], xnn_bf[:, c * 128:(c + 1) * 128], idb[:, :]),
                    inc=(mm if c == KC - 1 else None))
            p.wait("scalar", mm)
            p.emit("scalar", lambda e, t=t: e.activation(
                out=xnn_fm.rearrange("p (c j) -> p c j", j=R)[:, :, t * 128:(t + 1) * 128],
                in_=ps_t.rearrange("p (c j) -> p c j", j=128), func=AF.Copy), inc=ev)

        for t in range(NT):
            b, ch = t // 2, t % 2
            for part in range(3):
                p.wait("tensor", ev)
                for c in range(KC):
                    lhs = xnn_fm.rearrange("p (c j) -> p c j", j=R)[:, c, t * 128:(t + 1) * 128]
                    for jn in range(2):
                        rhs = w1b[:, c * 3 * D + part * D + jn * 512: c * 3 * D + part * D + (jn + 1) * 512]
                        p.emit("tensor", lambda e, l=lhs, r=rhs, jn=jn, c=c: e.matmul(
                            ps_qkv[:, jn * 512:(jn + 1) * 512], l, r,
                            start=(c == 0), stop=(c == KC - 1)),
                            inc=(mm if c == KC - 1 and jn == 1 else None))
                p.wait("scalar", mm)
                p.wait("vector", mm)
                if part < 2:
                    p.emit("scalar", lambda e: e.activation(out=phr[:, :], in_=ps_qkv[:, :], func=AF.Relu), inc=ev)
                    p.wait("vector", ev)
                    p.emit("vector", lambda e: e.tensor_tensor(out=phe[:, :], in0=ps_qkv[:, :], in1=phr[:, :], op=ALU.subtract), inc=ev)
                    p.wait("scalar", ev)
                    p.emit("scalar", lambda e: e.activation(out=phe[:, :], in_=phe[:, :], func=AF.Exp), inc=ev)
                    p.wait("vector", ev)
                    dst = phq_bf if part == 0 else phk_bf
                    p.emit("vector", lambda e, d=dst: e.tensor_tensor(out=d[:, :], in0=phe[:, :], in1=phr[:, :], op=ALU.add), inc=ev)
                else:
                    p.emit("vector", lambda e: e.memset(
                        v_bf.rearrange("p (h e) -> p h e", e=HD + 1)[:, :, HD:HD + 1], 1.0))
                    p.emit("vector", lambda e: e.tensor_copy(
                        out=v_bf.rearrange("p (h e) -> p h e", e=HD + 1)[:, :, 0:HD],
                        in_=ps_qkv.rearrange("p (h d) -> p h d", d=HD)), inc=ev)
                p.wait("tensor", ev)

            for src, dst_fm in ((phq_bf, phiq_fm_o), (phk_bf, phik_fm_o)):
                p.wait("tensor", ev)
                for c in range(KC):
                    p.emit("tensor", lambda e, s=src, c=c: e.transpose(
                        ps_t[:, c * 128:(c + 1) * 128], s[:, c * 128:(c + 1) * 128], idb[:, :]),
                        inc=(mm if c == KC - 1 else None))
                p.wait("scalar", mm)
                p.wait("scalar", st)
                p.emit("scalar", lambda e: e.activation(out=tstage[:, :], in_=ps_t[:, :], func=AF.Copy), inc=ev)
                p.wait("sync", ev)
                p.dma("sync", dst_fm.rearrange("p (b c j) -> p b c j", b=8, j=256)[
                          :, t // 2, :, (t % 2) * 128:(t % 2) * 128 + 128],
                      tstage.rearrange("p (c j) -> p c j", j=128), st)

            p.wait("tensor", ev)
            for hh in range(NH):
                bank = ps_scd0 if hh < 8 else ps_scd1
                col = (hh % 8) * 64
                lhs = phk_bf[:, hh * 64:(hh + 1) * 64]
                rhs = v_bf[:, hh * 65:hh * 65 + 64]
                p.emit("tensor", lambda e, bk=bank, co=col, l=lhs, r=rhs: e.matmul(
                    bk[:, co:co + 64], l, r, start=True, stop=True))
                rhs1 = v_bf[:, hh * 65 + 64:hh * 65 + 65]
                p.emit("tensor", lambda e, hh=hh, l=lhs, r=rhs1: e.matmul(
                    ps_scz[:, hh:hh + 1], l, r, start=True, stop=True),
                    inc=(mm if hh == NH - 1 else None))
            p.wait("vector", mm)
            scv = scst.rearrange("p (c x e) -> p c x e", c=2, e=65)
            p.emit("vector", lambda e, ch=ch, scv=scv: e.tensor_copy(
                out=scv[0:64, ch, 0:8, 0:64],
                in_=ps_scd0.rearrange("p (x d) -> p x d", d=64)))
            p.emit("vector", lambda e, ch=ch, scv=scv: e.tensor_copy(
                out=scv[0:64, ch, 8:16, 0:64],
                in_=ps_scd1.rearrange("p (x d) -> p x d", d=64)))
            p.emit("vector", lambda e, ch=ch, scv=scv: e.tensor_copy(
                out=scv[0:64, ch, 0:16, 64:65],
                in_=ps_scz.rearrange("p (x o) -> p x o", o=1)), inc=ev)
            p.wait("tensor", ev)
            if ch == 1:
                blk = NH * 65
                p.wait("sync", ev)
                p.dma("sync", sc0_o[0:64, b * blk:(b + 1) * blk], scst[0:64, 0:blk], st)
                p.wait("vector", st)
                p.emit("vector", lambda e, blk=blk: e.tensor_tensor(
                    out=scst[0:64, 0:blk], in0=scst[0:64, 0:blk],
                    in1=scst[0:64, blk:2 * blk], op=ALU.add), inc=ev)
                p.wait("sync", ev)
                p.dma("sync", ssum_o[0:64, b * blk:(b + 1) * blk], scst[0:64, 0:blk], st)

            p.wait("sync", ev)
            p.dma("sync", v_tm_o[t * 128:(t + 1) * 128, :], v_bf[:, :], st)
            p.wait("scalar", st)
            p.wait("vector", st)

        p.emit("vector", lambda e: e.reduce_max(out=mxc[:, :], in_=pmx[:, :], axis=AX.X), inc=ev)
        p.wait("tensor", ev)
        p.emit("tensor", lambda e: e.transpose(ps_mx[0:1, 0:128], mxc[:, 0:1], idf[:, :]), inc=mm)
        p.wait("vector", mm)
        p.emit("vector", lambda e: e.tensor_copy(out=mx1[:, :], in_=ps_mx[0:1, :]))
        p.drain("vector")
        p.emit("vector", lambda e: e.reduce_max(out=stat[0:1, 5:6], in_=mx1[0:1, :], axis=AX.X), inc=ev)
        p.wait("sync", ev)
        p.dma("sync", maxp1_o, stat[0:1, 5:6], st)

        p.wait("gpsimd", st)
        p.cc("AllReduce", ALU.max, T["cc_m1_in"], T["cc_m1_out"], G["ccs"])
        p.cc("AllGather", ALU.bypass, T["cc_st_in"], T["cc_st_out"], G["ccs"])
        return [ld, mm, ev, st, aux]


def _prep_host(inputs):
    h = {}
    x = np.ascontiguousarray(inputs["L0_attn_x"], dtype=np.float32)
    h["x_slices"] = [np.ascontiguousarray(x[:, c * 256:(c + 1) * 256, :].reshape(R, D))
                     for c in range(NCORES)]
    h["w1t"] = np.ascontiguousarray(inputs["L0_attn_W1"].T)
    h["identb"] = np.eye(128).astype(ml_dtypes.bfloat16)
    h["identf"] = np.eye(128, dtype=np.float32)
    h["triu"] = np.triu(np.ones((128, 128), np.float32))
    h["w2t"] = np.ascontiguousarray(inputs["L0_attn_W2"].T)
    h["wup"] = np.ascontiguousarray(inputs["L0_ffn_w_up"])
    h["wgate"] = np.ascontiguousarray(inputs["L0_ffn_w_gate"])
    h["wupt"] = np.ascontiguousarray(inputs["L0_ffn_w_up"].T)
    h["wgatet"] = np.ascontiguousarray(inputs["L0_ffn_w_gate"].T)
    h["wdown"] = np.ascontiguousarray(inputs["L0_ffn_w_down"])
    perm = np.zeros(3072, np.int64)
    for j in range(3):
        for hh in range(16):
            for d in range(64):
                perm[j * 1024 + hh * 64 + d] = hh * 192 + j * 64 + d
    def _swz(xT, K, OC, J):
        return np.ascontiguousarray(
            xT.reshape(K, 128, OC, J).transpose(2, 1, 0, 3).reshape(OC * 128, K * J))
    h["ipw_t"] = _swz(inputs["L1_attn_in_proj_weight"][perm, :].T, 8, 6, 512)
    h["ipb_row"] = np.ascontiguousarray(inputs["L1_attn_in_proj_bias"][perm][None, :])
    h["opw_t"] = _swz(inputs["L1_attn_out_proj_weight"].T, 8, 8, 128)
    h["opb_row"] = np.ascontiguousarray(inputs["L1_attn_out_proj_bias"][None, :])
    h["rt_t"] = _swz(inputs["L1_ffn_router_w"].T, 8, 1, 32)
    h["rb_row"] = np.ascontiguousarray(inputs["L1_ffn_router_bias"][None, :])
    h["gat"] = _swz(inputs["L1_ffn_gate_w"].reshape(4096, 1024).T, 8, 32, 128)
    h["upt"] = _swz(inputs["L1_ffn_up_w"].reshape(4096, 1024).T, 8, 32, 128)
    h["dat"] = _swz(inputs["L1_ffn_down_w"].transpose(0, 2, 1).reshape(4096, 1024), 32, 8, 128)
    h["sgt"] = _swz(inputs["L1_ffn_sgate_w"].T, 8, 4, 128)
    h["sut"] = _swz(inputs["L1_ffn_sup_w"].T, 8, 4, 128)
    h["sdt"] = _swz(inputs["L1_ffn_sdown_w"].T, 4, 8, 128)
    pm = np.zeros((NCORES, 64, 8), np.float32)
    for c in range(NCORES):
        pm[c, :, :c] = 1.0
    h["pmask"] = pm
    return h


def build_fused():
    nc = bass.Bass("TRN2", target_bir_lowering=False)
    ap = lambda n, sh, dt, kind: nc.dram_tensor(n, sh, dt, kind=kind).ap()
    T = {}
    T["xin"] = ap("xin", [R, D], F32, "ExternalInput")
    T["w1t"] = ap("w1t", [D, 3 * D], F32, "ExternalInput")
    T["w2t"] = ap("w2t", [D, D], F32, "ExternalInput")
    T["wup"] = ap("wup", [3 * D, D], F32, "ExternalInput")
    T["wgate"] = ap("wgate", [3 * D, D], F32, "ExternalInput")
    T["wdown"] = ap("wdown", [D, 3 * D], F32, "ExternalInput")
    T["ipw_t"] = ap("ipw_t", [768, 4096], F32, "ExternalInput")
    T["ipb_row"] = ap("ipb_row", [1, 3 * D], F32, "ExternalInput")
    T["opw_t"] = ap("opw_t", [1024, 1024], F32, "ExternalInput")
    T["opb_row"] = ap("opb_row", [1, D], F32, "ExternalInput")
    T["rt_t"] = ap("rt_t", [128, 256], F32, "ExternalInput")
    T["rb_row"] = ap("rb_row", [1, 32], F32, "ExternalInput")
    T["gat"] = ap("gat", [4096, 1024], F32, "ExternalInput")
    T["upt"] = ap("upt", [4096, 1024], F32, "ExternalInput")
    T["dat"] = ap("dat", [1024, 4096], F32, "ExternalInput")
    T["sgt"] = ap("sgt", [512, 1024], F32, "ExternalInput")
    T["sut"] = ap("sut", [512, 1024], F32, "ExternalInput")
    T["sdt"] = ap("sdt", [1024, 512], F32, "ExternalInput")
    T["identb"] = ap("identb", [128, 128], BF16, "ExternalInput")
    T["identf"] = ap("identf", [128, 128], F32, "ExternalInput")
    T["triu"] = ap("triu", [128, 128], F32, "ExternalInput")
    T["pmask_i"] = ap("pmask_i", [64, 8], F32, "ExternalInput")
    T["out_o"] = ap("out_o", [R, D], F32, "ExternalOutput")
    T["xn_d"] = ap("xn_d", [R, D], F32, "Internal")
    T["phiq_fm_d"] = ap("phiq_fm_d", [128, KC * R], BF16, "Internal")
    T["phik_fm_d"] = ap("phik_fm_d", [128, KC * R], BF16, "Internal")
    T["v_tm_d"] = ap("v_tm_d", [R, NH * (HD + 1)], BF16, "Internal")
    T["sc0_d"] = ap("sc0_d", [64, 128 * 65], F32, "Internal")
    T["xa_d"] = ap("xa_d", [R, D], F32, "Internal")
    T["up_raw_d"] = ap("up_raw_d", [128, 24 * R], BF16, "Internal")
    T["gate_raw_d"] = ap("gate_raw_d", [128, 24 * R], BF16, "Internal")
    T["upq_d"] = ap("upq_d", [128, 24 * R], BF16, "Internal")
    T["t_d"] = ap("t_d", [128, 24 * R], BF16, "Internal")
    T["oraw_d"] = ap("oraw_d", [128, 24 * R], BF16, "Internal")
    T["cc_st_in"] = ap("cc_st_in", [64, 128 * 65], F32, "Internal")
    T["cc_st_out"] = nc.dram_tensor("cc_st_out", [512, 128 * 65], F32,
                                    addr_space="Shared").ap()
    T["cc_m1_in"] = ap("cc_m1_in", [1, 1], F32, "Internal")
    T["cc_m1_out"] = nc.dram_tensor("cc_m1_out", [1, 1], F32, addr_space="Shared").ap()
    T["cc_m23_in"] = ap("cc_m23_in", [1, 2], F32, "Internal")
    T["cc_m23_out"] = nc.dram_tensor("cc_m23_out", [1, 2], F32, addr_space="Shared").ap()
    T["cc_m4_in"] = ap("cc_m4_in", [1, 1], F32, "Internal")
    T["cc_m4_out"] = nc.dram_tensor("cc_m4_out", [1, 1], F32, addr_space="Shared").ap()
    T["cc_m5_in"] = ap("cc_m5_in", [1, 1], F32, "Internal")
    T["cc_m5_out"] = nc.dram_tensor("cc_m5_out", [1, 1], F32, addr_space="Shared").ap()
    T["scr_t"] = nc.dram_tensor("scr", [1, 16], F32)
    T["wt_dram"] = nc.dram_tensor("wt_dram", [32, R], F32)

    with ExitStack() as octx:
        sbuf_o = lambda n, sh, dt: octx.enter_context(nc.sbuf_tensor(n, sh, dt))[:, :]
        G = {}
        G["idb"] = sbuf_o("g_idb", [128, 128], BF16)
        G["idf"] = sbuf_o("g_idf", [128, 128], F32)
        G["bardum"] = sbuf_o("g_bardum", [128, 8], F32)
        p = Prog(nc)
        p.sems = {"lncp": Sem(octx.enter_context(nc.semaphore("s_lncp")))}
        G["ccs"] = Sem(octx.enter_context(nc.semaphore("s_cc")))
        gbar = Sem(octx.enter_context(nc.semaphore("s_gbar")))
        g_ld = Sem(octx.enter_context(nc.semaphore("s_gld")))

        p.dma("sync", G["idb"], T["identb"], g_ld)
        p.dma("sync", G["idf"], T["identf"], g_ld)
        for ek in Prog.ENGINES:
            p.wait(ek, g_ld)

        def gbarrier(sems):
            _phase_barrier(p, sems + [p.sems["lncp"]],
                           G["bardum"][0:1, 0:1], G["bardum"][0:1, 1:2],
                           G["bardum"][0:1, 2:3], gbar)

        sems1 = emit_phase1(nc, p, octx, T, G)
        gbarrier(sems1)
        sems2 = emit_phase2(nc, p, octx, T, G)
        gbarrier(sems2)
        sems3 = emit_phase3(nc, p, octx, T, G)
        gbarrier(sems3)
        sems4 = emit_phase4(nc, p, octx, T, G)
        gbarrier(sems4)
        emit_phase5(nc, p, octx, T, G)
        p.build()
    return nc


def kernel(**inputs):
    h = _prep_host(inputs)
    cores = list(range(NCORES))
    nc = build_fused()
    im = [{
        "xin": h["x_slices"][c], "w1t": h["w1t"], "w2t": h["w2t"],
        "wup": h["wup"], "wgate": h["wgate"], "wdown": h["wdown"],
        "ipw_t": h["ipw_t"], "ipb_row": h["ipb_row"],
        "opw_t": h["opw_t"], "opb_row": h["opb_row"],
        "rt_t": h["rt_t"], "rb_row": h["rb_row"],
        "gat": h["gat"], "upt": h["upt"], "dat": h["dat"],
        "sgt": h["sgt"], "sut": h["sut"], "sdt": h["sdt"],
        "identb": h["identb"], "identf": h["identf"], "triu": h["triu"],
        "pmask_i": h["pmask"][c],
    } for c in cores]
    r = run_bass_kernel_spmd(nc, im, core_ids=cores).results
    out = np.zeros((8, 2048, 1024), np.float32)
    for c in cores:
        out[:, c * 256:(c + 1) * 256, :] = r[c]["out_o"].reshape(8, 256, 1024)
    return out


def emit_phase2(nc, p, octx, T, G):
    xin = T["xin"]
    xn_i = T["xn_d"]
    phiq_fm, phik_fm = T["phiq_fm_d"], T["phik_fm_d"]
    v_tm = T["v_tm_d"]
    sc0_i = T["sc0_d"]
    st_ag = T["cc_st_out"]
    w2t, wup, wgate = T["w2t"], T["wup"], T["wgate"]
    xa_o = T["xa_d"]
    up_raw_o, gate_raw_o = T["up_raw_d"], T["gate_raw_d"]
    maxp2_o = T["cc_m23_in"][0:1, 0:1]
    maxp3_o = T["cc_m23_in"][0:1, 1:2]
    scr = T["scr_t"]

    BLK = NH * 65

    with ExitStack() as ctx:
        sbuf = lambda n, sh, dt: ctx.enter_context(nc.sbuf_tensor("p2" + n, sh, dt))[:, :]
        psum = lambda n, sh, dt: ctx.enter_context(nc.psum_tensor("p2" + n, sh, dt))[:, :]
        sem = lambda n: octx.enter_context(nc.semaphore("p2" + n))
        sinf = sbuf("sinf", [64, BLK], F32)
        smidf = sbuf("smidf", [64, BLK], F32)
        stg = sbuf("stg", [64, BLK], F32)
        sin_all = sbuf("sin_all", [128, 8 * 8 * 65], BF16)
        smid_all = sbuf("smid_all", [128, 8 * 8 * 65], BF16)
        pq_b = sbuf("pq_b", [128, 2 * KC * 256], BF16)
        pk_b = sbuf("pk_b", [128, 2 * KC * 256], BF16)
        v_b = sbuf("v_b", [128, 2 * 2 * BLK], BF16)
        att_bf = sbuf("att_bf", [128, 2 * 512], BF16)
        rec4 = sbuf("rec4", [128, 8], F32)
        ctxt = sbuf("ctxt", [128, D], F32)
        ctx_bf = sbuf("ctx_bf", [128, D], BF16)
        ctx_fm = sbuf("ctx_fm", [128, NT * D], BF16)
        w2b = sbuf("w2b", [128, KC * D], BF16)
        w2o = sbuf("w2o", [128, 512], F32)
        xa_st = sbuf("xa_st", [128, 4 * D], F32)
        xnt = sbuf("xnt", [128, 2 * D], F32)
        xq_i32 = sbuf("xq_i32", [128, D], I32)
        dm_s = sbuf("dm_s", [128, D], F32)
        rr_s = sbuf("rr_s", [128, D], F32)
        xq_bf = sbuf("xq_bf", [128, D], BF16)
        xq_fm = sbuf("xq_fm", [128, KC * R], BF16)
        wtile = sbuf("wtile", [128, 2 * D], F32)
        bmx = sbuf("bmx", [128, 2 * 32], F32)
        sblk = sbuf("sblk", [128, 2 * 32], F32)
        rsblk = sbuf("rsblk", [128, 2 * 32], F32)
        yw = sbuf("yw", [128, 2 * D], F32)
        runm = sbuf("runm", [128, 2], F32)
        cols = sbuf("cols", [128, 4], F32)
        m1sc = sbuf("m1sc", [1, 8], F32)
        mx1 = sbuf("mx1", [1, 128], F32)
        trm = sbuf("trm", [128, 128], F32)
        pmaskb = sbuf("pmask", [64, 8], F32)
        idb, idf = G["idb"], G["idf"]
        pb0 = psum("pb0", [128, 512], F32)
        pb1 = psum("pb1", [128, 512], F32)
        pb2 = psum("pb2", [128, 512], F32)
        pb3 = psum("pb3", [128, 512], F32)
        ps_t = psum("ps_t", [128, D], BF16)
        ps_tf = psum("ps_tf", [128, 512], F32)
        ps_mx = psum("ps_mx", [1, 128], F32)
        s_ld = sem("s_ld"); s_mm = sem("s_mm"); s_ev = sem("s_ev")
        s_st = sem("s_st"); s_aux = sem("s_aux"); s_cp = sem("s_cp")
        slot_sems = [sem(f"s2_sl{i}") for i in range(6)]

        ld = Sem(s_ld); mm = Sem(s_mm); ev = Sem(s_ev); st = Sem(s_st)
        aux = Sem(s_aux); cp = Sem(s_cp)
        sA = [Sem(slot_sems[0]), Sem(slot_sems[1])]
        sW = [Sem(slot_sems[2]), Sem(slot_sems[3])]
        sX = [Sem(slot_sems[4]), Sem(slot_sems[5])]

        p.dma("sync", trm, T["triu"], ld)
        p.dma("sync", pmaskb, T["pmask_i"], ld)

        p.wait("sync", G["ccs"], at=1)
        p.dma("sync", m1sc[0:1, 0:1], T["cc_m1_out"], ld)
        ld0 = ld.n
        p.wait("vector", ld, ld0)
        p.emit("vector", lambda e: e.tensor_scalar(out=m1sc[0:1, 1:2], in0=m1sc[0:1, 0:1],
                                                   scalar1=1.0 / 32767.0, scalar2=1e-12, op0=ALU.mult, op1=ALU.max))
        p.drain("vector")
        p.emit("vector", lambda e: e.reciprocal(out=m1sc[0:1, 2:3], in_=m1sc[0:1, 1:2]), inc=cp)
        p.wait("sync", cp)
        p.dma("sync", scr.ap()[0:1, 0:2], m1sc[0:1, 1:3], aux)
        p.wait("gpsimd", aux)
        p.dma("gpsimd", cols[:, 0:1], scr.ap()[0:1, 0:1].broadcast_to([128, 1]), ld)
        p.dma("gpsimd", cols[:, 1:2], scr.ap()[0:1, 1:2].broadcast_to([128, 1]), ld)
        p.emit("vector", lambda e: e.memset(cols[:, 2:3], 32768.0))
        p.emit("vector", lambda e: e.memset(runm, 0.0), inc=cp)

        p.wait("scalar", ld, ld.n)
        p.wait("scalar", cp, cp.n)
        p.wait("vector", ld, ld.n)
        xn_free = [0, 0]
        for t in range(NT):
            sl = t % 2
            p.wait("sync", cp, at=xn_free[sl])
            p.dma("sync", xnt[:, sl * D:(sl + 1) * D], xn_i[t * 128:(t + 1) * 128, :], sX[sl])
            p.wait("scalar", sX[sl])
            xsl = xnt[:, sl * D:(sl + 1) * D]
            p.emit("scalar", lambda e, xsl=xsl: e.activation(out=xq_i32, in_=xsl, func=AF.Identity,
                                                             bias=cols[:, 2:3], scale=cols[:, 1:2]), inc=cp)
            p.wait("vector", cp, at=cp.n)
            p.wait("vector", sX[sl])
            p.emit("vector", lambda e, xsl=xsl: e.scalar_tensor_tensor(out=dm_s, in0=xsl, scalar=cols[:, 1:2],
                                                                       in1=xq_i32, op0=ALU.mult, op1=ALU.subtract))
            p.drain("vector")
            p.emit("vector", lambda e: e.scalar_tensor_tensor(out=rr_s, in0=dm_s, scalar=-32767.5,
                                                              in1=xq_i32, op0=ALU.is_ge, op1=ALU.add))
            p.drain("vector")
            p.emit("vector", lambda e: e.tensor_scalar(out=xq_bf, in0=rr_s, scalar1=32768.0,
                                                       scalar2=cols[:, 0:1], op0=ALU.subtract, op1=ALU.mult), inc=cp)
            xn_free[sl] = cp.n
            p.wait("tensor", cp, at=cp.n)
            p.wait("tensor", ev)
            for c in range(KC):
                p.emit("tensor", lambda e, c=c: e.transpose(
                    ps_t[:, c * 128:(c + 1) * 128], xq_bf[:, c * 128:(c + 1) * 128], idb),
                    inc=(mm if c == KC - 1 else None))
            p.wait("scalar", mm, at=mm.n)
            p.emit("scalar", lambda e, t=t: e.activation(
                out=xq_fm.rearrange("p (c j) -> p c j", j=R)[:, :, t * 128:(t + 1) * 128],
                in_=ps_t.rearrange("p (c j) -> p c j", j=128), func=AF.Copy), inc=ev)
            p.wait("vector", ev)

        p.wait("sync", G["ccs"], at=2)
        for b in range(8):
            p.wait("vector", ld)
            p.emit("vector", lambda e: e.memset(sinf, 0.0), inc=cp)
            for j in range(8):
                p.wait("sync", cp)
                p.dma("sync", stg, st_ag[64 * j:64 * (j + 1), b * BLK:(b + 1) * BLK], ld)
                p.wait("vector", ld)
                p.emit("vector", lambda e, j=j: e.scalar_tensor_tensor(
                    out=sinf, in0=stg, scalar=pmaskb[0:64, j:j + 1],
                    in1=sinf, op0=ALU.mult, op1=ALU.add), inc=cp)
            p.wait("sync", cp)
            p.dma("sync", stg, sc0_i[0:64, b * BLK:(b + 1) * BLK], ld)
            p.wait("vector", ld)
            p.emit("vector", lambda e: e.tensor_tensor(out=smidf, in0=sinf, in1=stg, op=ALU.add), inc=cp)
            p.wait("gpsimd", cp)
            for src, dst in ((sinf, sin_all), (smidf, smid_all)):
                sv = src.rearrange("p (h2 two e) -> p h2 two e", two=2, e=65)
                dv = dst.rearrange("p (b h2 e) -> p b h2 e", b=8, e=65)
                p.dma("gpsimd", dv[0:64, b, :, :], sv[0:64, :, 0, :], ld)
                p.dma("gpsimd", dv[64:128, b, :, :], sv[0:64, :, 1, :], ld)
        states_ld = ld.n

        p.wait("vector", ld, states_ld)
        batch_free = [0, 0]
        att_free = [0, 0]
        y_free = [0, 0]
        for b in range(8):
            sl = b % 2
            p.wait("gpsimd", mm, at=batch_free[sl])
            p.dma("gpsimd", pq_b[:, sl * KC * 256:(sl + 1) * KC * 256],
                  phiq_fm[:, b * KC * 256:(b + 1) * KC * 256], sA[sl])
            p.dma("gpsimd", pk_b[:, sl * KC * 256:(sl + 1) * KC * 256],
                  phik_fm[:, b * KC * 256:(b + 1) * KC * 256], sA[sl])
            p.dma("gpsimd", v_b.rearrange("p (s c j) -> p s c j", s=2, j=BLK)[:, sl, :, :],
                  v_tm[b * 256:(b + 1) * 256, :].rearrange("(c p) j -> p c j", p=128), sA[sl])
            p.wait("tensor", sA[sl])
            pqs = pq_b[:, sl * KC * 256:(sl + 1) * KC * 256]
            pks = pk_b[:, sl * KC * 256:(sl + 1) * KC * 256]
            vbs = v_b[:, sl * 2 * BLK:(sl + 1) * 2 * BLK]
            for chn in range(2):
                for g in range(4):
                    gp = g % 2
                    par = g // 2
                    base4 = (g % 2) * 4
                    heads = [2 * (base4 + i) + par for i in range(4)]
                    patt = pb0 if gp == 0 else pb1
                    py = pb2 if gp == 0 else pb3
                    absl = att_bf[:, gp * 512:(gp + 1) * 512]
                    p.wait("tensor", ev, at=att_free[gp])
                    for hi in range(4):
                        hh = heads[hi]
                        cc, po = hh // 2, (hh % 2) * 64
                        lk = pks[po:po + 64, cc * 256 + chn * 128: cc * 256 + (chn + 1) * 128]
                        lq = pqs[po:po + 64, cc * 256 + chn * 128: cc * 256 + (chn + 1) * 128]
                        p.emit("tensor", lambda e, patt=patt, hi=hi, lk=lk, lq=lq: e.matmul(
                            patt[:, hi * 128:(hi + 1) * 128], lk, lq, start=True, stop=True),
                            inc=(mm if hi == 3 else None))
                    p.wait("vector", mm, at=mm.n)
                    p.emit("vector", lambda e, patt=patt, absl=absl: e.tensor_tensor(
                        out=absl.rearrange("p (x j) -> p x j", j=128),
                        in0=patt.rearrange("p (x j) -> p x j", j=128),
                        in1=trm.rearrange("p (o j) -> p o j", o=1).broadcast_to([128, 4, 128]),
                        op=ALU.mult), inc=ev)
                    att_free[gp] = ev.n
                    p.wait("tensor", ev, at=ev.n)
                    p.wait("tensor", ev, at=y_free[gp])
                    sall = sin_all if chn == 0 else smid_all
                    for hi in range(4):
                        hh = heads[hi]
                        cc, po = hh // 2, (hh % 2) * 64
                        lq = pqs[po:po + 64, cc * 256 + chn * 128: cc * 256 + (chn + 1) * 128]
                        p.emit("tensor", lambda e, py=py, hi=hi, absl=absl, vbs=vbs, chn=chn, hh=hh: e.matmul(
                            py[:, hi * 65:hi * 65 + 65], absl[:, hi * 128:(hi + 1) * 128],
                            vbs[:, chn * BLK + hh * 65: chn * BLK + (hh + 1) * 65],
                            start=True, stop=False))
                        p.emit("tensor", lambda e, py=py, hi=hi, lq=lq, sall=sall, b=b, hh=hh, po=po: e.matmul(
                            py[:, hi * 65:hi * 65 + 65], lq,
                            sall[po:po + 64, (b * 8 + hh // 2) * 65:(b * 8 + hh // 2 + 1) * 65],
                            start=False, stop=True), inc=(mm if hi == 3 else None))
                    p.wait("vector", mm, at=mm.n)
                    rsl = rec4[:, gp * 4:(gp + 1) * 4]
                    p.emit("vector", lambda e, py=py, rsl=rsl: e.reciprocal(
                        out=rsl, in_=py[:, 0:260].rearrange("p (x e) -> p x e", e=65)[:, :, 64:65]))
                    p.drain("vector")
                    p.emit("vector", lambda e, py=py, par=par, base4=base4, rsl=rsl: e.tensor_tensor(
                        out=ctxt.rearrange("p (h2 par d) -> p h2 par d", par=2, d=64)[:, base4:base4 + 4, par, :],
                        in0=py[:, 0:260].rearrange("p (x e) -> p x e", e=65)[:, :, 0:64],
                        in1=rsl.rearrange("p (x o) -> p x o", o=1).broadcast_to([128, 4, 64]),
                        op=ALU.mult), inc=ev)
                    y_free[gp] = ev.n
                p.wait("scalar", ev)
                p.emit("scalar", lambda e: e.activation(out=ctx_bf, in_=ctxt, func=AF.Copy), inc=cp)
                p.wait("tensor", cp, at=cp.n)
                p.wait("tensor", ev)
                for c in range(KC):
                    p.emit("tensor", lambda e, c=c: e.transpose(
                        ps_t[:, c * 128:(c + 1) * 128], ctx_bf[:, c * 128:(c + 1) * 128], idb),
                        inc=(mm if c == KC - 1 else None))
                p.wait("scalar", mm, at=mm.n)
                t_idx = b * 2 + chn
                p.emit("scalar", lambda e, t_idx=t_idx: e.activation(
                    out=ctx_fm.rearrange("p (c j) -> p c j", j=R)[:, :, t_idx * 128:(t_idx + 1) * 128],
                    in_=ps_t.rearrange("p (c j) -> p c j", j=128), func=AF.Copy), inc=ev)
                p.wait("vector", ev)
            batch_free[sl] = mm.n

        for c in range(KC):
            p.dma("gpsimd", w2b[:, c * D:(c + 1) * D], w2t[c * 128:(c + 1) * 128, :], ld)
        w2_ld = ld.n
        p.wait("tensor", ld, w2_ld)
        p.wait("tensor", ev)
        for tokc in range(4):
            p.wait("sync", ev if tokc == 0 else st)
            for tb in range(4):
                p.dma("sync", xa_st[:, tb * D:(tb + 1) * D],
                      xin[(tokc * 4 + tb) * 128:(tokc * 4 + tb + 1) * 128, :], ld)
            xa_ld = ld.n
            p.wait("vector", ld, xa_ld)
            for oc in range(KC):
                pp = pb0 if oc % 2 == 0 else pb1
                for c in range(KC):
                    p.emit("tensor", lambda e, pp=pp, oc=oc, c=c, tokc=tokc: e.matmul(
                        pp, w2b[:, c * D + oc * 128: c * D + (oc + 1) * 128],
                        ctx_fm[:, c * R + tokc * 512: c * R + (tokc + 1) * 512],
                        start=(c == 0), stop=(c == KC - 1)), inc=(mm if c == KC - 1 else None))
                p.wait("scalar", mm, at=mm.n)
                p.wait("scalar", ev)
                p.emit("scalar", lambda e, pp=pp: e.activation(out=w2o, in_=pp, func=AF.Copy), inc=ev)
                p.wait("tensor", ev, at=ev.n)
                for tb in range(4):
                    p.emit("tensor", lambda e, tb=tb: e.transpose(
                        ps_tf[:, tb * 128:(tb + 1) * 128], w2o[:, tb * 128:(tb + 1) * 128], idf),
                        inc=(mm if tb == 3 else None))
                p.wait("vector", mm, at=mm.n)
                p.emit("vector", lambda e, oc=oc: e.tensor_tensor(
                    out=xa_st.rearrange("p (t j) -> p t j", j=D)[:, :, oc * 128:(oc + 1) * 128],
                    in0=xa_st.rearrange("p (t j) -> p t j", j=D)[:, :, oc * 128:(oc + 1) * 128],
                    in1=ps_tf.rearrange("p (t j) -> p t j", j=128), op=ALU.add), inc=ev)
                p.wait("tensor", ev)
            p.wait("sync", ev)
            for tb in range(4):
                p.dma("sync", xa_o[(tokc * 4 + tb) * 128:(tokc * 4 + tb + 1) * 128, :],
                      xa_st[:, tb * D:(tb + 1) * D], st)

        iw0 = dm_s.bitcast(I32); iw1 = rr_s.bitcast(I32)
        dw = xnt
        wq_bf = ctxt.bitcast(BF16)
        wlhs = xa_st[:, 0:1024].bitcast(BF16)
        upst = xa_st[:, 2048:3072]
        p.emit("vector", lambda e: e.memset(m1sc[0:1, 4:5], 16.0))
        p.emit("vector", lambda e: e.memset(cols[:, 3:4], 16.0), inc=cp)
        p.wait("scalar", cp)
        w_free = [0, 0]
        lhs_free = [0, 0]
        pp_ev = [0, 0]
        for wi, (wsrc, dst) in enumerate(((wup, up_raw_o), (wgate, gate_raw_o))):
            for oci in range(24):
                oc = oci
                sl = (wi * 24 + oci) % 2
                wts = wtile[:, sl * D:(sl + 1) * D]
                yws = yw[:, sl * D:(sl + 1) * D]
                iws = iw0 if sl == 0 else iw1
                dws = dw[:, sl * D:(sl + 1) * D]
                wqs = wq_bf[:, sl * D:(sl + 1) * D]
                wls = wlhs[:, sl * D:(sl + 1) * D]
                bms = bmx[:, sl * 32:(sl + 1) * 32]
                sbs = sblk[:, sl * 32:(sl + 1) * 32]
                rss = rsblk[:, sl * 32:(sl + 1) * 32]
                p.wait("sync", cp, at=w_free[sl])
                p.dma("sync", wts, wsrc[oc * 128:(oc + 1) * 128, :], sW[sl])
                p.wait("vector", sW[sl])
                p.emit("vector", lambda e, wts=wts, bms=bms: e.tensor_reduce(
                    out=bms, in_=wts.rearrange("p (g x) -> p g x", x=32), axis=AX.X,
                    op=ALU.max, apply_absolute_value=True))
                p.drain("vector")
                p.emit("vector", lambda e, bms=bms, sbs=sbs: e.tensor_scalar(
                    out=sbs, in0=bms, scalar1=1.0 / 7.0, scalar2=1e-12, op0=ALU.mult, op1=ALU.max))
                p.drain("vector")
                p.emit("vector", lambda e, sbs=sbs, rss=rss: e.reciprocal(out=rss, in_=sbs))
                p.drain("vector")
                p.emit("vector", lambda e, wts=wts, yws=yws, rss=rss: e.tensor_tensor(
                    out=yws.rearrange("p (g x) -> p g x", x=32), in0=wts.rearrange("p (g x) -> p g x", x=32),
                    in1=rss.rearrange("p (g o) -> p g o", o=1).broadcast_to([128, 32, 32]), op=ALU.mult), inc=cp)
                w_free[sl] = cp.n
                p.wait("scalar", cp, at=cp.n)
                p.emit("scalar", lambda e, yws=yws, iws=iws: e.activation(
                    out=iws, in_=yws, func=AF.Identity, bias=cols[:, 3:4], scale=1.0), inc=cp)
                p.wait("vector", cp, at=cp.n)
                p.emit("vector", lambda e, yws=yws, iws=iws, dws=dws: e.scalar_tensor_tensor(
                    out=dws, in0=yws, scalar=16.0, in1=iws, op0=ALU.add, op1=ALU.subtract))
                p.drain("vector")
                p.emit("vector", lambda e, dws=dws, iws=iws: e.scalar_tensor_tensor(
                    out=dws, in0=dws, scalar=0.5, in1=iws, op0=ALU.is_ge, op1=ALU.add))
                p.drain("vector")
                p.emit("vector", lambda e, dws=dws: e.tensor_scalar(
                    out=dws, in0=dws, scalar1=16.0, scalar2=None, op0=ALU.subtract))
                p.drain("vector")
                p.emit("vector", lambda e, dws=dws, wqs=wqs, sbs=sbs: e.tensor_tensor(
                    out=wqs.rearrange("p (g x) -> p g x", x=32), in0=dws.rearrange("p (g x) -> p g x", x=32),
                    in1=sbs.rearrange("p (g o) -> p g o", o=1).broadcast_to([128, 32, 32]), op=ALU.mult), inc=cp)
                p.wait("tensor", cp, at=cp.n)
                p.wait("tensor", mm, at=lhs_free[sl])
                p.wait("tensor", ev)
                for c in range(KC):
                    p.emit("tensor", lambda e, c=c, wqs=wqs: e.transpose(
                        ps_t[:, c * 128:(c + 1) * 128], wqs[:, c * 128:(c + 1) * 128], idb),
                        inc=(mm if c == KC - 1 else None))
                p.wait("scalar", mm, at=mm.n)
                p.emit("scalar", lambda e, wls=wls: e.activation(out=wls, in_=ps_t, func=AF.Copy), inc=ev)
                p.wait("tensor", ev, at=ev.n)
                for tokc in range(4):
                    pp = pb0 if tokc % 2 == 0 else pb1
                    usl = upst[:, (tokc % 2) * 512:(tokc % 2 + 1) * 512]
                    p.wait("tensor", ev, at=pp_ev[tokc % 2])
                    for c in range(KC):
                        p.emit("tensor", lambda e, pp=pp, c=c, tokc=tokc, wls=wls: e.matmul(
                            pp, wls[:, c * 128:(c + 1) * 128],
                            xq_fm[:, c * R + tokc * 512: c * R + (tokc + 1) * 512],
                            start=(c == 0), stop=(c == KC - 1)), inc=(mm if c == KC - 1 else None))
                    if tokc == 3:
                        lhs_free[sl] = mm.n
                    p.wait("scalar", mm, at=mm.n)
                    p.wait("scalar", st)
                    p.wait("scalar", cp)
                    p.emit("scalar", lambda e, usl=usl, pp=pp: e.activation(out=usl, in_=pp, func=AF.Copy), inc=ev)
                    pp_ev[tokc % 2] = ev.n
                    p.wait("gpsimd", ev, at=ev.n)
                    p.dma("gpsimd", dst.rearrange("p (o j) -> p o j", j=R)[:, oc, tokc * 512:(tokc + 1) * 512], usl, st)
                    p.wait("vector", ev, at=ev.n)
                    p.emit("vector", lambda e, usl=usl: e.tensor_reduce(
                        out=rec4[:, 0:1], in_=usl, axis=AX.X, op=ALU.max, apply_absolute_value=True))
                    p.drain("vector")
                    p.emit("vector", lambda e, wi=wi: e.tensor_tensor(
                        out=runm[:, wi:wi + 1], in0=runm[:, wi:wi + 1], in1=rec4[:, 0:1], op=ALU.max), inc=cp)
        for wi, dst in ((0, maxp2_o), (1, maxp3_o)):
            p.wait("tensor", cp)
            if wi == 1:
                p.wait("tensor", ev)
            p.emit("tensor", lambda e, wi=wi: e.transpose(ps_mx[0:1, 0:128], runm[:, wi:wi + 1], idf), inc=mm)
            p.wait("vector", mm, at=mm.n)
            p.emit("vector", lambda e: e.tensor_copy(out=mx1, in_=ps_mx[0:1, :]), inc=ev)
            p.drain("vector")
            p.emit("vector", lambda e: e.reduce_max(out=m1sc[0:1, 3:4], in_=mx1[0:1, :], axis=AX.X), inc=ev)
            p.wait("sync", ev)
            p.dma("sync", dst, m1sc[0:1, 3:4], st)
            p.wait("vector", st)

        p.wait("gpsimd", st)
        p.cc("AllReduce", ALU.max, T["cc_m23_in"], T["cc_m23_out"], G["ccs"])
        return [ld, mm, ev, st, aux, cp] + sA + sW + sX


def _phase_barrier(p, sems, dum_v, dum_s, dum_g, bar):
    p.emit("vector", lambda e, d=dum_v: e.memset(d, 0.0), inc=bar)
    p.emit("scalar", lambda e, d=dum_s: e.activation(out=d, in_=d, func=AF.Copy), inc=bar)
    p.emit("gpsimd", lambda e, d=dum_g: e.memset(d, 0.0), inc=bar)
    tgt = bar.n
    for ek in ("sync", "scalar", "vector", "tensor", "gpsimd"):
        p.wait(ek, bar, at=tgt)
        for sm in sems:
            p.wait(ek, sm)


def _emit_scales(p, nc_scr, mrow, msc, cols_s, cols_rs, ld, cp, aux, idx):
    p.emit("vector", lambda e: e.reduce_max(out=msc[0:1, 0:1], in_=mrow, axis=AX.X))
    p.drain("vector")
    p.emit("vector", lambda e: e.tensor_scalar(out=msc[0:1, 1:2], in0=msc[0:1, 0:1],
                                               scalar1=1.0 / 32767.0, scalar2=1e-12, op0=ALU.mult, op1=ALU.max))
    p.drain("vector")
    p.emit("vector", lambda e: e.reciprocal(out=msc[0:1, 2:3], in_=msc[0:1, 1:2]), inc=cp)
    p.wait("sync", cp)
    p.dma("sync", nc_scr.ap()[0:1, idx:idx + 2], msc[0:1, 1:3], aux)
    p.wait("gpsimd", aux)
    p.dma("gpsimd", cols_s, nc_scr.ap()[0:1, idx:idx + 1].broadcast_to([128, 1]), ld)
    p.dma("gpsimd", cols_rs, nc_scr.ap()[0:1, idx + 1:idx + 2].broadcast_to([128, 1]), ld)


def _emit_qdq(p, src, dst, i32b, dmb, s_col, rs_col, b_col, cp):
    p.wait("scalar", cp)
    p.emit("scalar", lambda e: e.activation(out=i32b, in_=src, func=AF.Identity,
                                            bias=b_col, scale=rs_col), inc=cp)
    p.wait("vector", cp)
    p.emit("vector", lambda e: e.scalar_tensor_tensor(out=dmb, in0=src, scalar=rs_col,
                                                      in1=i32b, op0=ALU.mult, op1=ALU.subtract))
    p.drain("vector")
    p.emit("vector", lambda e: e.scalar_tensor_tensor(out=dmb, in0=dmb, scalar=-32767.5,
                                                      in1=i32b, op0=ALU.is_ge, op1=ALU.add))
    p.drain("vector")
    p.emit("vector", lambda e: e.tensor_scalar(out=dst, in0=dmb, scalar1=32768.0,
                                               scalar2=s_col, op0=ALU.subtract, op1=ALU.mult), inc=cp)


def _emit_colmax_out(p, runcol, mx1, msc_sl, ps_mx, idf, dst, mm, ev, st):
    p.emit("tensor", lambda e: e.transpose(ps_mx[0:1, 0:128], runcol, idf), inc=mm)
    p.wait("vector", mm)
    p.emit("vector", lambda e: e.tensor_copy(out=mx1, in_=ps_mx[0:1, :]), inc=ev)
    p.drain("vector")
    p.emit("vector", lambda e: e.reduce_max(out=msc_sl, in_=mx1[0:1, :], axis=AX.X), inc=ev)
    p.wait("sync", ev)
    p.dma("sync", dst, msc_sl, st)
    p.wait("vector", st)


def emit_phase3(nc, p, octx, T, G):
    up_raw_i, gate_raw_i = T["up_raw_d"], T["gate_raw_d"]
    upq_o, t_o = T["upq_d"], T["t_d"]
    maxp4_o = T["cc_m4_in"]
    scr = T["scr_t"]

    with ExitStack() as ctx:
        sbuf = lambda n, sh, dt: ctx.enter_context(nc.sbuf_tensor("p3" + n, sh, dt))[:, :]
        psum = lambda n, sh, dt: ctx.enter_context(nc.psum_tensor("p3" + n, sh, dt))[:, :]
        sem = lambda n: octx.enter_context(nc.semaphore("p3" + n))
        upc = sbuf("upc", [128, 2 * R], BF16)
        gac = sbuf("gac", [128, 2 * R], BF16)
        i32u = sbuf("i32u", [128, 2 * R], I32)
        dmu = sbuf("dmu", [128, 2 * R], F32)
        i32g = sbuf("i32g", [128, 2 * R], I32)
        dmg = sbuf("dmg", [128, 2 * R], F32)
        gq = sbuf("gq", [128, 2 * R], F32)
        sg = sbuf("sg", [128, 2 * R], F32)
        outU = sbuf("outU", [128, 2 * R], F32)
        outT = sbuf("outT", [128, 2 * R], F32)
        m2row = sbuf("m2row", [1, 8], F32)
        m3row = sbuf("m3row", [1, 8], F32)
        msc = sbuf("msc", [1, 8], F32)
        cols = sbuf("cols", [128, 8], F32)
        runm = sbuf("runm", [128, 1], F32)
        mx1 = sbuf("mx1", [1, 128], F32)
        idf = G["idf"]
        ps_mx = psum("ps_mx", [1, 128], F32)
        s_ld = sem("s_ld"); s_cp = sem("s_cp"); s_ev = sem("s_ev")
        s_st = sem("s_st"); s_aux = sem("s_aux"); s_mm = sem("s_mm")
        ld = Sem(s_ld); cp = Sem(s_cp); ev = Sem(s_ev); st = Sem(s_st)
        aux = Sem(s_aux); mm = Sem(s_mm)

        p.wait("sync", G["ccs"], at=3)
        p.dma("sync", m2row[0:1, 0:1], T["cc_m23_out"][0:1, 0:1], ld)
        p.dma("sync", m3row[0:1, 0:1], T["cc_m23_out"][0:1, 1:2], ld)
        p.emit("vector", lambda e: e.memset(cols[:, 4:5], 32768.0))
        p.emit("vector", lambda e: e.memset(runm, 0.0), inc=cp)
        p.wait("vector", ld)
        _emit_scales(p, scr, m2row[0:1, 0:1], msc, cols[:, 0:1], cols[:, 1:2], ld, cp, aux, 4)
        _emit_scales(p, scr, m3row[0:1, 0:1], msc, cols[:, 2:3], cols[:, 3:4], ld, cp, aux, 6)
        p.wait("scalar", ld)
        p.wait("scalar", cp)
        p.wait("vector", ld)
        cpS = Sem(sem("s_cs")); cpV = Sem(sem("s_cv")); cpG = Sem(sem("s_cg"))
        sl_ld = [Sem(sem("s_l0")), Sem(sem("s_l1"))]
        v_free = [0, 0]; g_free = [0, 0]; s_free = [0, 0]
        upst_done = [0, 0]
        tst_done = [0, 0]

        def gchain(ops):
            for i, fn in enumerate(ops):
                p.emit("vector", fn, inc=cpG)
                if i < len(ops) - 1:
                    p.drain("vector")

        for oc in range(24):
            sl = oc % 2
            c2 = lambda buf, sl=sl: buf[:, sl * R:(sl + 1) * R]
            U, Gc = c2(upc), c2(gac)
            Iu, Du = c2(i32u), c2(dmu)
            Ig, Dg = c2(i32g), c2(dmg)
            Q, S_, Ou, Ot = c2(gq), c2(sg), c2(outU), c2(outT)
            p.wait("sync", cpV, at=v_free[sl])
            p.wait("sync", cpG, at=g_free[sl])
            p.wait("sync", cpS, at=s_free[sl])
            p.dma("sync", U, up_raw_i[:, oc * R:(oc + 1) * R], sl_ld[sl])
            p.dma("sync", Gc, gate_raw_i[:, oc * R:(oc + 1) * R], sl_ld[sl])
            ldv = sl_ld[sl].n
            p.wait("scalar", sl_ld[sl], at=ldv)
            p.emit("scalar", lambda e, U=U, Iu=Iu: e.activation(
                out=Iu, in_=U, func=AF.Identity, bias=cols[:, 4:5], scale=cols[:, 1:2]), inc=cpS)
            cv_u = cpS.n
            p.emit("scalar", lambda e, Gc=Gc, Ig=Ig: e.activation(
                out=Ig, in_=Gc, func=AF.Identity, bias=cols[:, 4:5], scale=cols[:, 3:4]), inc=cpS)
            cv_g = cpS.n
            p.wait("vector", cpS, at=cv_u)
            p.wait("vector", st, at=upst_done[sl])
            gchain([
                lambda e, U=U, Iu=Iu, Du=Du: e.scalar_tensor_tensor(
                    out=Du, in0=U, scalar=cols[:, 1:2], in1=Iu, op0=ALU.mult, op1=ALU.subtract),
                lambda e, Iu=Iu, Du=Du: e.scalar_tensor_tensor(
                    out=Du, in0=Du, scalar=-32767.5, in1=Iu, op0=ALU.is_ge, op1=ALU.add),
                lambda e, Du=Du, Ou=Ou: e.tensor_scalar(
                    out=Ou, in0=Du, scalar1=32768.0, scalar2=cols[:, 0:1],
                    op0=ALU.subtract, op1=ALU.mult),
            ])
            upq_done = cpG.n
            p.wait("vector", cpS, at=cv_g)
            p.emit("vector", lambda e, Gc=Gc, Ig=Ig, Dg=Dg: e.scalar_tensor_tensor(
                out=Dg, in0=Gc, scalar=cols[:, 3:4], in1=Ig, op0=ALU.mult, op1=ALU.subtract))
            p.drain("vector")
            p.emit("vector", lambda e, Ig=Ig, Dg=Dg: e.scalar_tensor_tensor(
                out=Dg, in0=Dg, scalar=-32767.5, in1=Ig, op0=ALU.is_ge, op1=ALU.add))
            p.drain("vector")
            p.emit("vector", lambda e, Dg=Dg, Q=Q: e.tensor_scalar(
                out=Q, in0=Dg, scalar1=32768.0, scalar2=cols[:, 2:3],
                op0=ALU.subtract, op1=ALU.mult), inc=cpV)
            gq_done = cpV.n
            p.wait("scalar", cpV, at=gq_done)
            p.emit("scalar", lambda e, Q=Q, S_=S_: e.activation(out=S_, in_=Q, func=AF.Sigmoid), inc=cpS)
            p.emit("scalar", lambda e, S_=S_, Ig=Ig: e.activation(
                out=Ig, in_=S_, func=AF.Identity, scale=65536.0), inc=cpS)
            sgc = cpS.n
            p.wait("vector", cpS, at=sgc)
            gchain([
                lambda e, S_=S_, Ig=Ig, Du=Du: e.scalar_tensor_tensor(
                    out=Du, in0=S_, scalar=65536.0, in1=Ig, op0=ALU.mult, op1=ALU.subtract),
                lambda e, Ig=Ig, Du=Du: e.scalar_tensor_tensor(
                    out=Du, in0=Du, scalar=0.5, in1=Ig, op0=ALU.is_ge, op1=ALU.add),
                lambda e, Du=Du, S_=S_: e.tensor_scalar(
                    out=S_, in0=Du, scalar1=65535.0, scalar2=1.0 / 65536.0,
                    op0=ALU.min, op1=ALU.mult),
            ])
            fq_done = cpG.n
            p.wait("vector", cpG, at=fq_done)
            p.wait("vector", st, at=tst_done[sl])
            p.emit("vector", lambda e, Q=Q, S_=S_, Ot=Ot: e.tensor_tensor(
                out=Ot, in0=Q, in1=S_, op=ALU.mult), inc=cpV)
            t_done = cpV.n
            p.drain("vector")
            p.emit("vector", lambda e, Ot=Ot, Dg=Dg: e.tensor_reduce(
                out=Dg[:, 0:1], in_=Ot, axis=AX.X, op=ALU.max, apply_absolute_value=True))
            p.drain("vector")
            p.emit("vector", lambda e, Dg=Dg: e.tensor_tensor(
                out=runm, in0=runm, in1=Dg[:, 0:1], op=ALU.max), inc=cpV)
            v_free[sl] = cpV.n
            g_free[sl] = cpG.n
            s_free[sl] = cpS.n
            p.wait("gpsimd", cpG, at=upq_done)
            p.dma("gpsimd", upq_o[:, oc * R:(oc + 1) * R], Ou, st)
            upst_done[sl] = st.n
            p.wait("gpsimd", cpV, at=t_done)
            p.dma("gpsimd", t_o[:, oc * R:(oc + 1) * R], Ot, st)
            tst_done[sl] = st.n
        p.wait("tensor", cpV)
        _emit_colmax_out(p, runm[:, 0:1], mx1, msc[0:1, 7:8], ps_mx, idf, maxp4_o, mm, ev, st)
        p.wait("gpsimd", st)
        p.cc("AllReduce", ALU.max, T["cc_m4_in"], T["cc_m4_out"], G["ccs"])
        return [ld, cp, ev, st, aux, mm, cpS, cpV, cpG] + sl_ld


def emit_phase4(nc, p, octx, T, G):
    t_i, upq_i = T["t_d"], T["upq_d"]
    oraw_o = T["oraw_d"]
    maxp5_o = T["cc_m5_in"]
    scr = T["scr_t"]

    with ExitStack() as ctx:
        sbuf = lambda n, sh, dt: ctx.enter_context(nc.sbuf_tensor("p4" + n, sh, dt))[:, :]
        psum = lambda n, sh, dt: ctx.enter_context(nc.psum_tensor("p4" + n, sh, dt))[:, :]
        sem = lambda n: octx.enter_context(nc.semaphore("p4" + n))
        tc = sbuf("tc", [128, 2 * R], BF16)
        uc = sbuf("uc", [128, 2 * R], BF16)
        i32b = sbuf("i32b", [128, 2 * R], I32)
        dmb = sbuf("dmb", [128, 2 * R], F32)
        g2 = sbuf("g2", [128, 2 * R], F32)
        outb = sbuf("outb", [128, 2 * R], F32)
        m4row = sbuf("m4row", [1, 8], F32)
        msc = sbuf("msc", [1, 8], F32)
        cols = sbuf("cols", [128, 4], F32)
        runm = sbuf("runm", [128, 1], F32)
        mx1 = sbuf("mx1", [1, 128], F32)
        idf = G["idf"]
        ps_mx = psum("ps_mx", [1, 128], F32)
        s_ld = sem("s_ld"); s_cp = sem("s_cp"); s_ev = sem("s_ev")
        s_st = sem("s_st"); s_aux = sem("s_aux"); s_mm = sem("s_mm")
        ld = Sem(s_ld); cp = Sem(s_cp); ev = Sem(s_ev); st = Sem(s_st)
        aux = Sem(s_aux); mm = Sem(s_mm)

        p.wait("sync", G["ccs"], at=4)
        p.dma("sync", m4row[0:1, 0:1], T["cc_m4_out"], ld)
        p.emit("vector", lambda e: e.memset(cols[:, 2:3], 32768.0))
        p.emit("vector", lambda e: e.memset(runm, 0.0), inc=cp)
        p.wait("vector", ld)
        _emit_scales(p, scr, m4row[0:1, 0:1], msc, cols[:, 0:1], cols[:, 1:2], ld, cp, aux, 8)
        p.wait("scalar", ld)
        p.wait("scalar", cp)
        cpS = Sem(sem("s_cs")); cpV = Sem(sem("s_cv")); cpG = Sem(sem("s_cg"))
        sl_ld = [Sem(sem("s_l0")), Sem(sem("s_l1"))]
        v_free = [0, 0]; g_free = [0, 0]; s_free = [0, 0]
        ost_done = [0, 0]
        for oc in range(24):
            sl = oc % 2
            c2 = lambda buf, sl=sl: buf[:, sl * R:(sl + 1) * R]
            Tc, Uc, I, Dm, G2, Ob = c2(tc), c2(uc), c2(i32b), c2(dmb), c2(g2), c2(outb)
            p.wait("sync", cpV, at=v_free[sl])
            p.wait("sync", cpG, at=g_free[sl])
            p.wait("sync", cpS, at=s_free[sl])
            p.dma("sync", Tc, t_i[:, oc * R:(oc + 1) * R], sl_ld[sl])
            p.dma("sync", Uc, upq_i[:, oc * R:(oc + 1) * R], sl_ld[sl])
            ldv = sl_ld[sl].n
            p.wait("scalar", sl_ld[sl], at=ldv)
            p.emit("scalar", lambda e, Tc=Tc, I=I: e.activation(
                out=I, in_=Tc, func=AF.Identity, bias=cols[:, 2:3], scale=cols[:, 1:2]), inc=cpS)
            cv = cpS.n
            p.wait("vector", cpS, at=cv)
            p.emit("vector", lambda e, Tc=Tc, I=I, Dm=Dm: e.scalar_tensor_tensor(
                out=Dm, in0=Tc, scalar=cols[:, 1:2], in1=I, op0=ALU.mult, op1=ALU.subtract))
            p.drain("vector")
            p.emit("vector", lambda e, I=I, Dm=Dm: e.scalar_tensor_tensor(
                out=Dm, in0=Dm, scalar=-32767.5, in1=I, op0=ALU.is_ge, op1=ALU.add))
            p.drain("vector")
            p.emit("vector", lambda e, Dm=Dm, G2=G2: e.tensor_scalar(
                out=G2, in0=Dm, scalar1=32768.0, scalar2=cols[:, 0:1],
                op0=ALU.subtract, op1=ALU.mult), inc=cpV)
            q_done = cpV.n
            p.wait("vector", st, at=ost_done[sl])
            p.emit("vector", lambda e, G2=G2, Uc=Uc, Ob=Ob: e.tensor_tensor(
                out=Ob, in0=G2, in1=Uc, op=ALU.mult), inc=cpG)
            o_done = cpG.n
            p.wait("gpsimd", cpG, at=o_done)
            p.dma("gpsimd", oraw_o[:, oc * R:(oc + 1) * R], Ob, st)
            ost_done[sl] = st.n
            p.wait("vector", cpG, at=o_done)
            p.emit("vector", lambda e, Ob=Ob, Dm=Dm: e.tensor_reduce(
                out=Dm[:, 0:1], in_=Ob, axis=AX.X, op=ALU.max, apply_absolute_value=True))
            p.drain("vector")
            p.emit("vector", lambda e, Dm=Dm: e.tensor_tensor(
                out=runm, in0=runm, in1=Dm[:, 0:1], op=ALU.max), inc=cpV)
            v_free[sl] = cpV.n
            g_free[sl] = cpG.n
            s_free[sl] = cpS.n
        p.wait("tensor", cpV)
        _emit_colmax_out(p, runm[:, 0:1], mx1, msc[0:1, 7:8], ps_mx, idf, maxp5_o, mm, ev, st)
        p.wait("gpsimd", st)
        p.cc("AllReduce", ALU.max, T["cc_m5_in"], T["cc_m5_out"], G["ccs"])
        return [ld, cp, ev, st, aux, mm, cpS, cpV, cpG] + sl_ld


def emit_phase5(nc, p, octx, T, G):
    oraw_i = T["oraw_d"]
    xa_i = T["xa_d"]
    wdown, ipw_t, ipb_row = T["wdown"], T["ipw_t"], T["ipb_row"]
    opw_t, opb_row, rt_t, rb_row = T["opw_t"], T["opb_row"], T["rt_t"], T["rb_row"]
    gat, upt, dat = T["gat"], T["upt"], T["dat"]
    sgt, sut, sdt = T["sgt"], T["sut"], T["sdt"]
    out_o = T["out_o"]
    scr = T["scr_t"]
    wt_dram = T["wt_dram"]
    debug = 0

    with ExitStack() as ctx:
        sbuf = lambda n, sh, dt: ctx.enter_context(nc.sbuf_tensor("p5" + n, sh, dt))[:, :]
        psum = lambda n, sh, dt: ctx.enter_context(nc.psum_tensor("p5" + n, sh, dt))[:, :]
        sem = lambda n: octx.enter_context(nc.semaphore("p5" + n))
        x1 = sbuf("x1", [128, NT * D], F32)
        xn1_fm = sbuf("xn1_fm", [128, KC * R], BF16)
        big1 = sbuf("big1", [128, 12288], F32)
        big2 = sbuf("big2", [128, 8192], F32)
        big3 = sbuf("big3", [128, 4096], F32)
        q512b = sbuf("q512b", [128, 512], F32)
        q512c = sbuf("q512c", [128, 512], F32)
        ev512 = sbuf("ev512", [128, 512], F32)
        bias5 = q512b
        wpool = sbuf("wpool", [128, 4096], BF16)
        dlhs = wpool
        gulhs = wpool[:, 0:2048]
        oplhs = wpool[:, 0:1024]
        rtb = wpool[:, 0:256]
        sglhs = wpool[:, 0:1024]
        sulhs = wpool[:, 1024:2048]
        sdlhs = wpool[:, 0:512]
        wbc = big1[:, 0:512]
        wtstg = big3[0:32, 1024:1152]
        wall = sbuf("wall", [128, 32], F32)
        opb = sbuf("opb", [128, 8], F32)
        rbb = sbuf("rbb", [128, 32], F32)
        bmx3 = sbuf("bmx3", [128, 96], F32)
        stat = sbuf("stat", [128, 8], F32)
        cols = sbuf("cols", [128, 4], F32)
        m5row = sbuf("m5row", [1, 8], F32)
        msc = sbuf("msc", [1, 8], F32)
        idb, idf = G["idb"], G["idf"]
        ps_a = psum("ps_a", [128, 512], F32)
        ps_b = psum("ps_b", [128, 512], F32)
        ps_c = psum("ps_c", [128, 512], F32)
        ps_d = psum("ps_d", [128, 512], F32)
        ps_t = psum("ps_t", [128, D], BF16)
        ps_tf = psum("ps_tf", [128, 512], F32)
        s_ld = sem("s_ld"); s_cp = sem("s_cp"); s_ev = sem("s_ev")
        s_st = sem("s_st"); s_aux = sem("s_aux"); s_mm = sem("s_mm")
        s_bar = sem("s_bar")
        s_vc = sem("s_vc")
        slot_sems = [sem(f"s_sl{i}") for i in range(6)]
        ld = Sem(s_ld); cp = Sem(s_cp); ev = Sem(s_ev); st = Sem(s_st)
        aux = Sem(s_aux); mm = Sem(s_mm); cpvc = Sem(s_vc)
        sg = [Sem(slot_sems[0]), Sem(slot_sems[1])]
        sw = [Sem(slot_sems[2]), Sem(slot_sems[3])]
        sd = [Sem(slot_sems[4]), Sem(slot_sems[5])]
        bar = Sem(s_bar)
        allsems = [ld, cp, ev, st, aux, mm, cpvc, p.sems["lncp"]] + sg + sw + sd
        barrier = lambda: _phase_barrier(p, allsems, stat[0:1, 6:7], stat[0:1, 7:8], msc[0:1, 6:7], bar)

        p.wait("sync", G["ccs"], at=5)
        p.dma("sync", m5row[0:1, 0:1], T["cc_m5_out"], ld)
        p.emit("vector", lambda e: e.memset(cols[:, 2:3], 32768.0), inc=cp)
        p.wait("vector", ld)
        _emit_scales(p, scr, m5row[0:1, 0:1], msc, cols[:, 0:1], cols[:, 1:2], ld, cp, aux, 10)
        p.dma("gpsimd", opb, opb_row.rearrange("a (c p) -> p a c", p=128)[:, 0, :], ld,
              allow_slow_non_contiguous=True)
        p.dma("gpsimd", rbb, rb_row.broadcast_to([128, 32]), ld)
        for t in range(NT):
            p.dma("sync", x1[:, t * D:(t + 1) * D], xa_i[t * 128:(t + 1) * 128, :], ld)
        xa_ld = ld.n

        wdlhs = big1.bitcast(BF16)
        wtile = big3[:, 0:1024]
        yw = big3[:, 1024:2048]
        iw = big3[:, 2048:3072].bitcast(I32)[:, 0:1024]
        dw = big3[:, 3072:4096]
        wq_bf = ev512.bitcast(BF16)
        p.emit("vector", lambda e: e.memset(stat[:, 0:1], 16.0), inc=cp)
        p.wait("scalar", ld, xa_ld)
        for oc in range(8):
            for pc in range(3):
                p.wait("sync", cp)
                p.dma("sync", wtile, wdown[oc * 128:(oc + 1) * 128, pc * 1024:(pc + 1) * 1024], ld)
                p.wait("vector", ld)
                p.emit("vector", lambda e: e.tensor_reduce(
                    out=bmx3[:, 0:32], in_=wtile.rearrange("p (g x) -> p g x", x=32), axis=AX.X,
                    op=ALU.max, apply_absolute_value=True))
                p.drain("vector")
                p.emit("vector", lambda e: e.tensor_scalar(out=bmx3[:, 32:64], in0=bmx3[:, 0:32],
                                                           scalar1=1.0 / 7.0, scalar2=1e-12, op0=ALU.mult, op1=ALU.max))
                p.drain("vector")
                p.emit("vector", lambda e: e.reciprocal(out=bmx3[:, 64:96], in_=bmx3[:, 32:64]))
                p.drain("vector")
                p.emit("vector", lambda e: e.tensor_tensor(
                    out=yw.rearrange("p (g x) -> p g x", x=32), in0=wtile.rearrange("p (g x) -> p g x", x=32),
                    in1=bmx3[:, 64:96].rearrange("p (g o) -> p g o", o=1).broadcast_to([128, 32, 32]),
                    op=ALU.mult), inc=cp)
                p.wait("scalar", cp)
                p.emit("scalar", lambda e: e.activation(out=iw, in_=yw, func=AF.Identity,
                                                        bias=stat[:, 0:1], scale=1.0), inc=cp)
                p.wait("vector", cp)
                p.emit("vector", lambda e: e.scalar_tensor_tensor(out=dw, in0=yw, scalar=16.0,
                                                                  in1=iw, op0=ALU.add, op1=ALU.subtract))
                p.drain("vector")
                p.emit("vector", lambda e: e.scalar_tensor_tensor(out=dw, in0=dw, scalar=0.5,
                                                                  in1=iw, op0=ALU.is_ge, op1=ALU.add))
                p.drain("vector")
                p.emit("vector", lambda e: e.tensor_scalar(out=dw, in0=dw, scalar1=16.0, scalar2=None,
                                                           op0=ALU.subtract))
                p.drain("vector")
                p.emit("vector", lambda e: e.tensor_tensor(
                    out=wq_bf.rearrange("p (g x) -> p g x", x=32), in0=dw.rearrange("p (g x) -> p g x", x=32),
                    in1=bmx3[:, 32:64].rearrange("p (g o) -> p g o", o=1).broadcast_to([128, 32, 32]),
                    op=ALU.mult), inc=cp)
                p.wait("tensor", cp)
                p.wait("tensor", ev)
                for c in range(8):
                    p.emit("tensor", lambda e, c=c: e.transpose(
                        ps_t[:, c * 128:(c + 1) * 128], wq_bf[:, c * 128:(c + 1) * 128], idb),
                        inc=(mm if c == 7 else None))
                p.wait("scalar", mm)
                p.emit("scalar", lambda e, oc=oc, pc=pc: e.activation(
                    out=wdlhs.rearrange("p (o k j) -> p o k j", o=8, j=128)[:, oc, pc * 8:(pc + 1) * 8, :],
                    in_=ps_t.rearrange("p (k j) -> p k j", j=128), func=AF.Copy), inc=ev)
        o_slab = big2.bitcast(BF16)[:, 0:24 * 512]
        qstage = big3[:, 0:2048]
        qi32 = big3[:, 2048:4096].bitcast(I32)[:, 0:2048]
        p.wait("scalar", cp)
        p.wait("scalar", ev)
        for tq in range(4):
            for g in range(6):
                p.wait("gpsimd", cp)
                p.dma("gpsimd", qstage.rearrange("p (k j) -> p k j", j=512),
                      oraw_i.rearrange("p (k r) -> p k r", r=R)[:, 4 * g:4 * g + 4,
                                                               tq * 512:(tq + 1) * 512], ld)
                p.wait("scalar", ld)
                _emit_qdq(p, qstage, o_slab[:, g * 2048:(g + 1) * 2048],
                          qi32, qstage, cols[:, 0:1], cols[:, 1:2], cols[:, 2:3], cp)
            p.wait("tensor", cp)
            for oc in range(8):
                pp = ps_a if oc % 2 == 0 else ps_b
                for kc in range(24):
                    p.emit("tensor", lambda e, pp=pp, oc=oc, kc=kc: e.matmul(
                        pp, wdlhs.rearrange("p (o k j) -> p o k j", o=8, j=128)[:, oc, kc, :],
                        o_slab[:, kc * 512:(kc + 1) * 512],
                        start=(kc == 0), stop=(kc == 23)), inc=(mm if kc == 23 else None))
                p.wait("scalar", mm)
                p.wait("scalar", st)
                p.emit("scalar", lambda e, pp=pp: e.activation(out=ev512, in_=pp, func=AF.Copy), inc=ev)
                p.wait("tensor", ev)
                for tb in range(4):
                    p.emit("tensor", lambda e, tb=tb: e.transpose(
                        ps_tf[:, tb * 128:(tb + 1) * 128], ev512[:, tb * 128:(tb + 1) * 128], idf),
                        inc=(mm if tb == 3 else None))
                p.wait("vector", mm)
                p.wait("vector", ld, xa_ld)
                p.emit("vector", lambda e, tq=tq, oc=oc: e.tensor_tensor(
                    out=x1.rearrange("p (t j) -> p t j", j=D)[:, tq * 4:(tq + 1) * 4, oc * 128:(oc + 1) * 128],
                    in0=x1.rearrange("p (t j) -> p t j", j=D)[:, tq * 4:(tq + 1) * 4, oc * 128:(oc + 1) * 128],
                    in1=ps_tf.rearrange("p (t j) -> p t j", j=128), op=ALU.add), inc=ev)
                p.wait("tensor", ev)
            p.wait("scalar", ev)

        if debug & 1:
            p.wait("sync", ev)
            for t in range(NT):
                p.dma("sync", dbg_x1a[t * 128:(t + 1) * 128, :], x1[:, t * D:(t + 1) * D], st)
            p.wait("vector", st)
        barrier()
        sa = big3[:, 0:1024]
        sb = big3[:, 1024:2048]
        xb = big3[:, 2048:2560].bitcast(BF16)[:, 0:1024]
        stats = {"m": stat[:, 1:2], "var": stat[:, 2:3], "sd": stat[:, 3:4],
                 "rstd": stat[:, 4:5], "eps_col": stat[:, 5:6]}
        p.emit("vector", lambda e: e.memset(stat[:, 5:6], EPS))
        p.wait("vector", ev)
        for t in range(NT):
            xt = x1[:, t * D:(t + 1) * D]
            _ln_tile(p, xt, sb, sa, stats)
            p.emit("vector", lambda e: e.tensor_copy(out=xb, in_=sb), inc=p.sems["lncp"])
            cpv = p.sems["lncp"].n
            p.wait("tensor", p.sems["lncp"], cpv)
            p.wait("tensor", ev)
            for c in range(KC):
                p.emit("tensor", lambda e, c=c: e.transpose(
                    ps_t[:, c * 128:(c + 1) * 128], xb[:, c * 128:(c + 1) * 128], idb),
                    inc=(mm if c == KC - 1 else None))
            p.wait("scalar", mm)
            p.emit("scalar", lambda e, t=t: e.activation(
                out=xn1_fm.rearrange("p (c j) -> p c j", j=R)[:, :, t * 128:(t + 1) * 128],
                in_=ps_t.rearrange("p (c j) -> p c j", j=128), func=AF.Copy), inc=ev)
            p.wait("vector", ev)

        if debug & 2:
            p.wait("gpsimd", ev)
            p.dma("gpsimd", dbg_xn1, xn1_fm, st)
            p.wait("vector", st)
        barrier()
        qkv1 = big1.bitcast(BF16)
        wblks = [big3[:, 0:2048].bitcast(BF16), wpool[:, 0:4096]]
        bias5s = [q512b, q512c]
        prodv = big3[:, 0:2048]
        prodg = big3[:, 2048:3072]
        ctxsf = big3[:, 3072:4096]
        xbC = q512b.bitcast(BF16)[:, 0:1024]
        scoresb = q512c[:, 0:128]
        smx = q512c[:, 128:144]
        ssm = q512c[:, 144:160]
        src_ = q512c[:, 160:176]
        ctx_fm = big2.bitcast(BF16)[:, 0:16384]
        wb_used = [0, 0]; wb_ld = [0, 0]; b5_used = [0, 0]
        pp_free = [0, 0]
        p.wait("gpsimd", p.sems["lncp"])
        p.wait("gpsimd", ev)
        for pb in range(2):
            if pb == 1:
                p.wait("gpsimd", cp)
            for oc in range(6):
                sl = oc % 2
                p.wait("gpsimd", mm, at=wb_used[sl])
                p.dma("gpsimd", wblks[sl], ipw_t[oc * 128:(oc + 1) * 128, :], sg[sl])
                p.wait("gpsimd", ev, at=b5_used[sl])
                p.dma("gpsimd", bias5s[sl].rearrange("p (o j) -> p o j", o=1),
                      ipb_row[0:1, oc * 512:(oc + 1) * 512].rearrange(
                          "a (o j) -> a o j", o=1).broadcast_to([128, 1, 512]), sw[sl])
                p.wait("tensor", sg[sl])
                for si in range(8):
                    t_idx = si * 2 + pb
                    psl = si % 2
                    pp = ps_a if psl == 0 else ps_b
                    p.wait("tensor", ev, at=pp_free[psl])
                    for kc in range(KC):
                        p.emit("tensor", lambda e, pp=pp, kc=kc, t_idx=t_idx, sl=sl: e.matmul(
                            pp, xn1_fm.rearrange("p (c j) -> p c j", j=R)[:, kc, t_idx * 128:(t_idx + 1) * 128],
                            wblks[sl][:, kc * 512:(kc + 1) * 512],
                            start=(kc == 0), stop=(kc == KC - 1)), inc=(mm if kc == KC - 1 else None))
                    if si == 7:
                        wb_used[sl] = mm.n
                    p.wait("vector", mm, at=mm.n)
                    p.wait("vector", sw[sl])
                    p.emit("vector", lambda e, pp=pp, si=si, oc=oc, sl=sl: e.tensor_tensor(
                        out=qkv1.rearrange("p (s f) -> p s f", f=3072)[:, si, oc * 512:(oc + 1) * 512],
                        in0=pp, in1=bias5s[sl], op=ALU.add), inc=ev)
                    pp_free[psl] = ev.n
                    if si == 7:
                        b5_used[sl] = ev.n
            if debug & 4:
                p.wait("gpsimd", ev)
                if pb == 1:
                    p.dma("gpsimd", dbg_qkv1, qkv1, st)
                    p.wait("vector", st)
            qv = qkv1.rearrange("p (s th d) -> p s th d", th=3 * NH, d=64)
            p.wait("vector", ev)
            p.wait("gpsimd", ev)
            sc_rdy = 0
            for si in range(8):
                for tp in range(4):
                    p.emit("vector", lambda e, si=si, tp=tp, qv=qv: e.tensor_tensor(
                        out=prodv.rearrange("p (t2 h d) -> p t2 h d", t2=2, d=64),
                        in0=qv[:, si:si + 1, 0:NH, :].broadcast_to([128, 2, NH, 64]),
                        in1=qv[:, 2 * tp:2 * tp + 2, NH:2 * NH, :], op=ALU.mult))
                    p.emit("vector", lambda e, tp=tp: e.tensor_reduce(
                        out=scoresb.rearrange("p (h t) -> p h t", t=8)[:, :, 2 * tp:2 * tp + 2],
                        in_=prodv.rearrange("p (t2 h d) -> p h t2 d", t2=2, d=64), axis=AX.X, op=ALU.add))
                p.drain("vector")
                p.emit("vector", lambda e: e.tensor_reduce(
                    out=smx, in_=scoresb.rearrange("p (h t) -> p h t", t=8), axis=AX.X, op=ALU.max))
                p.drain("vector")
                p.emit("vector", lambda e: e.tensor_tensor(
                    out=scoresb.rearrange("p (h t) -> p h t", t=8), in0=scoresb.rearrange("p (h t) -> p h t", t=8),
                    in1=smx.rearrange("p (h o) -> p h o", o=1).broadcast_to([128, 16, 8]), op=ALU.subtract), inc=cp)
                p.wait("scalar", cp, at=cp.n)
                p.emit("scalar", lambda e: e.activation(out=scoresb, in_=scoresb, func=AF.Exp, scale=0.125), inc=cp)
                p.wait("vector", cp, at=cp.n)
                p.emit("vector", lambda e: e.tensor_reduce(
                    out=ssm, in_=scoresb.rearrange("p (h t) -> p h t", t=8), axis=AX.X, op=ALU.add))
                p.drain("vector")
                p.emit("vector", lambda e: e.reciprocal(out=src_, in_=ssm))
                p.drain("vector")
                p.emit("vector", lambda e: e.tensor_tensor(
                    out=scoresb.rearrange("p (h t) -> p h t", t=8), in0=scoresb.rearrange("p (h t) -> p h t", t=8),
                    in1=src_.rearrange("p (h o) -> p h o", o=1).broadcast_to([128, 16, 8]), op=ALU.mult), inc=cp)
                sc_rdy = cp.n
                ctx2 = prodv[:, 0:1024]
                prodv2 = prodv[:, 1024:2048]
                p.wait("gpsimd", cp, at=sc_rdy)
                for t in range(4):
                    p.emit("gpsimd", lambda e, t=t, qv=qv: e.tensor_tensor(
                        out=prodg[:, 0:1024].rearrange("p (h d) -> p h d", d=64),
                        in0=qv[:, t, 2 * NH:3 * NH, :],
                        in1=scoresb.rearrange("p (h t) -> p h t", t=8)[:, :, t:t + 1].broadcast_to([128, 16, 64]),
                        op=ALU.mult))
                    if t == 0:
                        p.emit("gpsimd", lambda e: e.tensor_copy(out=ctxsf, in_=prodg[:, 0:1024]))
                    elif t < 3:
                        p.emit("gpsimd", lambda e: e.tensor_tensor(out=ctxsf, in0=ctxsf, in1=prodg[:, 0:1024], op=ALU.add))
                    else:
                        p.emit("gpsimd", lambda e: e.tensor_tensor(out=ctxsf, in0=ctxsf, in1=prodg[:, 0:1024], op=ALU.add), inc=cp)
                g_done = cp.n
                for t in range(4, 8):
                    p.emit("vector", lambda e, t=t, qv=qv, prodv2=prodv2: e.tensor_tensor(
                        out=prodv2.rearrange("p (h d) -> p h d", d=64),
                        in0=qv[:, t, 2 * NH:3 * NH, :],
                        in1=scoresb.rearrange("p (h t) -> p h t", t=8)[:, :, t:t + 1].broadcast_to([128, 16, 64]),
                        op=ALU.mult))
                    p.drain("vector")
                    if t == 4:
                        p.emit("vector", lambda e, ctx2=ctx2, prodv2=prodv2: e.tensor_copy(out=ctx2, in_=prodv2))
                    else:
                        p.emit("vector", lambda e, ctx2=ctx2, prodv2=prodv2: e.tensor_tensor(
                            out=ctx2, in0=ctx2, in1=prodv2, op=ALU.add))
                    p.drain("vector")
                p.wait("vector", cp, at=g_done)
                p.emit("vector", lambda e, ctx2=ctx2: e.tensor_tensor(out=ctxsf, in0=ctxsf, in1=ctx2, op=ALU.add))
                p.drain("vector")
                p.emit("vector", lambda e: e.tensor_copy(out=xbC, in_=ctxsf), inc=cpvc)
                cast_done = cpvc.n
                p.wait("tensor", cpvc, at=cast_done)
                p.wait("tensor", ev)
                t_idx = si * 2 + pb
                for c in range(KC):
                    p.emit("tensor", lambda e, c=c: e.transpose(
                        ps_t[:, c * 128:(c + 1) * 128], xbC[:, c * 128:(c + 1) * 128], idb),
                        inc=(mm if c == KC - 1 else None))
                p.wait("scalar", mm, at=mm.n)
                p.emit("scalar", lambda e, t_idx=t_idx: e.activation(
                    out=ctx_fm.rearrange("p (c j) -> p c j", j=R)[:, :, t_idx * 128:(t_idx + 1) * 128],
                    in_=ps_t.rearrange("p (c j) -> p c j", j=128), func=AF.Copy), inc=ev)
                p.wait("gpsimd", cpvc, at=cast_done)
                p.wait("gpsimd", ev)
                p.wait("vector", ev)
                p.wait("vector", cp)

        if debug & 8:
            p.wait("gpsimd", ev)
            p.dma("gpsimd", dbg_ctx, ctx_fm, st)
            p.wait("vector", st)
        barrier()
        for oc in range(8):
            p.wait("gpsimd", mm)
            p.dma("gpsimd", oplhs, opw_t[oc * 128:(oc + 1) * 128, :], ld)
            p.wait("tensor", ld)
            for tq in range(4):
                pp = ps_a if tq % 2 == 0 else ps_b
                p.wait("tensor", ev)
                for kc in range(KC):
                    p.emit("tensor", lambda e, pp=pp, kc=kc, oc=oc, tq=tq: e.matmul(
                        pp, oplhs[:, kc * 128:(kc + 1) * 128],
                        ctx_fm[:, kc * R + tq * 512: kc * R + (tq + 1) * 512],
                        start=(kc == 0), stop=(kc == KC - 1)), inc=(mm if kc == KC - 1 else None))
                p.wait("scalar", mm)
                p.emit("scalar", lambda e, pp=pp, oc=oc: e.activation(
                    out=ev512, in_=pp, func=AF.Identity, bias=opb[:, oc:oc + 1], scale=1.0), inc=ev)
                p.wait("tensor", ev)
                for tb in range(4):
                    p.emit("tensor", lambda e, tb=tb: e.transpose(
                        ps_tf[:, tb * 128:(tb + 1) * 128], ev512[:, tb * 128:(tb + 1) * 128], idf),
                        inc=(mm if tb == 3 else None))
                p.wait("vector", mm)
                p.emit("vector", lambda e, tq=tq, oc=oc: e.tensor_tensor(
                    out=x1.rearrange("p (t j) -> p t j", j=D)[:, tq * 4:(tq + 1) * 4, oc * 128:(oc + 1) * 128],
                    in0=x1.rearrange("p (t j) -> p t j", j=D)[:, tq * 4:(tq + 1) * 4, oc * 128:(oc + 1) * 128],
                    in1=ps_tf.rearrange("p (t j) -> p t j", j=128), op=ALU.add), inc=ev)
                p.wait("tensor", ev)

        if debug & 16:
            p.wait("sync", ev)
            for t in range(NT):
                p.dma("sync", dbg_x1d1[t * 128:(t + 1) * 128, :], x1[:, t * D:(t + 1) * D], st)
            p.wait("vector", st)
        barrier()
        prod = prodv
        p.wait("gpsimd", mm)
        p.dma("gpsimd", rtb, rt_t, ld)
        p.wait("tensor", ld)
        for t in range(NT):
            pp = ps_a if t % 2 == 0 else ps_b
            p.wait("tensor", ev)
            for kc in range(KC):
                p.emit("tensor", lambda e, pp=pp, kc=kc, t=t: e.matmul(
                    pp[:, 0:32],
                    xn1_fm.rearrange("p (c j) -> p c j", j=R)[:, kc, t * 128:(t + 1) * 128],
                    rtb[:, kc * 32:(kc + 1) * 32],
                    start=(kc == 0), stop=(kc == KC - 1)), inc=(mm if kc == KC - 1 else None))
            p.wait("vector", mm)
            wsl = wall[:, 0:32]
            p.emit("vector", lambda e, pp=pp, wsl=wsl: e.tensor_tensor(out=prod[:, 0:32], in0=pp[:, 0:32], in1=rbb, op=ALU.add), inc=ev)
            p.drain("vector")
            p.emit("vector", lambda e: e.reduce_max(out=stat[:, 6:7], in_=prod[:, 0:32], axis=AX.X))
            p.drain("vector")
            p.emit("vector", lambda e: e.tensor_scalar(out=prod[:, 0:32], in0=prod[:, 0:32],
                                                       scalar1=stat[:, 6:7], scalar2=None, op0=ALU.subtract), inc=cp)
            p.wait("scalar", cp)
            p.emit("scalar", lambda e: e.activation(out=prod[:, 0:32], in_=prod[:, 0:32], func=AF.Exp), inc=cp)
            p.wait("vector", cp)
            p.emit("vector", lambda e: e.reduce_sum(out=stat[:, 6:7], in_=prod[:, 0:32], axis=AX.X))
            p.drain("vector")
            p.emit("vector", lambda e: e.reciprocal(out=stat[:, 7:8], in_=stat[:, 6:7]))
            p.drain("vector")
            p.emit("vector", lambda e: e.tensor_scalar(out=prod[:, 0:32], in0=prod[:, 0:32],
                                                       scalar1=stat[:, 7:8], scalar2=None, op0=ALU.mult))
            p.drain("vector")
            p.emit("vector", lambda e: e.tensor_copy(out=prod[:, 32:64], in_=prod[:, 0:32]))
            p.emit("vector", lambda e, wsl=wsl: e.memset(wsl, 0.0))
            p.drain("vector")
            for it in range(4):
                p.emit("vector", lambda e: e.reduce_max(out=stat[:, 6:7], in_=prod[:, 32:64], axis=AX.X))
                p.drain("vector")
                p.emit("vector", lambda e: e.tensor_scalar(out=prod[:, 64:96], in0=prod[:, 32:64],
                                                           scalar1=stat[:, 6:7], scalar2=None, op0=ALU.is_ge))
                p.drain("vector")
                p.emit("vector", lambda e: e.tensor_tensor(out=prod[:, 96:128], in0=prod[:, 64:96], in1=prod[:, 0:32], op=ALU.mult))
                p.drain("vector")
                p.emit("vector", lambda e, wsl=wsl: e.tensor_tensor(out=wsl, in0=wsl, in1=prod[:, 96:128], op=ALU.add))
                p.emit("vector", lambda e: e.tensor_scalar(out=prod[:, 96:128], in0=prod[:, 64:96],
                                                           scalar1=-1.0, scalar2=1.0, op0=ALU.mult, op1=ALU.add))
                p.drain("vector")
                p.emit("vector", lambda e: e.tensor_tensor(out=prod[:, 32:64], in0=prod[:, 32:64], in1=prod[:, 96:128], op=ALU.mult))
                p.drain("vector")
                p.emit("vector", lambda e: e.tensor_tensor(out=prod[:, 32:64], in0=prod[:, 32:64], in1=prod[:, 64:96], op=ALU.subtract))
                p.drain("vector")
            p.emit("vector", lambda e, wsl=wsl: e.reduce_sum(out=stat[:, 6:7], in_=wsl, axis=AX.X))
            p.drain("vector")
            p.emit("vector", lambda e: e.tensor_scalar(out=stat[:, 6:7], in0=stat[:, 6:7],
                                                       scalar1=1e-12, scalar2=None, op0=ALU.max))
            p.drain("vector")
            p.emit("vector", lambda e: e.reciprocal(out=stat[:, 7:8], in_=stat[:, 6:7]))
            p.drain("vector")
            p.emit("vector", lambda e, wsl=wsl: e.tensor_scalar(out=wsl, in0=wsl,
                                                                scalar1=stat[:, 7:8], scalar2=None, op0=ALU.mult), inc=cp)
            p.wait("tensor", cp)
            p.wait("tensor", aux)
            p.emit("tensor", lambda e, wsl=wsl: e.transpose(ps_tf[0:32, 0:128], wsl, idf), inc=mm)
            p.wait("vector", mm)
            p.wait("vector", st)
            p.emit("vector", lambda e: e.tensor_copy(out=wtstg, in_=ps_tf[0:32, 0:128]), inc=aux)
            p.wait("sync", aux)
            p.dma("sync", wt_dram.ap()[0:32, t * 128:(t + 1) * 128], wtstg, st)
        p.wait("vector", st)

        barrier()
        h_fm = big2.bitcast(BF16)[:, 0:32 * 512]
        sils = [big3[:, 0:512], big3[:, 512:1024]]
        h1s = [big3[:, 1024:1536], big3[:, 1536:2048]]
        gus = [wpool[:, 0:2048], wpool[:, 2048:4096]]
        wbcs = [big1[:, 0:512], big1[:, 512:1024]]
        dls = [big1[:, 1024:3072].bitcast(BF16), big1[:, 3072:5120].bitcast(BF16)]
        gu_used = [0, 0]; wbc_used = [0, 0]; gu_ld = [0, 0]; wbc_ld = [0, 0]
        pe_free = [0, 0]
        dl_used = [0, 0]; dl_ld = [0, 0]; dn_free = [0, 0]
        if debug & 32:
            p.wait("sync", st)
            p.dma("sync", dbg_wt, wt_dram.ap(), st)
        p.wait("gpsimd", st)
        for tq in range(4):
            p.wait("vector", mm)
            p.wait("gpsimd", mm)
            p.wait("tensor", ev)
            for ei in range(32):
                sl = ei % 2
                pg = ps_a if sl == 0 else ps_c
                pu = ps_b if sl == 0 else ps_d
                p.wait("gpsimd", mm, at=gu_used[sl])
                p.dma("gpsimd", gus[sl][:, 0:1024], gat[ei * 128:(ei + 1) * 128, :], sg[sl])
                p.dma("gpsimd", gus[sl][:, 1024:2048], upt[ei * 128:(ei + 1) * 128, :], sg[sl])
                p.wait("gpsimd", cp, at=wbc_used[sl])
                p.dma("gpsimd", wbcs[sl].rearrange("p (o j) -> p o j", o=1),
                      wt_dram.ap()[ei:ei + 1, tq * 512:(tq + 1) * 512].rearrange(
                          "a (o j) -> a o j", o=1).broadcast_to([128, 1, 512]), sw[sl])
                p.wait("tensor", sg[sl])
                p.wait("tensor", cp, at=pe_free[sl])
                for kc in range(KC):
                    p.emit("tensor", lambda e, pg=pg, kc=kc, tq=tq, sl=sl: e.matmul(
                        pg, gus[sl][:, kc * 128:(kc + 1) * 128],
                        xn1_fm[:, kc * R + tq * 512: kc * R + (tq + 1) * 512],
                        start=(kc == 0), stop=(kc == KC - 1)))
                    p.emit("tensor", lambda e, pu=pu, kc=kc, tq=tq, sl=sl: e.matmul(
                        pu, gus[sl][:, 1024 + kc * 128: 1024 + (kc + 1) * 128],
                        xn1_fm[:, kc * R + tq * 512: kc * R + (tq + 1) * 512],
                        start=(kc == 0), stop=(kc == KC - 1)), inc=(mm if kc == KC - 1 else None))
                gu_used[sl] = mm.n
                p.wait("scalar", mm, at=mm.n)
                p.emit("scalar", lambda e, pg=pg, sl=sl: e.activation(out=sils[sl], in_=pg, func=AF.Silu), inc=cp)
                p.wait("vector", cp, at=cp.n)
                p.wait("vector", sw[sl])
                p.emit("vector", lambda e, pu=pu, sl=sl: e.tensor_tensor(out=h1s[sl], in0=pu, in1=sils[sl], op=ALU.mult))
                p.emit("vector", lambda e, ei=ei, sl=sl: e.tensor_tensor(
                    out=h_fm[:, ei * 512:(ei + 1) * 512], in0=h1s[sl], in1=wbcs[sl], op=ALU.mult), inc=cp)
                wbc_used[sl] = cp.n
                pe_free[sl] = cp.n
            p.wait("tensor", cp)
            if (debug & 64) and tq == 0:
                p.wait("gpsimd", cp)
                p.dma("gpsimd", dbg_hfm, h_fm, st)
                p.wait("vector", st)
            for oc in range(8):
                sl = oc % 2
                pp = ps_a if sl == 0 else ps_b
                p.wait("gpsimd", mm, at=dl_used[sl])
                p.dma("gpsimd", dls[sl], dat[oc * 128:(oc + 1) * 128, :], sd[sl])
                p.wait("tensor", sd[sl])
                p.wait("tensor", ev, at=dn_free[sl])
                for kc in range(32):
                    p.emit("tensor", lambda e, pp=pp, kc=kc, sl=sl: e.matmul(
                        pp, dls[sl][:, kc * 128:(kc + 1) * 128], h_fm[:, kc * 512:(kc + 1) * 512],
                        start=(kc == 0), stop=(kc == 31)), inc=(mm if kc == 31 else None))
                dl_used[sl] = mm.n
                p.wait("scalar", mm, at=mm.n)
                p.wait("scalar", ev)
                p.emit("scalar", lambda e, pp=pp: e.activation(out=ev512, in_=pp, func=AF.Copy), inc=ev)
                p.wait("tensor", ev, at=ev.n)
                for tb in range(4):
                    p.emit("tensor", lambda e, tb=tb: e.transpose(
                        ps_tf[:, tb * 128:(tb + 1) * 128], ev512[:, tb * 128:(tb + 1) * 128], idf),
                        inc=(mm if tb == 3 else None))
                p.wait("vector", mm, at=mm.n)
                p.emit("vector", lambda e, tq=tq, oc=oc: e.tensor_tensor(
                    out=x1.rearrange("p (t j) -> p t j", j=D)[:, tq * 4:(tq + 1) * 4, oc * 128:(oc + 1) * 128],
                    in0=x1.rearrange("p (t j) -> p t j", j=D)[:, tq * 4:(tq + 1) * 4, oc * 128:(oc + 1) * 128],
                    in1=ps_tf.rearrange("p (t j) -> p t j", j=128), op=ALU.add), inc=ev)
                dn_free[sl] = ev.n
                p.wait("tensor", ev)

        if debug & 128:
            p.wait("sync", ev)
            for t in range(NT):
                p.dma("sync", dbg_x1d2[t * 128:(t + 1) * 128, :], x1[:, t * D:(t + 1) * D], st)
            p.wait("vector", st)
        barrier()
        s_h = big3.bitcast(BF16)[:, 0:4 * R]
        for oc in range(4):
            p.wait("gpsimd", mm)
            p.dma("gpsimd", sglhs, sgt[oc * 128:(oc + 1) * 128, :], ld)
            p.dma("gpsimd", sulhs, sut[oc * 128:(oc + 1) * 128, :], ld)
            p.wait("tensor", ld)
            for tq in range(4):
                pg = ps_a if tq % 2 == 0 else ps_c
                pu = ps_b if tq % 2 == 0 else ps_d
                p.wait("tensor", cp)
                for kc in range(KC):
                    p.emit("tensor", lambda e, pg=pg, kc=kc, tq=tq: e.matmul(
                        pg, sglhs[:, kc * 128:(kc + 1) * 128],
                        xn1_fm[:, kc * R + tq * 512: kc * R + (tq + 1) * 512],
                        start=(kc == 0), stop=(kc == KC - 1)))
                    p.emit("tensor", lambda e, pu=pu, kc=kc, tq=tq: e.matmul(
                        pu, sulhs[:, kc * 128:(kc + 1) * 128],
                        xn1_fm[:, kc * R + tq * 512: kc * R + (tq + 1) * 512],
                        start=(kc == 0), stop=(kc == KC - 1)), inc=(mm if kc == KC - 1 else None))
                p.wait("scalar", mm)
                p.emit("scalar", lambda e, pg=pg: e.activation(out=q512b, in_=pg, func=AF.Silu), inc=cp)
                p.wait("vector", cp)
                p.emit("vector", lambda e, pu=pu, oc=oc, tq=tq: e.tensor_tensor(
                    out=s_h[:, oc * R + tq * 512: oc * R + (tq + 1) * 512],
                    in0=pu, in1=q512b, op=ALU.mult), inc=cp)
        p.wait("tensor", cp)
        for oc in range(8):
            p.wait("gpsimd", mm)
            p.dma("gpsimd", sdlhs, sdt[oc * 128:(oc + 1) * 128, :], ld)
            p.wait("tensor", ld)
            for tq in range(4):
                pp = ps_a if tq % 2 == 0 else ps_b
                p.wait("tensor", ev)
                for kc in range(4):
                    p.emit("tensor", lambda e, pp=pp, kc=kc, tq=tq: e.matmul(
                        pp, sdlhs[:, kc * 128:(kc + 1) * 128],
                        s_h[:, kc * R + tq * 512: kc * R + (tq + 1) * 512],
                        start=(kc == 0), stop=(kc == 3)), inc=(mm if kc == 3 else None))
                p.wait("scalar", mm)
                p.emit("scalar", lambda e, pp=pp: e.activation(out=ev512, in_=pp, func=AF.Copy), inc=ev)
                p.wait("tensor", ev)
                for tb in range(4):
                    p.emit("tensor", lambda e, tb=tb: e.transpose(
                        ps_tf[:, tb * 128:(tb + 1) * 128], ev512[:, tb * 128:(tb + 1) * 128], idf),
                        inc=(mm if tb == 3 else None))
                p.wait("vector", mm)
                p.emit("vector", lambda e, tq=tq, oc=oc: e.tensor_tensor(
                    out=x1.rearrange("p (t j) -> p t j", j=D)[:, tq * 4:(tq + 1) * 4, oc * 128:(oc + 1) * 128],
                    in0=x1.rearrange("p (t j) -> p t j", j=D)[:, tq * 4:(tq + 1) * 4, oc * 128:(oc + 1) * 128],
                    in1=ps_tf.rearrange("p (t j) -> p t j", j=128), op=ALU.add), inc=ev)
                p.wait("tensor", ev)

        barrier()
        p.wait("sync", ev)
        for t in range(NT):
            p.dma("sync", out_o[t * 128:(t + 1) * 128, :], x1[:, t * D:(t + 1) * D], st)
        p.wait("gpsimd", st)
        p.wait("sync", st)
        return [ld, cp, ev, st, aux, mm] + sg + sw + sd


# version 17 (final)
# speedup vs baseline: 1.0534x; 1.0021x over previous
from contextlib import ExitStack
import numpy as np
import ml_dtypes
import concourse.bass as bass
import concourse.mybir as mybir
from concourse.bass_utils import run_bass_kernel_spmd

F32 = mybir.dt.float32
BF16 = mybir.dt.bfloat16
I32 = mybir.dt.int32
ALU = mybir.AluOpType
AF = mybir.ActivationFunctionType
AX = mybir.AxisListType

NCORES = 8
R = 2048
NT = 16
D = 1024
KC = 8
NH = 16
HD = 64
EPS = 1e-5


class Sem:
    __slots__ = ("h", "n")

    def __init__(self, h):
        self.h = h
        self.n = 0


class Prog:
    ENGINES = ("sync", "scalar", "vector", "tensor", "gpsimd")

    def __init__(self, nc):
        self.nc = nc
        self.ops = {e: [] for e in self.ENGINES}

    def emit(self, ek, make, inc=None, inck=1):
        if inc is not None:
            def fn(eng, make=make, s=inc, k=inck):
                make(eng).then_inc(s.h, k)
            inc.n += inck
        else:
            fn = make
        self.ops[ek].append(fn)

    def dma(self, ek, out, in_, inc, **kw):
        self.emit(ek, lambda eng, o=out, i=in_, kw=kw: eng.dma_start(out=o, in_=i, **kw),
                  inc=inc, inck=16)

    def cc(self, kind, op, in_ap, out_ap, sem):
        def fn(eng, k=kind, o=op, i=in_ap, u=out_ap, s=sem):
            eng.collective_compute(k, o, replica_groups=[list(range(NCORES))],
                                   ins=[i], outs=[u]).then_inc(s.h)
        sem.n += 1
        self.ops["gpsimd"].append(fn)

    def wait(self, ek, sem, at=None):
        v = sem.n if at is None else at
        self.ops[ek].append(lambda eng, s=sem, vv=v: eng.wait_ge(s.h, vv))

    def drain(self, ek):
        self.ops[ek].append(lambda eng: eng.drain())

    def build(self):
        with self.nc.Block() as block:
            for ek in self.ENGINES:
                def runner(eng, ek=ek):
                    for fn in self.ops[ek]:
                        fn(eng)
                getattr(block, ek)(runner)


def _ln_tile(p, xs, xn_out, sa, stats, extra_absmax=None):
    cp = p.sems["lncp"]
    v, s = "vector", "scalar"
    m, var, sd, rstd, eps_col = stats["m"], stats["var"], stats["sd"], stats["rstd"], stats["eps_col"]
    p.emit(v, lambda e, a=xs, o=m: e.reduce_sum(out=o, in_=a, axis=AX.X))
    p.drain(v)
    p.emit(v, lambda e, o=m, a=m: e.tensor_scalar(out=o, in0=a, scalar1=1.0 / D, scalar2=None, op0=ALU.mult))
    p.drain(v)
    p.emit(v, lambda e, o=sa, a=xs, mm=m: e.tensor_scalar(out=o, in0=a, scalar1=mm, scalar2=None, op0=ALU.subtract), inc=cp)
    p.wait(s, cp)
    p.emit(s, lambda e, o=xn_out, a=sa: e.activation(out=o, in_=a, func=AF.Square), inc=cp)
    p.wait(v, cp)
    p.emit(v, lambda e, o=var, a=xn_out: e.reduce_sum(out=o, in_=a, axis=AX.X))
    p.drain(v)
    p.emit(v, lambda e, o=var, a=var: e.tensor_scalar(out=o, in0=a, scalar1=1.0 / D, scalar2=None, op0=ALU.mult), inc=cp)
    p.wait(s, cp)
    p.emit(s, lambda e, o=sd, a=var, b=eps_col: e.activation(out=o, in_=a, func=AF.Sqrt, bias=b), inc=cp)
    p.wait(v, cp)
    p.emit(v, lambda e, o=rstd, a=sd: e.reciprocal(out=o, in_=a))
    p.drain(v)
    p.emit(v, lambda e, o=xn_out, a=sa, rr=rstd: e.tensor_scalar(out=o, in0=a, scalar1=rr, scalar2=None, op0=ALU.mult), inc=cp)
    if extra_absmax is not None:
        p.drain(v)
        p.emit(v, lambda e, o=extra_absmax, a=xn_out: e.tensor_reduce(
            out=o, in_=a, axis=AX.X, op=ALU.max, apply_absolute_value=True))
        p.drain(v)


def emit_phase1(nc, p, octx, T, G):
    xin, w1t = T["xin"], T["w1t"]
    xn_o = T["xn_d"]
    phiq_fm_o, phik_fm_o = T["phiq_fm_d"], T["phik_fm_d"]
    v_tm_o = T["v_tm_d"]
    ssum_o, sc0_o = T["cc_st_in"], T["sc0_d"]
    maxp1_o = T["cc_m1_in"]

    with ExitStack() as ctx:
        sbuf = lambda n, sh, dt: ctx.enter_context(nc.sbuf_tensor("p1" + n, sh, dt))[:, :]
        psum = lambda n, sh, dt: ctx.enter_context(nc.psum_tensor("p1" + n, sh, dt))[:, :]
        sem = lambda n: octx.enter_context(nc.semaphore("p1" + n))
        xw = sbuf("xw", [128, NT * D], F32)
        sa = sbuf("sa", [128, D], F32)
        sb = sbuf("sb", [128, D], F32)
        stat = sbuf("stat", [128, 8], F32)
        pmx = sbuf("pmx", [128, NT], F32)
        mxc = sbuf("mxc", [128, 1], F32)
        mx1 = sbuf("mx1", [1, 128], F32)
        xnn_bf = sbuf("xnn_bf", [128, D], BF16)
        xnn_fm = sbuf("xnn_fm", [128, KC * R], BF16)
        w1b = sbuf("w1b", [128, KC * 3 * D], BF16)
        idb, idf = G["idb"], G["idf"]
        phr = sbuf("phr", [128, D], F32)
        phe = sbuf("phe", [128, D], F32)
        phq_bf = sbuf("phq_bf", [128, D], BF16)
        phk_bf = sbuf("phk_bf", [128, D], BF16)
        v_bf = sbuf("v_bf", [128, NH * (HD + 1)], BF16)
        tstage = sbuf("tstage", [128, D], BF16)
        scst = sbuf("scst", [64, 2 * NH * 65], F32)
        ps_qkv = psum("ps_qkv", [128, D], F32)
        ps_t = psum("ps_t", [128, D], BF16)
        ps_scd0 = psum("ps_scd0", [64, 512], F32)
        ps_scd1 = psum("ps_scd1", [64, 512], F32)
        ps_scz = psum("ps_scz", [64, 16], F32)
        ps_mx = psum("ps_mx", [1, 128], F32)
        s_ld = sem("s_ld"); s_mm = sem("s_mm")
        s_ev = sem("s_ev"); s_st = sem("s_st"); s_aux = sem("s_aux")

        ld = Sem(s_ld); mm = Sem(s_mm); ev = Sem(s_ev); st = Sem(s_st); aux = Sem(s_aux)
        stats = {"m": stat[:, 0:1], "var": stat[:, 1:2], "sd": stat[:, 2:3],
                 "rstd": stat[:, 3:4], "eps_col": stat[:, 4:5]}

        p.emit("vector", lambda e: e.memset(stat[:, 4:5], EPS), inc=p.sems["lncp"])
        for t in range(NT):
            p.dma("sync", xw[:, t * D:(t + 1) * D], xin[t * 128:(t + 1) * 128, :], ld)
        for c in range(KC):
            p.dma("gpsimd", w1b[:, c * 3 * D:(c + 1) * 3 * D], w1t[c * 128:(c + 1) * 128, :], ld)
        ld_all = ld.n

        p.wait("vector", ld, ld_all)
        p.wait("scalar", ld, ld_all)
        for t in range(NT):
            xt = xw[:, t * D:(t + 1) * D]
            _ln_tile(p, xt, sb, sa, stats, extra_absmax=pmx[:, t:t + 1])
            p.emit("vector", lambda e, o=xt, a=sb: e.tensor_copy(out=o, in_=a), inc=p.sems["lncp"])
            _ln_tile(p, xt, sb, sa, stats)
            p.emit("vector", lambda e, o=xnn_bf, a=sb: e.tensor_copy(out=o, in_=a), inc=p.sems["lncp"])
            cpv = p.sems["lncp"].n
            p.wait("sync", p.sems["lncp"], cpv)
            p.dma("sync", xn_o[t * 128:(t + 1) * 128, :], xt, st)
            p.wait("tensor", p.sems["lncp"], cpv)
            if t > 0:
                p.wait("tensor", ev)
            for c in range(KC):
                p.emit("tensor", lambda e, c=c: e.transpose(
                    ps_t[:, c * 128:(c + 1) * 128], xnn_bf[:, c * 128:(c + 1) * 128], idb[:, :]),
                    inc=(mm if c == KC - 1 else None))
            p.wait("scalar", mm)
            p.emit("scalar", lambda e, t=t: e.activation(
                out=xnn_fm.rearrange("p (c j) -> p c j", j=R)[:, :, t * 128:(t + 1) * 128],
                in_=ps_t.rearrange("p (c j) -> p c j", j=128), func=AF.Copy), inc=ev)

        for t in range(NT):
            b, ch = t // 2, t % 2
            for part in range(3):
                p.wait("tensor", ev)
                for c in range(KC):
                    lhs = xnn_fm.rearrange("p (c j) -> p c j", j=R)[:, c, t * 128:(t + 1) * 128]
                    for jn in range(2):
                        rhs = w1b[:, c * 3 * D + part * D + jn * 512: c * 3 * D + part * D + (jn + 1) * 512]
                        p.emit("tensor", lambda e, l=lhs, r=rhs, jn=jn, c=c: e.matmul(
                            ps_qkv[:, jn * 512:(jn + 1) * 512], l, r,
                            start=(c == 0), stop=(c == KC - 1)),
                            inc=(mm if c == KC - 1 and jn == 1 else None))
                p.wait("scalar", mm)
                p.wait("vector", mm)
                if part < 2:
                    p.emit("scalar", lambda e: e.activation(out=phr[:, :], in_=ps_qkv[:, :], func=AF.Relu), inc=ev)
                    p.wait("vector", ev)
                    p.emit("vector", lambda e: e.tensor_tensor(out=phe[:, :], in0=ps_qkv[:, :], in1=phr[:, :], op=ALU.subtract), inc=ev)
                    p.wait("scalar", ev)
                    p.emit("scalar", lambda e: e.activation(out=phe[:, :], in_=phe[:, :], func=AF.Exp), inc=ev)
                    p.wait("vector", ev)
                    dst = phq_bf if part == 0 else phk_bf
                    p.emit("vector", lambda e, d=dst: e.tensor_tensor(out=d[:, :], in0=phe[:, :], in1=phr[:, :], op=ALU.add), inc=ev)
                else:
                    p.emit("vector", lambda e: e.memset(
                        v_bf.rearrange("p (h e) -> p h e", e=HD + 1)[:, :, HD:HD + 1], 1.0))
                    p.emit("vector", lambda e: e.tensor_copy(
                        out=v_bf.rearrange("p (h e) -> p h e", e=HD + 1)[:, :, 0:HD],
                        in_=ps_qkv.rearrange("p (h d) -> p h d", d=HD)), inc=ev)
                p.wait("tensor", ev)

            for src, dst_fm in ((phq_bf, phiq_fm_o), (phk_bf, phik_fm_o)):
                p.wait("tensor", ev)
                for c in range(KC):
                    p.emit("tensor", lambda e, s=src, c=c: e.transpose(
                        ps_t[:, c * 128:(c + 1) * 128], s[:, c * 128:(c + 1) * 128], idb[:, :]),
                        inc=(mm if c == KC - 1 else None))
                p.wait("scalar", mm)
                p.wait("scalar", st)
                p.emit("scalar", lambda e: e.activation(out=tstage[:, :], in_=ps_t[:, :], func=AF.Copy), inc=ev)
                p.wait("sync", ev)
                p.dma("sync", dst_fm.rearrange("p (b c j) -> p b c j", b=8, j=256)[
                          :, t // 2, :, (t % 2) * 128:(t % 2) * 128 + 128],
                      tstage.rearrange("p (c j) -> p c j", j=128), st)

            p.wait("tensor", ev)
            for hh in range(NH):
                bank = ps_scd0 if hh < 8 else ps_scd1
                col = (hh % 8) * 64
                lhs = phk_bf[:, hh * 64:(hh + 1) * 64]
                rhs = v_bf[:, hh * 65:hh * 65 + 64]
                p.emit("tensor", lambda e, bk=bank, co=col, l=lhs, r=rhs: e.matmul(
                    bk[:, co:co + 64], l, r, start=True, stop=True))
                rhs1 = v_bf[:, hh * 65 + 64:hh * 65 + 65]
                p.emit("tensor", lambda e, hh=hh, l=lhs, r=rhs1: e.matmul(
                    ps_scz[:, hh:hh + 1], l, r, start=True, stop=True),
                    inc=(mm if hh == NH - 1 else None))
            p.wait("vector", mm)
            scv = scst.rearrange("p (c x e) -> p c x e", c=2, e=65)
            p.emit("vector", lambda e, ch=ch, scv=scv: e.tensor_copy(
                out=scv[0:64, ch, 0:8, 0:64],
                in_=ps_scd0.rearrange("p (x d) -> p x d", d=64)))
            p.emit("vector", lambda e, ch=ch, scv=scv: e.tensor_copy(
                out=scv[0:64, ch, 8:16, 0:64],
                in_=ps_scd1.rearrange("p (x d) -> p x d", d=64)))
            p.emit("vector", lambda e, ch=ch, scv=scv: e.tensor_copy(
                out=scv[0:64, ch, 0:16, 64:65],
                in_=ps_scz.rearrange("p (x o) -> p x o", o=1)), inc=ev)
            p.wait("tensor", ev)
            if ch == 1:
                blk = NH * 65
                p.wait("sync", ev)
                p.dma("sync", sc0_o[0:64, b * blk:(b + 1) * blk], scst[0:64, 0:blk], st)
                p.wait("vector", st)
                p.emit("vector", lambda e, blk=blk: e.tensor_tensor(
                    out=scst[0:64, 0:blk], in0=scst[0:64, 0:blk],
                    in1=scst[0:64, blk:2 * blk], op=ALU.add), inc=ev)
                p.wait("sync", ev)
                p.dma("sync", ssum_o[0:64, b * blk:(b + 1) * blk], scst[0:64, 0:blk], st)

            p.wait("sync", ev)
            p.dma("sync", v_tm_o[t * 128:(t + 1) * 128, :], v_bf[:, :], st)
            p.wait("scalar", st)
            p.wait("vector", st)

        p.emit("vector", lambda e: e.reduce_max(out=mxc[:, :], in_=pmx[:, :], axis=AX.X), inc=ev)
        p.wait("tensor", ev)
        p.emit("tensor", lambda e: e.transpose(ps_mx[0:1, 0:128], mxc[:, 0:1], idf[:, :]), inc=mm)
        p.wait("vector", mm)
        p.emit("vector", lambda e: e.tensor_copy(out=mx1[:, :], in_=ps_mx[0:1, :]))
        p.drain("vector")
        p.emit("vector", lambda e: e.reduce_max(out=stat[0:1, 5:6], in_=mx1[0:1, :], axis=AX.X), inc=ev)
        p.wait("sync", ev)
        p.dma("sync", maxp1_o, stat[0:1, 5:6], st)

        p.wait("gpsimd", st)
        p.cc("AllReduce", ALU.max, T["cc_m1_in"], T["cc_m1_out"], G["ccs"])
        p.cc("AllGather", ALU.bypass, T["cc_st_in"], T["cc_st_out"], G["ccs"])
        return [ld, mm, ev, st, aux]


def _prep_host(inputs):
    h = {}
    x = np.ascontiguousarray(inputs["L0_attn_x"], dtype=np.float32)
    h["x_slices"] = [np.ascontiguousarray(x[:, c * 256:(c + 1) * 256, :].reshape(R, D))
                     for c in range(NCORES)]
    h["w1t"] = np.ascontiguousarray(inputs["L0_attn_W1"].T)
    h["identb"] = np.eye(128).astype(ml_dtypes.bfloat16)
    h["identf"] = np.eye(128, dtype=np.float32)
    h["triu"] = np.triu(np.ones((128, 128), np.float32))
    h["w2t"] = np.ascontiguousarray(inputs["L0_attn_W2"].T)
    h["wup"] = np.ascontiguousarray(inputs["L0_ffn_w_up"])
    h["wgate"] = np.ascontiguousarray(inputs["L0_ffn_w_gate"])
    h["wupt"] = np.ascontiguousarray(inputs["L0_ffn_w_up"].T)
    h["wgatet"] = np.ascontiguousarray(inputs["L0_ffn_w_gate"].T)
    h["wdown"] = np.ascontiguousarray(inputs["L0_ffn_w_down"])
    perm = np.zeros(3072, np.int64)
    for j in range(3):
        for hh in range(16):
            for d in range(64):
                perm[j * 1024 + hh * 64 + d] = hh * 192 + j * 64 + d
    def _swz(xT, K, OC, J):
        return np.ascontiguousarray(
            xT.reshape(K, 128, OC, J).transpose(2, 1, 0, 3).reshape(OC * 128, K * J))
    h["ipw_t"] = _swz(inputs["L1_attn_in_proj_weight"][perm, :].T, 8, 6, 512)
    h["ipb_row"] = np.ascontiguousarray(inputs["L1_attn_in_proj_bias"][perm][None, :])
    h["opw_t"] = _swz(inputs["L1_attn_out_proj_weight"].T, 8, 8, 128)
    h["opb_row"] = np.ascontiguousarray(inputs["L1_attn_out_proj_bias"][None, :])
    h["rt_t"] = _swz(inputs["L1_ffn_router_w"].T, 8, 1, 32)
    h["rb_row"] = np.ascontiguousarray(inputs["L1_ffn_router_bias"][None, :])
    h["gat"] = _swz(inputs["L1_ffn_gate_w"].reshape(4096, 1024).T, 8, 32, 128)
    h["upt"] = _swz(inputs["L1_ffn_up_w"].reshape(4096, 1024).T, 8, 32, 128)
    h["dat"] = _swz(inputs["L1_ffn_down_w"].transpose(0, 2, 1).reshape(4096, 1024), 32, 8, 128)
    h["sgt"] = _swz(inputs["L1_ffn_sgate_w"].T, 8, 4, 128)
    h["sut"] = _swz(inputs["L1_ffn_sup_w"].T, 8, 4, 128)
    h["sdt"] = _swz(inputs["L1_ffn_sdown_w"].T, 4, 8, 128)
    pm = np.zeros((NCORES, 64, 8), np.float32)
    for c in range(NCORES):
        pm[c, :, :c] = 1.0
    h["pmask"] = pm
    return h


def build_fused():
    nc = bass.Bass("TRN2", target_bir_lowering=False)
    ap = lambda n, sh, dt, kind: nc.dram_tensor(n, sh, dt, kind=kind).ap()
    T = {}
    T["xin"] = ap("xin", [R, D], F32, "ExternalInput")
    T["w1t"] = ap("w1t", [D, 3 * D], F32, "ExternalInput")
    T["w2t"] = ap("w2t", [D, D], F32, "ExternalInput")
    T["wup"] = ap("wup", [3 * D, D], F32, "ExternalInput")
    T["wgate"] = ap("wgate", [3 * D, D], F32, "ExternalInput")
    T["wdown"] = ap("wdown", [D, 3 * D], F32, "ExternalInput")
    T["ipw_t"] = ap("ipw_t", [768, 4096], F32, "ExternalInput")
    T["ipb_row"] = ap("ipb_row", [1, 3 * D], F32, "ExternalInput")
    T["opw_t"] = ap("opw_t", [1024, 1024], F32, "ExternalInput")
    T["opb_row"] = ap("opb_row", [1, D], F32, "ExternalInput")
    T["rt_t"] = ap("rt_t", [128, 256], F32, "ExternalInput")
    T["rb_row"] = ap("rb_row", [1, 32], F32, "ExternalInput")
    T["gat"] = ap("gat", [4096, 1024], F32, "ExternalInput")
    T["upt"] = ap("upt", [4096, 1024], F32, "ExternalInput")
    T["dat"] = ap("dat", [1024, 4096], F32, "ExternalInput")
    T["sgt"] = ap("sgt", [512, 1024], F32, "ExternalInput")
    T["sut"] = ap("sut", [512, 1024], F32, "ExternalInput")
    T["sdt"] = ap("sdt", [1024, 512], F32, "ExternalInput")
    T["identb"] = ap("identb", [128, 128], BF16, "ExternalInput")
    T["identf"] = ap("identf", [128, 128], F32, "ExternalInput")
    T["triu"] = ap("triu", [128, 128], F32, "ExternalInput")
    T["pmask_i"] = ap("pmask_i", [64, 8], F32, "ExternalInput")
    T["out_o"] = ap("out_o", [R, D], F32, "ExternalOutput")
    T["xn_d"] = ap("xn_d", [R, D], F32, "Internal")
    T["phiq_fm_d"] = ap("phiq_fm_d", [128, KC * R], BF16, "Internal")
    T["phik_fm_d"] = ap("phik_fm_d", [128, KC * R], BF16, "Internal")
    T["v_tm_d"] = ap("v_tm_d", [R, NH * (HD + 1)], BF16, "Internal")
    T["sc0_d"] = ap("sc0_d", [64, 128 * 65], F32, "Internal")
    T["xa_d"] = ap("xa_d", [R, D], F32, "Internal")
    T["up_raw_d"] = ap("up_raw_d", [128, 24 * R], BF16, "Internal")
    T["gate_raw_d"] = ap("gate_raw_d", [128, 24 * R], BF16, "Internal")
    T["upq_d"] = ap("upq_d", [128, 24 * R], BF16, "Internal")
    T["t_d"] = ap("t_d", [128, 24 * R], BF16, "Internal")
    T["oraw_d"] = ap("oraw_d", [128, 24 * R], BF16, "Internal")
    T["cc_st_in"] = ap("cc_st_in", [64, 128 * 65], F32, "Internal")
    T["cc_st_out"] = nc.dram_tensor("cc_st_out", [512, 128 * 65], F32,
                                    addr_space="Shared").ap()
    T["cc_m1_in"] = ap("cc_m1_in", [1, 1], F32, "Internal")
    T["cc_m1_out"] = nc.dram_tensor("cc_m1_out", [1, 1], F32, addr_space="Shared").ap()
    T["cc_m23_in"] = ap("cc_m23_in", [1, 2], F32, "Internal")
    T["cc_m23_out"] = nc.dram_tensor("cc_m23_out", [1, 2], F32, addr_space="Shared").ap()
    T["cc_m4_in"] = ap("cc_m4_in", [1, 1], F32, "Internal")
    T["cc_m4_out"] = nc.dram_tensor("cc_m4_out", [1, 1], F32, addr_space="Shared").ap()
    T["cc_m5_in"] = ap("cc_m5_in", [1, 1], F32, "Internal")
    T["cc_m5_out"] = nc.dram_tensor("cc_m5_out", [1, 1], F32, addr_space="Shared").ap()
    T["scr_t"] = nc.dram_tensor("scr", [1, 16], F32)
    T["wt_dram"] = nc.dram_tensor("wt_dram", [32, R], F32)

    with ExitStack() as octx:
        sbuf_o = lambda n, sh, dt: octx.enter_context(nc.sbuf_tensor(n, sh, dt))[:, :]
        G = {}
        G["idb"] = sbuf_o("g_idb", [128, 128], BF16)
        G["idf"] = sbuf_o("g_idf", [128, 128], F32)
        G["bardum"] = sbuf_o("g_bardum", [128, 8], F32)
        p = Prog(nc)
        p.sems = {"lncp": Sem(octx.enter_context(nc.semaphore("s_lncp")))}
        G["ccs"] = Sem(octx.enter_context(nc.semaphore("s_cc")))
        gbar = Sem(octx.enter_context(nc.semaphore("s_gbar")))
        g_ld = Sem(octx.enter_context(nc.semaphore("s_gld")))

        p.dma("sync", G["idb"], T["identb"], g_ld)
        p.dma("sync", G["idf"], T["identf"], g_ld)
        for ek in Prog.ENGINES:
            p.wait(ek, g_ld)

        def gbarrier(sems):
            _phase_barrier(p, sems + [p.sems["lncp"]],
                           G["bardum"][0:1, 0:1], G["bardum"][0:1, 1:2],
                           G["bardum"][0:1, 2:3], gbar)

        sems1 = emit_phase1(nc, p, octx, T, G)
        gbarrier(sems1)
        sems2 = emit_phase2(nc, p, octx, T, G)
        gbarrier(sems2)
        sems3 = emit_phase3(nc, p, octx, T, G)
        gbarrier(sems3)
        sems4 = emit_phase4(nc, p, octx, T, G)
        gbarrier(sems4)
        emit_phase5(nc, p, octx, T, G)
        p.build()
    return nc


def kernel(**inputs):
    h = _prep_host(inputs)
    cores = list(range(NCORES))
    nc = build_fused()
    im = [{
        "xin": h["x_slices"][c], "w1t": h["w1t"], "w2t": h["w2t"],
        "wup": h["wup"], "wgate": h["wgate"], "wdown": h["wdown"],
        "ipw_t": h["ipw_t"], "ipb_row": h["ipb_row"],
        "opw_t": h["opw_t"], "opb_row": h["opb_row"],
        "rt_t": h["rt_t"], "rb_row": h["rb_row"],
        "gat": h["gat"], "upt": h["upt"], "dat": h["dat"],
        "sgt": h["sgt"], "sut": h["sut"], "sdt": h["sdt"],
        "identb": h["identb"], "identf": h["identf"], "triu": h["triu"],
        "pmask_i": h["pmask"][c],
    } for c in cores]
    r = run_bass_kernel_spmd(nc, im, core_ids=cores).results
    out = np.zeros((8, 2048, 1024), np.float32)
    for c in cores:
        out[:, c * 256:(c + 1) * 256, :] = r[c]["out_o"].reshape(8, 256, 1024)
    return out


def emit_phase2(nc, p, octx, T, G):
    xin = T["xin"]
    xn_i = T["xn_d"]
    phiq_fm, phik_fm = T["phiq_fm_d"], T["phik_fm_d"]
    v_tm = T["v_tm_d"]
    sc0_i = T["sc0_d"]
    st_ag = T["cc_st_out"]
    w2t, wup, wgate = T["w2t"], T["wup"], T["wgate"]
    xa_o = T["xa_d"]
    up_raw_o, gate_raw_o = T["up_raw_d"], T["gate_raw_d"]
    maxp2_o = T["cc_m23_in"][0:1, 0:1]
    maxp3_o = T["cc_m23_in"][0:1, 1:2]
    scr = T["scr_t"]

    BLK = NH * 65

    with ExitStack() as ctx:
        sbuf = lambda n, sh, dt: ctx.enter_context(nc.sbuf_tensor("p2" + n, sh, dt))[:, :]
        psum = lambda n, sh, dt: ctx.enter_context(nc.psum_tensor("p2" + n, sh, dt))[:, :]
        sem = lambda n: octx.enter_context(nc.semaphore("p2" + n))
        sinf = sbuf("sinf", [64, BLK], F32)
        smidf = sbuf("smidf", [64, BLK], F32)
        stg = sbuf("stg", [64, BLK], F32)
        sin_all = sbuf("sin_all", [128, 8 * 8 * 65], BF16)
        smid_all = sbuf("smid_all", [128, 8 * 8 * 65], BF16)
        pq_b = sbuf("pq_b", [128, 2 * KC * 256], BF16)
        pk_b = sbuf("pk_b", [128, 2 * KC * 256], BF16)
        v_b = sbuf("v_b", [128, 2 * 2 * BLK], BF16)
        att_bf = sbuf("att_bf", [128, 2 * 512], BF16)
        rec4 = sbuf("rec4", [128, 8], F32)
        ctxt = sbuf("ctxt", [128, D], F32)
        ctx_bf = sbuf("ctx_bf", [128, D], BF16)
        ctx_fm = sbuf("ctx_fm", [128, NT * D], BF16)
        w2b = sbuf("w2b", [128, KC * D], BF16)
        w2o = sbuf("w2o", [128, 512], F32)
        xa_st = sbuf("xa_st", [128, 4 * D], F32)
        xnt = sbuf("xnt", [128, 2 * D], F32)
        xq_i32 = sbuf("xq_i32", [128, D], I32)
        dm_s = sbuf("dm_s", [128, D], F32)
        rr_s = sbuf("rr_s", [128, D], F32)
        xq_bf = sbuf("xq_bf", [128, D], BF16)
        xq_fm = sbuf("xq_fm", [128, KC * R], BF16)
        wtile = sbuf("wtile", [128, 2 * D], F32)
        bmx = sbuf("bmx", [128, 2 * 32], F32)
        sblk = sbuf("sblk", [128, 2 * 32], F32)
        rsblk = sbuf("rsblk", [128, 2 * 32], F32)
        yw = sbuf("yw", [128, 2 * D], F32)
        runm = sbuf("runm", [128, 2], F32)
        cols = sbuf("cols", [128, 4], F32)
        m1sc = sbuf("m1sc", [1, 8], F32)
        mx1 = sbuf("mx1", [1, 128], F32)
        trm = sbuf("trm", [128, 128], F32)
        pmaskb = sbuf("pmask", [64, 8], F32)
        idb, idf = G["idb"], G["idf"]
        pb0 = psum("pb0", [128, 512], F32)
        pb1 = psum("pb1", [128, 512], F32)
        pb2 = psum("pb2", [128, 512], F32)
        pb3 = psum("pb3", [128, 512], F32)
        ps_t = psum("ps_t", [128, D], BF16)
        ps_tf = psum("ps_tf", [128, 512], F32)
        ps_mx = psum("ps_mx", [1, 128], F32)
        s_ld = sem("s_ld"); s_mm = sem("s_mm"); s_ev = sem("s_ev")
        s_st = sem("s_st"); s_aux = sem("s_aux"); s_cp = sem("s_cp")
        slot_sems = [sem(f"s2_sl{i}") for i in range(6)]

        ld = Sem(s_ld); mm = Sem(s_mm); ev = Sem(s_ev); st = Sem(s_st)
        aux = Sem(s_aux); cp = Sem(s_cp)
        sA = [Sem(slot_sems[0]), Sem(slot_sems[1])]
        sW = [Sem(slot_sems[2]), Sem(slot_sems[3])]
        sX = [Sem(slot_sems[4]), Sem(slot_sems[5])]

        p.dma("sync", trm, T["triu"], ld)
        p.dma("sync", pmaskb, T["pmask_i"], ld)

        p.wait("sync", G["ccs"], at=1)
        p.dma("sync", m1sc[0:1, 0:1], T["cc_m1_out"], ld)
        ld0 = ld.n
        p.wait("vector", ld, ld0)
        p.emit("vector", lambda e: e.tensor_scalar(out=m1sc[0:1, 1:2], in0=m1sc[0:1, 0:1],
                                                   scalar1=1.0 / 32767.0, scalar2=1e-12, op0=ALU.mult, op1=ALU.max))
        p.drain("vector")
        p.emit("vector", lambda e: e.reciprocal(out=m1sc[0:1, 2:3], in_=m1sc[0:1, 1:2]), inc=cp)
        p.wait("sync", cp)
        p.dma("sync", scr.ap()[0:1, 0:2], m1sc[0:1, 1:3], aux)
        p.wait("gpsimd", aux)
        p.dma("gpsimd", cols[:, 0:1], scr.ap()[0:1, 0:1].broadcast_to([128, 1]), ld)
        p.dma("gpsimd", cols[:, 1:2], scr.ap()[0:1, 1:2].broadcast_to([128, 1]), ld)
        p.emit("vector", lambda e: e.memset(cols[:, 2:3], 32768.0))
        p.emit("vector", lambda e: e.memset(runm, 0.0), inc=cp)

        p.wait("scalar", ld, ld.n)
        p.wait("scalar", cp, cp.n)
        p.wait("vector", ld, ld.n)
        xn_free = [0, 0]
        for t in range(NT):
            sl = t % 2
            p.wait("sync", cp, at=xn_free[sl])
            p.dma("sync", xnt[:, sl * D:(sl + 1) * D], xn_i[t * 128:(t + 1) * 128, :], sX[sl])
            p.wait("scalar", sX[sl])
            xsl = xnt[:, sl * D:(sl + 1) * D]
            p.emit("scalar", lambda e, xsl=xsl: e.activation(out=xq_i32, in_=xsl, func=AF.Identity,
                                                             bias=cols[:, 2:3], scale=cols[:, 1:2]), inc=cp)
            p.wait("vector", cp, at=cp.n)
            p.wait("vector", sX[sl])
            p.emit("vector", lambda e, xsl=xsl: e.scalar_tensor_tensor(out=dm_s, in0=xsl, scalar=cols[:, 1:2],
                                                                       in1=xq_i32, op0=ALU.mult, op1=ALU.subtract))
            p.drain("vector")
            p.emit("vector", lambda e: e.scalar_tensor_tensor(out=rr_s, in0=dm_s, scalar=-32767.5,
                                                              in1=xq_i32, op0=ALU.is_ge, op1=ALU.add))
            p.drain("vector")
            p.emit("vector", lambda e: e.tensor_scalar(out=xq_bf, in0=rr_s, scalar1=32768.0,
                                                       scalar2=cols[:, 0:1], op0=ALU.subtract, op1=ALU.mult), inc=cp)
            xn_free[sl] = cp.n
            p.wait("tensor", cp, at=cp.n)
            p.wait("tensor", ev)
            for c in range(KC):
                p.emit("tensor", lambda e, c=c: e.transpose(
                    ps_t[:, c * 128:(c + 1) * 128], xq_bf[:, c * 128:(c + 1) * 128], idb),
                    inc=(mm if c == KC - 1 else None))
            p.wait("scalar", mm, at=mm.n)
            p.emit("scalar", lambda e, t=t: e.activation(
                out=xq_fm.rearrange("p (c j) -> p c j", j=R)[:, :, t * 128:(t + 1) * 128],
                in_=ps_t.rearrange("p (c j) -> p c j", j=128), func=AF.Copy), inc=ev)
            p.wait("vector", ev)

        p.wait("sync", G["ccs"], at=2)
        for b in range(8):
            p.wait("vector", ld)
            p.emit("vector", lambda e: e.memset(sinf, 0.0), inc=cp)
            for j in range(8):
                p.wait("sync", cp)
                p.dma("sync", stg, st_ag[64 * j:64 * (j + 1), b * BLK:(b + 1) * BLK], ld)
                p.wait("vector", ld)
                p.emit("vector", lambda e, j=j: e.scalar_tensor_tensor(
                    out=sinf, in0=stg, scalar=pmaskb[0:64, j:j + 1],
                    in1=sinf, op0=ALU.mult, op1=ALU.add), inc=cp)
            p.wait("sync", cp)
            p.dma("sync", stg, sc0_i[0:64, b * BLK:(b + 1) * BLK], ld)
            p.wait("vector", ld)
            p.emit("vector", lambda e: e.tensor_tensor(out=smidf, in0=sinf, in1=stg, op=ALU.add), inc=cp)
            p.wait("gpsimd", cp)
            for src, dst in ((sinf, sin_all), (smidf, smid_all)):
                sv = src.rearrange("p (h2 two e) -> p h2 two e", two=2, e=65)
                dv = dst.rearrange("p (b h2 e) -> p b h2 e", b=8, e=65)
                p.dma("gpsimd", dv[0:64, b, :, :], sv[0:64, :, 0, :], ld)
                p.dma("gpsimd", dv[64:128, b, :, :], sv[0:64, :, 1, :], ld)
        states_ld = ld.n

        p.wait("vector", ld, states_ld)
        batch_free = [0, 0]
        att_free = [0, 0]
        y_free = [0, 0]
        for b in range(8):
            sl = b % 2
            p.wait("gpsimd", mm, at=batch_free[sl])
            p.dma("gpsimd", pq_b[:, sl * KC * 256:(sl + 1) * KC * 256],
                  phiq_fm[:, b * KC * 256:(b + 1) * KC * 256], sA[sl])
            p.dma("gpsimd", pk_b[:, sl * KC * 256:(sl + 1) * KC * 256],
                  phik_fm[:, b * KC * 256:(b + 1) * KC * 256], sA[sl])
            p.dma("gpsimd", v_b.rearrange("p (s c j) -> p s c j", s=2, j=BLK)[:, sl, :, :],
                  v_tm[b * 256:(b + 1) * 256, :].rearrange("(c p) j -> p c j", p=128), sA[sl])
            p.wait("tensor", sA[sl])
            pqs = pq_b[:, sl * KC * 256:(sl + 1) * KC * 256]
            pks = pk_b[:, sl * KC * 256:(sl + 1) * KC * 256]
            vbs = v_b[:, sl * 2 * BLK:(sl + 1) * 2 * BLK]
            for chn in range(2):
                for g in range(4):
                    gp = g % 2
                    par = g // 2
                    base4 = (g % 2) * 4
                    heads = [2 * (base4 + i) + par for i in range(4)]
                    patt = pb0 if gp == 0 else pb1
                    py = pb2 if gp == 0 else pb3
                    absl = att_bf[:, gp * 512:(gp + 1) * 512]
                    p.wait("tensor", ev, at=att_free[gp])
                    for hi in range(4):
                        hh = heads[hi]
                        cc, po = hh // 2, (hh % 2) * 64
                        lk = pks[po:po + 64, cc * 256 + chn * 128: cc * 256 + (chn + 1) * 128]
                        lq = pqs[po:po + 64, cc * 256 + chn * 128: cc * 256 + (chn + 1) * 128]
                        p.emit("tensor", lambda e, patt=patt, hi=hi, lk=lk, lq=lq: e.matmul(
                            patt[:, hi * 128:(hi + 1) * 128], lk, lq, start=True, stop=True),
                            inc=(mm if hi == 3 else None))
                    p.wait("vector", mm, at=mm.n)
                    p.emit("vector", lambda e, patt=patt, absl=absl: e.tensor_tensor(
                        out=absl.rearrange("p (x j) -> p x j", j=128),
                        in0=patt.rearrange("p (x j) -> p x j", j=128),
                        in1=trm.rearrange("p (o j) -> p o j", o=1).broadcast_to([128, 4, 128]),
                        op=ALU.mult), inc=ev)
                    att_free[gp] = ev.n
                    p.wait("tensor", ev, at=ev.n)
                    p.wait("tensor", ev, at=y_free[gp])
                    sall = sin_all if chn == 0 else smid_all
                    for hi in range(4):
                        hh = heads[hi]
                        cc, po = hh // 2, (hh % 2) * 64
                        lq = pqs[po:po + 64, cc * 256 + chn * 128: cc * 256 + (chn + 1) * 128]
                        p.emit("tensor", lambda e, py=py, hi=hi, absl=absl, vbs=vbs, chn=chn, hh=hh: e.matmul(
                            py[:, hi * 65:hi * 65 + 65], absl[:, hi * 128:(hi + 1) * 128],
                            vbs[:, chn * BLK + hh * 65: chn * BLK + (hh + 1) * 65],
                            start=True, stop=False))
                        p.emit("tensor", lambda e, py=py, hi=hi, lq=lq, sall=sall, b=b, hh=hh, po=po: e.matmul(
                            py[:, hi * 65:hi * 65 + 65], lq,
                            sall[po:po + 64, (b * 8 + hh // 2) * 65:(b * 8 + hh // 2 + 1) * 65],
                            start=False, stop=True), inc=(mm if hi == 3 else None))
                    p.wait("vector", mm, at=mm.n)
                    rsl = rec4[:, gp * 4:(gp + 1) * 4]
                    p.emit("vector", lambda e, py=py, rsl=rsl: e.reciprocal(
                        out=rsl, in_=py[:, 0:260].rearrange("p (x e) -> p x e", e=65)[:, :, 64:65]))
                    p.drain("vector")
                    p.emit("vector", lambda e, py=py, par=par, base4=base4, rsl=rsl: e.tensor_tensor(
                        out=ctxt.rearrange("p (h2 par d) -> p h2 par d", par=2, d=64)[:, base4:base4 + 4, par, :],
                        in0=py[:, 0:260].rearrange("p (x e) -> p x e", e=65)[:, :, 0:64],
                        in1=rsl.rearrange("p (x o) -> p x o", o=1).broadcast_to([128, 4, 64]),
                        op=ALU.mult), inc=ev)
                    y_free[gp] = ev.n
                p.wait("scalar", ev)
                p.emit("scalar", lambda e: e.activation(out=ctx_bf, in_=ctxt, func=AF.Copy), inc=cp)
                p.wait("tensor", cp, at=cp.n)
                p.wait("tensor", ev)
                for c in range(KC):
                    p.emit("tensor", lambda e, c=c: e.transpose(
                        ps_t[:, c * 128:(c + 1) * 128], ctx_bf[:, c * 128:(c + 1) * 128], idb),
                        inc=(mm if c == KC - 1 else None))
                p.wait("scalar", mm, at=mm.n)
                t_idx = b * 2 + chn
                p.emit("scalar", lambda e, t_idx=t_idx: e.activation(
                    out=ctx_fm.rearrange("p (c j) -> p c j", j=R)[:, :, t_idx * 128:(t_idx + 1) * 128],
                    in_=ps_t.rearrange("p (c j) -> p c j", j=128), func=AF.Copy), inc=ev)
                p.wait("vector", ev)
            batch_free[sl] = mm.n

        for c in range(KC):
            p.dma("gpsimd", w2b[:, c * D:(c + 1) * D], w2t[c * 128:(c + 1) * 128, :], ld)
        w2_ld = ld.n
        p.wait("tensor", ld, w2_ld)
        p.wait("tensor", ev)
        for tokc in range(4):
            p.wait("sync", ev if tokc == 0 else st)
            for tb in range(4):
                p.dma("sync", xa_st[:, tb * D:(tb + 1) * D],
                      xin[(tokc * 4 + tb) * 128:(tokc * 4 + tb + 1) * 128, :], ld)
            xa_ld = ld.n
            p.wait("vector", ld, xa_ld)
            for oc in range(KC):
                pp = pb0 if oc % 2 == 0 else pb1
                for c in range(KC):
                    p.emit("tensor", lambda e, pp=pp, oc=oc, c=c, tokc=tokc: e.matmul(
                        pp, w2b[:, c * D + oc * 128: c * D + (oc + 1) * 128],
                        ctx_fm[:, c * R + tokc * 512: c * R + (tokc + 1) * 512],
                        start=(c == 0), stop=(c == KC - 1)), inc=(mm if c == KC - 1 else None))
                p.wait("scalar", mm, at=mm.n)
                p.wait("scalar", ev)
                p.emit("scalar", lambda e, pp=pp: e.activation(out=w2o, in_=pp, func=AF.Copy), inc=ev)
                p.wait("tensor", ev, at=ev.n)
                for tb in range(4):
                    p.emit("tensor", lambda e, tb=tb: e.transpose(
                        ps_tf[:, tb * 128:(tb + 1) * 128], w2o[:, tb * 128:(tb + 1) * 128], idf),
                        inc=(mm if tb == 3 else None))
                p.wait("vector", mm, at=mm.n)
                p.emit("vector", lambda e, oc=oc: e.tensor_tensor(
                    out=xa_st.rearrange("p (t j) -> p t j", j=D)[:, :, oc * 128:(oc + 1) * 128],
                    in0=xa_st.rearrange("p (t j) -> p t j", j=D)[:, :, oc * 128:(oc + 1) * 128],
                    in1=ps_tf.rearrange("p (t j) -> p t j", j=128), op=ALU.add), inc=ev)
                p.wait("tensor", ev)
            p.wait("sync", ev)
            for tb in range(4):
                p.dma("sync", xa_o[(tokc * 4 + tb) * 128:(tokc * 4 + tb + 1) * 128, :],
                      xa_st[:, tb * D:(tb + 1) * D], st)

        iw0 = dm_s.bitcast(I32); iw1 = rr_s.bitcast(I32)
        dw = xnt
        wq_bf = ctxt.bitcast(BF16)
        wlhs = xa_st[:, 0:1024].bitcast(BF16)
        upst = xa_st[:, 2048:3072]
        p.emit("vector", lambda e: e.memset(m1sc[0:1, 4:5], 16.0))
        p.emit("vector", lambda e: e.memset(cols[:, 3:4], 16.0), inc=cp)
        p.wait("scalar", cp)
        w_free = [0, 0]
        lhs_free = [0, 0]
        pp_ev = [0, 0]
        for wi, (wsrc, dst) in enumerate(((wup, up_raw_o), (wgate, gate_raw_o))):
            for oci in range(24):
                oc = oci
                sl = (wi * 24 + oci) % 2
                wts = wtile[:, sl * D:(sl + 1) * D]
                yws = yw[:, sl * D:(sl + 1) * D]
                iws = iw0 if sl == 0 else iw1
                dws = dw[:, sl * D:(sl + 1) * D]
                wqs = wq_bf[:, sl * D:(sl + 1) * D]
                wls = wlhs[:, sl * D:(sl + 1) * D]
                bms = bmx[:, sl * 32:(sl + 1) * 32]
                sbs = sblk[:, sl * 32:(sl + 1) * 32]
                rss = rsblk[:, sl * 32:(sl + 1) * 32]
                p.wait("sync", cp, at=w_free[sl])
                p.dma("sync", wts, wsrc[oc * 128:(oc + 1) * 128, :], sW[sl])
                p.wait("vector", sW[sl])
                p.emit("vector", lambda e, wts=wts, bms=bms: e.tensor_reduce(
                    out=bms, in_=wts.rearrange("p (g x) -> p g x", x=32), axis=AX.X,
                    op=ALU.max, apply_absolute_value=True))
                p.drain("vector")
                p.emit("vector", lambda e, bms=bms, sbs=sbs: e.tensor_scalar(
                    out=sbs, in0=bms, scalar1=1.0 / 7.0, scalar2=1e-12, op0=ALU.mult, op1=ALU.max))
                p.drain("vector")
                p.emit("vector", lambda e, sbs=sbs, rss=rss: e.reciprocal(out=rss, in_=sbs))
                p.drain("vector")
                p.emit("vector", lambda e, wts=wts, yws=yws, rss=rss: e.tensor_tensor(
                    out=yws.rearrange("p (g x) -> p g x", x=32), in0=wts.rearrange("p (g x) -> p g x", x=32),
                    in1=rss.rearrange("p (g o) -> p g o", o=1).broadcast_to([128, 32, 32]), op=ALU.mult), inc=cp)
                w_free[sl] = cp.n
                p.wait("scalar", cp, at=cp.n)
                p.emit("scalar", lambda e, yws=yws, iws=iws: e.activation(
                    out=iws, in_=yws, func=AF.Identity, bias=cols[:, 3:4], scale=1.0), inc=cp)
                p.wait("vector", cp, at=cp.n)
                p.emit("vector", lambda e, yws=yws, iws=iws, dws=dws: e.scalar_tensor_tensor(
                    out=dws, in0=yws, scalar=16.0, in1=iws, op0=ALU.add, op1=ALU.subtract))
                p.drain("vector")
                p.emit("vector", lambda e, dws=dws, iws=iws: e.scalar_tensor_tensor(
                    out=dws, in0=dws, scalar=0.5, in1=iws, op0=ALU.is_ge, op1=ALU.add))
                p.drain("vector")
                p.emit("vector", lambda e, dws=dws: e.tensor_scalar(
                    out=dws, in0=dws, scalar1=16.0, scalar2=None, op0=ALU.subtract))
                p.drain("vector")
                p.emit("vector", lambda e, dws=dws, wqs=wqs, sbs=sbs: e.tensor_tensor(
                    out=wqs.rearrange("p (g x) -> p g x", x=32), in0=dws.rearrange("p (g x) -> p g x", x=32),
                    in1=sbs.rearrange("p (g o) -> p g o", o=1).broadcast_to([128, 32, 32]), op=ALU.mult), inc=cp)
                p.wait("tensor", cp, at=cp.n)
                p.wait("tensor", mm, at=lhs_free[sl])
                p.wait("tensor", ev)
                for c in range(KC):
                    p.emit("tensor", lambda e, c=c, wqs=wqs: e.transpose(
                        ps_t[:, c * 128:(c + 1) * 128], wqs[:, c * 128:(c + 1) * 128], idb),
                        inc=(mm if c == KC - 1 else None))
                p.wait("scalar", mm, at=mm.n)
                p.emit("scalar", lambda e, wls=wls: e.activation(out=wls, in_=ps_t, func=AF.Copy), inc=ev)
                p.wait("tensor", ev, at=ev.n)
                for tokc in range(4):
                    pp = pb0 if tokc % 2 == 0 else pb1
                    usl = upst[:, (tokc % 2) * 512:(tokc % 2 + 1) * 512]
                    p.wait("tensor", ev, at=pp_ev[tokc % 2])
                    for c in range(KC):
                        p.emit("tensor", lambda e, pp=pp, c=c, tokc=tokc, wls=wls: e.matmul(
                            pp, wls[:, c * 128:(c + 1) * 128],
                            xq_fm[:, c * R + tokc * 512: c * R + (tokc + 1) * 512],
                            start=(c == 0), stop=(c == KC - 1)), inc=(mm if c == KC - 1 else None))
                    if tokc == 3:
                        lhs_free[sl] = mm.n
                    p.wait("scalar", mm, at=mm.n)
                    p.wait("scalar", st)
                    p.wait("scalar", cp)
                    p.emit("scalar", lambda e, usl=usl, pp=pp: e.activation(out=usl, in_=pp, func=AF.Copy), inc=ev)
                    pp_ev[tokc % 2] = ev.n
                    p.wait("gpsimd", ev, at=ev.n)
                    p.dma("gpsimd", dst.rearrange("p (o j) -> p o j", j=R)[:, oc, tokc * 512:(tokc + 1) * 512], usl, st)
                    p.wait("vector", ev, at=ev.n)
                    p.emit("vector", lambda e, usl=usl: e.tensor_reduce(
                        out=rec4[:, 0:1], in_=usl, axis=AX.X, op=ALU.max, apply_absolute_value=True))
                    p.drain("vector")
                    p.emit("vector", lambda e, wi=wi: e.tensor_tensor(
                        out=runm[:, wi:wi + 1], in0=runm[:, wi:wi + 1], in1=rec4[:, 0:1], op=ALU.max), inc=cp)
        for wi, dst in ((0, maxp2_o), (1, maxp3_o)):
            p.wait("tensor", cp)
            if wi == 1:
                p.wait("tensor", ev)
            p.emit("tensor", lambda e, wi=wi: e.transpose(ps_mx[0:1, 0:128], runm[:, wi:wi + 1], idf), inc=mm)
            p.wait("vector", mm, at=mm.n)
            p.emit("vector", lambda e: e.tensor_copy(out=mx1, in_=ps_mx[0:1, :]), inc=ev)
            p.drain("vector")
            p.emit("vector", lambda e: e.reduce_max(out=m1sc[0:1, 3:4], in_=mx1[0:1, :], axis=AX.X), inc=ev)
            p.wait("sync", ev)
            p.dma("sync", dst, m1sc[0:1, 3:4], st)
            p.wait("vector", st)

        p.wait("gpsimd", st)
        p.cc("AllReduce", ALU.max, T["cc_m23_in"], T["cc_m23_out"], G["ccs"])
        return [ld, mm, ev, st, aux, cp] + sA + sW + sX


def _phase_barrier(p, sems, dum_v, dum_s, dum_g, bar):
    p.emit("vector", lambda e, d=dum_v: e.memset(d, 0.0), inc=bar)
    p.emit("scalar", lambda e, d=dum_s: e.activation(out=d, in_=d, func=AF.Copy), inc=bar)
    p.emit("gpsimd", lambda e, d=dum_g: e.memset(d, 0.0), inc=bar)
    tgt = bar.n
    for ek in ("sync", "scalar", "vector", "tensor", "gpsimd"):
        p.wait(ek, bar, at=tgt)
        for sm in sems:
            p.wait(ek, sm)


def _emit_scales(p, nc_scr, mrow, msc, cols_s, cols_rs, ld, cp, aux, idx):
    p.emit("vector", lambda e: e.reduce_max(out=msc[0:1, 0:1], in_=mrow, axis=AX.X))
    p.drain("vector")
    p.emit("vector", lambda e: e.tensor_scalar(out=msc[0:1, 1:2], in0=msc[0:1, 0:1],
                                               scalar1=1.0 / 32767.0, scalar2=1e-12, op0=ALU.mult, op1=ALU.max))
    p.drain("vector")
    p.emit("vector", lambda e: e.reciprocal(out=msc[0:1, 2:3], in_=msc[0:1, 1:2]), inc=cp)
    p.wait("sync", cp)
    p.dma("sync", nc_scr.ap()[0:1, idx:idx + 2], msc[0:1, 1:3], aux)
    p.wait("gpsimd", aux)
    p.dma("gpsimd", cols_s, nc_scr.ap()[0:1, idx:idx + 1].broadcast_to([128, 1]), ld)
    p.dma("gpsimd", cols_rs, nc_scr.ap()[0:1, idx + 1:idx + 2].broadcast_to([128, 1]), ld)


def _emit_qdq(p, src, dst, i32b, dmb, s_col, rs_col, b_col, cp):
    p.wait("scalar", cp)
    p.emit("scalar", lambda e: e.activation(out=i32b, in_=src, func=AF.Identity,
                                            bias=b_col, scale=rs_col), inc=cp)
    p.wait("vector", cp)
    p.emit("vector", lambda e: e.scalar_tensor_tensor(out=dmb, in0=src, scalar=rs_col,
                                                      in1=i32b, op0=ALU.mult, op1=ALU.subtract))
    p.drain("vector")
    p.emit("vector", lambda e: e.scalar_tensor_tensor(out=dmb, in0=dmb, scalar=-32767.5,
                                                      in1=i32b, op0=ALU.is_ge, op1=ALU.add))
    p.drain("vector")
    p.emit("vector", lambda e: e.tensor_scalar(out=dst, in0=dmb, scalar1=32768.0,
                                               scalar2=s_col, op0=ALU.subtract, op1=ALU.mult), inc=cp)


def _emit_colmax_out(p, runcol, mx1, msc_sl, ps_mx, idf, dst, mm, ev, st):
    p.emit("tensor", lambda e: e.transpose(ps_mx[0:1, 0:128], runcol, idf), inc=mm)
    p.wait("vector", mm)
    p.emit("vector", lambda e: e.tensor_copy(out=mx1, in_=ps_mx[0:1, :]), inc=ev)
    p.drain("vector")
    p.emit("vector", lambda e: e.reduce_max(out=msc_sl, in_=mx1[0:1, :], axis=AX.X), inc=ev)
    p.wait("sync", ev)
    p.dma("sync", dst, msc_sl, st)
    p.wait("vector", st)


def emit_phase3(nc, p, octx, T, G):
    up_raw_i, gate_raw_i = T["up_raw_d"], T["gate_raw_d"]
    upq_o, t_o = T["upq_d"], T["t_d"]
    maxp4_o = T["cc_m4_in"]
    scr = T["scr_t"]

    with ExitStack() as ctx:
        sbuf = lambda n, sh, dt: ctx.enter_context(nc.sbuf_tensor("p3" + n, sh, dt))[:, :]
        psum = lambda n, sh, dt: ctx.enter_context(nc.psum_tensor("p3" + n, sh, dt))[:, :]
        sem = lambda n: octx.enter_context(nc.semaphore("p3" + n))
        upc = sbuf("upc", [128, 2 * R], BF16)
        gac = sbuf("gac", [128, 2 * R], BF16)
        i32u = sbuf("i32u", [128, 2 * R], I32)
        dmu = sbuf("dmu", [128, 2 * R], F32)
        i32g = sbuf("i32g", [128, 2 * R], I32)
        dmg = sbuf("dmg", [128, 2 * R], F32)
        gq = sbuf("gq", [128, 2 * R], F32)
        sg = sbuf("sg", [128, 2 * R], F32)
        outU = sbuf("outU", [128, 2 * R], F32)
        outT = sbuf("outT", [128, 2 * R], F32)
        m2row = sbuf("m2row", [1, 8], F32)
        m3row = sbuf("m3row", [1, 8], F32)
        msc = sbuf("msc", [1, 8], F32)
        cols = sbuf("cols", [128, 8], F32)
        runm = sbuf("runm", [128, 1], F32)
        mx1 = sbuf("mx1", [1, 128], F32)
        idf = G["idf"]
        ps_mx = psum("ps_mx", [1, 128], F32)
        s_ld = sem("s_ld"); s_cp = sem("s_cp"); s_ev = sem("s_ev")
        s_st = sem("s_st"); s_aux = sem("s_aux"); s_mm = sem("s_mm")
        ld = Sem(s_ld); cp = Sem(s_cp); ev = Sem(s_ev); st = Sem(s_st)
        aux = Sem(s_aux); mm = Sem(s_mm)

        p.wait("sync", G["ccs"], at=3)
        p.dma("sync", m2row[0:1, 0:1], T["cc_m23_out"][0:1, 0:1], ld)
        p.dma("sync", m3row[0:1, 0:1], T["cc_m23_out"][0:1, 1:2], ld)
        p.emit("vector", lambda e: e.memset(cols[:, 4:5], 32768.0))
        p.emit("vector", lambda e: e.memset(runm, 0.0), inc=cp)
        p.wait("vector", ld)
        _emit_scales(p, scr, m2row[0:1, 0:1], msc, cols[:, 0:1], cols[:, 1:2], ld, cp, aux, 4)
        _emit_scales(p, scr, m3row[0:1, 0:1], msc, cols[:, 2:3], cols[:, 3:4], ld, cp, aux, 6)
        p.wait("scalar", ld)
        p.wait("scalar", cp)
        p.wait("vector", ld)
        cpS = Sem(sem("s_cs")); cpV = Sem(sem("s_cv")); cpG = Sem(sem("s_cg"))
        sl_ld = [Sem(sem("s_l0")), Sem(sem("s_l1"))]
        v_free = [0, 0]; g_free = [0, 0]; s_free = [0, 0]
        upst_done = [0, 0]
        tst_done = [0, 0]

        def gchain(ops):
            for i, fn in enumerate(ops):
                p.emit("vector", fn, inc=cpG)
                if i < len(ops) - 1:
                    p.drain("vector")

        for oc in range(24):
            sl = oc % 2
            c2 = lambda buf, sl=sl: buf[:, sl * R:(sl + 1) * R]
            U, Gc = c2(upc), c2(gac)
            Iu, Du = c2(i32u), c2(dmu)
            Ig, Dg = c2(i32g), c2(dmg)
            Q, S_, Ou, Ot = c2(gq), c2(sg), c2(outU), c2(outT)
            p.wait("sync", cpV, at=v_free[sl])
            p.wait("sync", cpG, at=g_free[sl])
            p.wait("sync", cpS, at=s_free[sl])
            p.dma("sync", U, up_raw_i[:, oc * R:(oc + 1) * R], sl_ld[sl])
            p.dma("sync", Gc, gate_raw_i[:, oc * R:(oc + 1) * R], sl_ld[sl])
            ldv = sl_ld[sl].n
            p.wait("scalar", sl_ld[sl], at=ldv)
            p.emit("scalar", lambda e, U=U, Iu=Iu: e.activation(
                out=Iu, in_=U, func=AF.Identity, bias=cols[:, 4:5], scale=cols[:, 1:2]), inc=cpS)
            cv_u = cpS.n
            p.emit("scalar", lambda e, Gc=Gc, Ig=Ig: e.activation(
                out=Ig, in_=Gc, func=AF.Identity, bias=cols[:, 4:5], scale=cols[:, 3:4]), inc=cpS)
            cv_g = cpS.n
            p.wait("vector", cpS, at=cv_u)
            p.wait("vector", st, at=upst_done[sl])
            gchain([
                lambda e, U=U, Iu=Iu, Du=Du: e.scalar_tensor_tensor(
                    out=Du, in0=U, scalar=cols[:, 1:2], in1=Iu, op0=ALU.mult, op1=ALU.subtract),
                lambda e, Iu=Iu, Du=Du: e.scalar_tensor_tensor(
                    out=Du, in0=Du, scalar=-32767.5, in1=Iu, op0=ALU.is_ge, op1=ALU.add),
                lambda e, Du=Du, Ou=Ou: e.tensor_scalar(
                    out=Ou, in0=Du, scalar1=32768.0, scalar2=cols[:, 0:1],
                    op0=ALU.subtract, op1=ALU.mult),
            ])
            upq_done = cpG.n
            p.wait("vector", cpS, at=cv_g)
            p.emit("vector", lambda e, Gc=Gc, Ig=Ig, Dg=Dg: e.scalar_tensor_tensor(
                out=Dg, in0=Gc, scalar=cols[:, 3:4], in1=Ig, op0=ALU.mult, op1=ALU.subtract))
            p.drain("vector")
            p.emit("vector", lambda e, Ig=Ig, Dg=Dg: e.scalar_tensor_tensor(
                out=Dg, in0=Dg, scalar=-32767.5, in1=Ig, op0=ALU.is_ge, op1=ALU.add))
            p.drain("vector")
            p.emit("vector", lambda e, Dg=Dg, Q=Q: e.tensor_scalar(
                out=Q, in0=Dg, scalar1=32768.0, scalar2=cols[:, 2:3],
                op0=ALU.subtract, op1=ALU.mult), inc=cpV)
            gq_done = cpV.n
            p.wait("scalar", cpV, at=gq_done)
            p.emit("scalar", lambda e, Q=Q, S_=S_: e.activation(out=S_, in_=Q, func=AF.Sigmoid), inc=cpS)
            p.emit("scalar", lambda e, S_=S_, Ig=Ig: e.activation(
                out=Ig, in_=S_, func=AF.Identity, scale=65536.0), inc=cpS)
            sgc = cpS.n
            p.wait("vector", cpS, at=sgc)
            gchain([
                lambda e, S_=S_, Ig=Ig, Du=Du: e.scalar_tensor_tensor(
                    out=Du, in0=S_, scalar=65536.0, in1=Ig, op0=ALU.mult, op1=ALU.subtract),
                lambda e, Ig=Ig, Du=Du: e.scalar_tensor_tensor(
                    out=Du, in0=Du, scalar=0.5, in1=Ig, op0=ALU.is_ge, op1=ALU.add),
                lambda e, Du=Du, S_=S_: e.tensor_scalar(
                    out=S_, in0=Du, scalar1=65535.0, scalar2=1.0 / 65536.0,
                    op0=ALU.min, op1=ALU.mult),
            ])
            fq_done = cpG.n
            p.wait("vector", cpG, at=fq_done)
            p.wait("vector", st, at=tst_done[sl])
            p.emit("vector", lambda e, Q=Q, S_=S_, Ot=Ot: e.tensor_tensor(
                out=Ot, in0=Q, in1=S_, op=ALU.mult), inc=cpV)
            t_done = cpV.n
            p.drain("vector")
            p.emit("vector", lambda e, Ot=Ot, Dg=Dg: e.tensor_reduce(
                out=Dg[:, 0:1], in_=Ot, axis=AX.X, op=ALU.max, apply_absolute_value=True))
            p.drain("vector")
            p.emit("vector", lambda e, Dg=Dg: e.tensor_tensor(
                out=runm, in0=runm, in1=Dg[:, 0:1], op=ALU.max), inc=cpV)
            v_free[sl] = cpV.n
            g_free[sl] = cpG.n
            s_free[sl] = cpS.n
            p.wait("gpsimd", cpG, at=upq_done)
            p.dma("gpsimd", upq_o[:, oc * R:(oc + 1) * R], Ou, st)
            upst_done[sl] = st.n
            p.wait("gpsimd", cpV, at=t_done)
            p.dma("gpsimd", t_o[:, oc * R:(oc + 1) * R], Ot, st)
            tst_done[sl] = st.n
        p.wait("tensor", cpV)
        _emit_colmax_out(p, runm[:, 0:1], mx1, msc[0:1, 7:8], ps_mx, idf, maxp4_o, mm, ev, st)
        p.wait("gpsimd", st)
        p.cc("AllReduce", ALU.max, T["cc_m4_in"], T["cc_m4_out"], G["ccs"])
        return [ld, cp, ev, st, aux, mm, cpS, cpV, cpG] + sl_ld


def emit_phase4(nc, p, octx, T, G):
    t_i, upq_i = T["t_d"], T["upq_d"]
    oraw_o = T["oraw_d"]
    maxp5_o = T["cc_m5_in"]
    scr = T["scr_t"]

    with ExitStack() as ctx:
        sbuf = lambda n, sh, dt: ctx.enter_context(nc.sbuf_tensor("p4" + n, sh, dt))[:, :]
        psum = lambda n, sh, dt: ctx.enter_context(nc.psum_tensor("p4" + n, sh, dt))[:, :]
        sem = lambda n: octx.enter_context(nc.semaphore("p4" + n))
        tc = sbuf("tc", [128, 2 * R], BF16)
        uc = sbuf("uc", [128, 2 * R], BF16)
        i32b = sbuf("i32b", [128, 2 * R], I32)
        dmb = sbuf("dmb", [128, 2 * R], F32)
        g2 = sbuf("g2", [128, 2 * R], F32)
        outb = sbuf("outb", [128, 2 * R], F32)
        m4row = sbuf("m4row", [1, 8], F32)
        msc = sbuf("msc", [1, 8], F32)
        cols = sbuf("cols", [128, 4], F32)
        runm = sbuf("runm", [128, 1], F32)
        mx1 = sbuf("mx1", [1, 128], F32)
        idf = G["idf"]
        ps_mx = psum("ps_mx", [1, 128], F32)
        s_ld = sem("s_ld"); s_cp = sem("s_cp"); s_ev = sem("s_ev")
        s_st = sem("s_st"); s_aux = sem("s_aux"); s_mm = sem("s_mm")
        ld = Sem(s_ld); cp = Sem(s_cp); ev = Sem(s_ev); st = Sem(s_st)
        aux = Sem(s_aux); mm = Sem(s_mm)

        p.wait("sync", G["ccs"], at=4)
        p.dma("sync", m4row[0:1, 0:1], T["cc_m4_out"], ld)
        p.emit("vector", lambda e: e.memset(cols[:, 2:3], 32768.0))
        p.emit("vector", lambda e: e.memset(runm, 0.0), inc=cp)
        p.wait("vector", ld)
        _emit_scales(p, scr, m4row[0:1, 0:1], msc, cols[:, 0:1], cols[:, 1:2], ld, cp, aux, 8)
        p.wait("scalar", ld)
        p.wait("scalar", cp)
        cpS = Sem(sem("s_cs")); cpV = Sem(sem("s_cv")); cpG = Sem(sem("s_cg"))
        sl_ld = [Sem(sem("s_l0")), Sem(sem("s_l1"))]
        v_free = [0, 0]; g_free = [0, 0]; s_free = [0, 0]
        ost_done = [0, 0]
        for oc in range(24):
            sl = oc % 2
            c2 = lambda buf, sl=sl: buf[:, sl * R:(sl + 1) * R]
            Tc, Uc, I, Dm, G2, Ob = c2(tc), c2(uc), c2(i32b), c2(dmb), c2(g2), c2(outb)
            p.wait("sync", cpV, at=v_free[sl])
            p.wait("sync", cpG, at=g_free[sl])
            p.wait("sync", cpS, at=s_free[sl])
            p.dma("sync", Tc, t_i[:, oc * R:(oc + 1) * R], sl_ld[sl])
            p.dma("sync", Uc, upq_i[:, oc * R:(oc + 1) * R], sl_ld[sl])
            ldv = sl_ld[sl].n
            p.wait("scalar", sl_ld[sl], at=ldv)
            p.emit("scalar", lambda e, Tc=Tc, I=I: e.activation(
                out=I, in_=Tc, func=AF.Identity, bias=cols[:, 2:3], scale=cols[:, 1:2]), inc=cpS)
            cv = cpS.n
            p.wait("vector", cpS, at=cv)
            p.emit("vector", lambda e, Tc=Tc, I=I, Dm=Dm: e.scalar_tensor_tensor(
                out=Dm, in0=Tc, scalar=cols[:, 1:2], in1=I, op0=ALU.mult, op1=ALU.subtract))
            p.drain("vector")
            p.emit("vector", lambda e, I=I, Dm=Dm: e.scalar_tensor_tensor(
                out=Dm, in0=Dm, scalar=-32767.5, in1=I, op0=ALU.is_ge, op1=ALU.add))
            p.drain("vector")
            p.emit("vector", lambda e, Dm=Dm, G2=G2: e.tensor_scalar(
                out=G2, in0=Dm, scalar1=32768.0, scalar2=cols[:, 0:1],
                op0=ALU.subtract, op1=ALU.mult), inc=cpV)
            q_done = cpV.n
            p.wait("vector", st, at=ost_done[sl])
            p.emit("vector", lambda e, G2=G2, Uc=Uc, Ob=Ob: e.tensor_tensor(
                out=Ob, in0=G2, in1=Uc, op=ALU.mult), inc=cpG)
            o_done = cpG.n
            p.wait("gpsimd", cpG, at=o_done)
            p.dma("gpsimd", oraw_o[:, oc * R:(oc + 1) * R], Ob, st)
            ost_done[sl] = st.n
            p.wait("vector", cpG, at=o_done)
            p.emit("vector", lambda e, Ob=Ob, Dm=Dm: e.tensor_reduce(
                out=Dm[:, 0:1], in_=Ob, axis=AX.X, op=ALU.max, apply_absolute_value=True))
            p.drain("vector")
            p.emit("vector", lambda e, Dm=Dm: e.tensor_tensor(
                out=runm, in0=runm, in1=Dm[:, 0:1], op=ALU.max), inc=cpV)
            v_free[sl] = cpV.n
            g_free[sl] = cpG.n
            s_free[sl] = cpS.n
        p.wait("tensor", cpV)
        _emit_colmax_out(p, runm[:, 0:1], mx1, msc[0:1, 7:8], ps_mx, idf, maxp5_o, mm, ev, st)
        p.wait("gpsimd", st)
        p.cc("AllReduce", ALU.max, T["cc_m5_in"], T["cc_m5_out"], G["ccs"])
        return [ld, cp, ev, st, aux, mm, cpS, cpV, cpG] + sl_ld


def emit_phase5(nc, p, octx, T, G):
    oraw_i = T["oraw_d"]
    xa_i = T["xa_d"]
    wdown, ipw_t, ipb_row = T["wdown"], T["ipw_t"], T["ipb_row"]
    opw_t, opb_row, rt_t, rb_row = T["opw_t"], T["opb_row"], T["rt_t"], T["rb_row"]
    gat, upt, dat = T["gat"], T["upt"], T["dat"]
    sgt, sut, sdt = T["sgt"], T["sut"], T["sdt"]
    out_o = T["out_o"]
    scr = T["scr_t"]
    wt_dram = T["wt_dram"]
    debug = 0

    with ExitStack() as ctx:
        sbuf = lambda n, sh, dt: ctx.enter_context(nc.sbuf_tensor("p5" + n, sh, dt))[:, :]
        psum = lambda n, sh, dt: ctx.enter_context(nc.psum_tensor("p5" + n, sh, dt))[:, :]
        sem = lambda n: octx.enter_context(nc.semaphore("p5" + n))
        x1 = sbuf("x1", [128, NT * D], F32)
        xn1_fm = sbuf("xn1_fm", [128, KC * R], BF16)
        big1 = sbuf("big1", [128, 12288], F32)
        big2 = sbuf("big2", [128, 8192], F32)
        big3 = sbuf("big3", [128, 4096], F32)
        q512b = sbuf("q512b", [128, 512], F32)
        q512c = sbuf("q512c", [128, 512], F32)
        ev512 = sbuf("ev512", [128, 512], F32)
        bias5 = q512b
        wpool = sbuf("wpool", [128, 4096], BF16)
        dlhs = wpool
        gulhs = wpool[:, 0:2048]
        oplhs = wpool[:, 0:1024]
        rtb = wpool[:, 0:256]
        sglhs = wpool[:, 0:1024]
        sulhs = wpool[:, 1024:2048]
        sdlhs = wpool[:, 0:512]
        wbc = big1[:, 0:512]
        wtstg = big3[0:32, 1024:1152]
        wall = sbuf("wall", [128, 32], F32)
        opb = sbuf("opb", [128, 8], F32)
        rbb = sbuf("rbb", [128, 32], F32)
        bmx3 = sbuf("bmx3", [128, 96], F32)
        stat = sbuf("stat", [128, 8], F32)
        cols = sbuf("cols", [128, 4], F32)
        m5row = sbuf("m5row", [1, 8], F32)
        msc = sbuf("msc", [1, 8], F32)
        idb, idf = G["idb"], G["idf"]
        ps_a = psum("ps_a", [128, 512], F32)
        ps_b = psum("ps_b", [128, 512], F32)
        ps_c = psum("ps_c", [128, 512], F32)
        ps_d = psum("ps_d", [128, 512], F32)
        ps_t = psum("ps_t", [128, D], BF16)
        ps_tf = psum("ps_tf", [128, 512], F32)
        s_ld = sem("s_ld"); s_cp = sem("s_cp"); s_ev = sem("s_ev")
        s_st = sem("s_st"); s_aux = sem("s_aux"); s_mm = sem("s_mm")
        s_bar = sem("s_bar")
        s_vc = sem("s_vc")
        slot_sems = [sem(f"s_sl{i}") for i in range(6)]
        ld = Sem(s_ld); cp = Sem(s_cp); ev = Sem(s_ev); st = Sem(s_st)
        aux = Sem(s_aux); mm = Sem(s_mm); cpvc = Sem(s_vc)
        sg = [Sem(slot_sems[0]), Sem(slot_sems[1])]
        sw = [Sem(slot_sems[2]), Sem(slot_sems[3])]
        sd = [Sem(slot_sems[4]), Sem(slot_sems[5])]
        bar = Sem(s_bar)
        allsems = [ld, cp, ev, st, aux, mm, cpvc, p.sems["lncp"]] + sg + sw + sd
        barrier = lambda: _phase_barrier(p, allsems, stat[0:1, 6:7], stat[0:1, 7:8], msc[0:1, 6:7], bar)

        p.wait("sync", G["ccs"], at=5)
        p.dma("sync", m5row[0:1, 0:1], T["cc_m5_out"], ld)
        p.emit("vector", lambda e: e.memset(cols[:, 2:3], 32768.0), inc=cp)
        p.wait("vector", ld)
        _emit_scales(p, scr, m5row[0:1, 0:1], msc, cols[:, 0:1], cols[:, 1:2], ld, cp, aux, 10)
        p.dma("gpsimd", opb, opb_row.rearrange("a (c p) -> p a c", p=128)[:, 0, :], ld,
              allow_slow_non_contiguous=True)
        p.dma("gpsimd", rbb, rb_row.broadcast_to([128, 32]), ld)
        for t in range(NT):
            p.dma("sync", x1[:, t * D:(t + 1) * D], xa_i[t * 128:(t + 1) * 128, :], ld)
        xa_ld = ld.n

        wdlhs = big1.bitcast(BF16)
        wtile = big3[:, 0:1024]
        yw = big3[:, 1024:2048]
        iw = big3[:, 2048:3072].bitcast(I32)[:, 0:1024]
        dw = big3[:, 3072:4096]
        wq_bf = ev512.bitcast(BF16)
        p.emit("vector", lambda e: e.memset(stat[:, 0:1], 16.0), inc=cp)
        p.wait("scalar", ld, xa_ld)
        for oc in range(8):
            for pc in range(3):
                p.wait("sync", cp)
                p.dma("sync", wtile, wdown[oc * 128:(oc + 1) * 128, pc * 1024:(pc + 1) * 1024], ld)
                p.wait("vector", ld)
                p.emit("vector", lambda e: e.tensor_reduce(
                    out=bmx3[:, 0:32], in_=wtile.rearrange("p (g x) -> p g x", x=32), axis=AX.X,
                    op=ALU.max, apply_absolute_value=True))
                p.drain("vector")
                p.emit("vector", lambda e: e.tensor_scalar(out=bmx3[:, 32:64], in0=bmx3[:, 0:32],
                                                           scalar1=1.0 / 7.0, scalar2=1e-12, op0=ALU.mult, op1=ALU.max))
                p.drain("vector")
                p.emit("vector", lambda e: e.reciprocal(out=bmx3[:, 64:96], in_=bmx3[:, 32:64]))
                p.drain("vector")
                p.emit("vector", lambda e: e.tensor_tensor(
                    out=yw.rearrange("p (g x) -> p g x", x=32), in0=wtile.rearrange("p (g x) -> p g x", x=32),
                    in1=bmx3[:, 64:96].rearrange("p (g o) -> p g o", o=1).broadcast_to([128, 32, 32]),
                    op=ALU.mult), inc=cp)
                p.wait("scalar", cp)
                p.emit("scalar", lambda e: e.activation(out=iw, in_=yw, func=AF.Identity,
                                                        bias=stat[:, 0:1], scale=1.0), inc=cp)
                p.wait("vector", cp)
                p.emit("vector", lambda e: e.scalar_tensor_tensor(out=dw, in0=yw, scalar=16.0,
                                                                  in1=iw, op0=ALU.add, op1=ALU.subtract))
                p.drain("vector")
                p.emit("vector", lambda e: e.scalar_tensor_tensor(out=dw, in0=dw, scalar=0.5,
                                                                  in1=iw, op0=ALU.is_ge, op1=ALU.add))
                p.drain("vector")
                p.emit("vector", lambda e: e.tensor_scalar(out=dw, in0=dw, scalar1=16.0, scalar2=None,
                                                           op0=ALU.subtract))
                p.drain("vector")
                p.emit("vector", lambda e: e.tensor_tensor(
                    out=wq_bf.rearrange("p (g x) -> p g x", x=32), in0=dw.rearrange("p (g x) -> p g x", x=32),
                    in1=bmx3[:, 32:64].rearrange("p (g o) -> p g o", o=1).broadcast_to([128, 32, 32]),
                    op=ALU.mult), inc=cp)
                p.wait("tensor", cp)
                p.wait("tensor", ev)
                for c in range(8):
                    p.emit("tensor", lambda e, c=c: e.transpose(
                        ps_t[:, c * 128:(c + 1) * 128], wq_bf[:, c * 128:(c + 1) * 128], idb),
                        inc=(mm if c == 7 else None))
                p.wait("scalar", mm)
                p.emit("scalar", lambda e, oc=oc, pc=pc: e.activation(
                    out=wdlhs.rearrange("p (o k j) -> p o k j", o=8, j=128)[:, oc, pc * 8:(pc + 1) * 8, :],
                    in_=ps_t.rearrange("p (k j) -> p k j", j=128), func=AF.Copy), inc=ev)
        o_slab = big2.bitcast(BF16)[:, 0:24 * 512]
        qstage = big3[:, 0:2048]
        qi32 = big3[:, 2048:4096].bitcast(I32)[:, 0:2048]
        qbf = wpool[:, 0:2048]
        p.wait("scalar", cp)
        p.wait("scalar", ev)
        for tq in range(4):
            for g in range(6):
                p.wait("sync", cp)
                p.dma("sync", qbf.rearrange("p (k j) -> p k j", j=512),
                      oraw_i.rearrange("p (k r) -> p k r", r=R)[:, 4 * g:4 * g + 4,
                                                               tq * 512:(tq + 1) * 512], ld)
                p.wait("scalar", ld)
                _emit_qdq(p, qbf, o_slab[:, g * 2048:(g + 1) * 2048],
                          qi32, qstage, cols[:, 0:1], cols[:, 1:2], cols[:, 2:3], cp)
            p.wait("tensor", cp)
            for oc in range(8):
                pp = ps_a if oc % 2 == 0 else ps_b
                for kc in range(24):
                    p.emit("tensor", lambda e, pp=pp, oc=oc, kc=kc: e.matmul(
                        pp, wdlhs.rearrange("p (o k j) -> p o k j", o=8, j=128)[:, oc, kc, :],
                        o_slab[:, kc * 512:(kc + 1) * 512],
                        start=(kc == 0), stop=(kc == 23)), inc=(mm if kc == 23 else None))
                p.wait("scalar", mm)
                p.wait("scalar", st)
                p.emit("scalar", lambda e, pp=pp: e.activation(out=ev512, in_=pp, func=AF.Copy), inc=ev)
                p.wait("tensor", ev)
                for tb in range(4):
                    p.emit("tensor", lambda e, tb=tb: e.transpose(
                        ps_tf[:, tb * 128:(tb + 1) * 128], ev512[:, tb * 128:(tb + 1) * 128], idf),
                        inc=(mm if tb == 3 else None))
                p.wait("vector", mm)
                p.wait("vector", ld, xa_ld)
                p.emit("vector", lambda e, tq=tq, oc=oc: e.tensor_tensor(
                    out=x1.rearrange("p (t j) -> p t j", j=D)[:, tq * 4:(tq + 1) * 4, oc * 128:(oc + 1) * 128],
                    in0=x1.rearrange("p (t j) -> p t j", j=D)[:, tq * 4:(tq + 1) * 4, oc * 128:(oc + 1) * 128],
                    in1=ps_tf.rearrange("p (t j) -> p t j", j=128), op=ALU.add), inc=ev)
                p.wait("tensor", ev)
            p.wait("scalar", ev)

        if debug & 1:
            p.wait("sync", ev)
            for t in range(NT):
                p.dma("sync", dbg_x1a[t * 128:(t + 1) * 128, :], x1[:, t * D:(t + 1) * D], st)
            p.wait("vector", st)
        barrier()
        sa = big3[:, 0:1024]
        sb = big3[:, 1024:2048]
        xb = big3[:, 2048:2560].bitcast(BF16)[:, 0:1024]
        stats = {"m": stat[:, 1:2], "var": stat[:, 2:3], "sd": stat[:, 3:4],
                 "rstd": stat[:, 4:5], "eps_col": stat[:, 5:6]}
        p.emit("vector", lambda e: e.memset(stat[:, 5:6], EPS))
        p.wait("vector", ev)
        for t in range(NT):
            xt = x1[:, t * D:(t + 1) * D]
            _ln_tile(p, xt, sb, sa, stats)
            p.emit("vector", lambda e: e.tensor_copy(out=xb, in_=sb), inc=p.sems["lncp"])
            cpv = p.sems["lncp"].n
            p.wait("tensor", p.sems["lncp"], cpv)
            p.wait("tensor", ev)
            for c in range(KC):
                p.emit("tensor", lambda e, c=c: e.transpose(
                    ps_t[:, c * 128:(c + 1) * 128], xb[:, c * 128:(c + 1) * 128], idb),
                    inc=(mm if c == KC - 1 else None))
            p.wait("scalar", mm)
            p.emit("scalar", lambda e, t=t: e.activation(
                out=xn1_fm.rearrange("p (c j) -> p c j", j=R)[:, :, t * 128:(t + 1) * 128],
                in_=ps_t.rearrange("p (c j) -> p c j", j=128), func=AF.Copy), inc=ev)
            p.wait("vector", ev)

        if debug & 2:
            p.wait("gpsimd", ev)
            p.dma("gpsimd", dbg_xn1, xn1_fm, st)
            p.wait("vector", st)
        barrier()
        qkv1 = big1.bitcast(BF16)
        wblks = [big3[:, 0:2048].bitcast(BF16), wpool[:, 0:4096]]
        bias5s = [q512b, q512c]
        prodv = big3[:, 0:2048]
        prodg = big3[:, 2048:3072]
        ctxsf = big3[:, 3072:4096]
        xbC = q512b.bitcast(BF16)[:, 0:1024]
        scoresb = q512c[:, 0:128]
        smx = q512c[:, 128:144]
        ssm = q512c[:, 144:160]
        src_ = q512c[:, 160:176]
        ctx_fm = big2.bitcast(BF16)[:, 0:16384]
        wb_used = [0, 0]; wb_ld = [0, 0]; b5_used = [0, 0]
        pp_free = [0, 0]
        p.wait("gpsimd", p.sems["lncp"])
        p.wait("gpsimd", ev)
        for pb in range(2):
            if pb == 1:
                p.wait("gpsimd", cp)
            for oc in range(6):
                sl = oc % 2
                p.wait("gpsimd", mm, at=wb_used[sl])
                p.dma("gpsimd", wblks[sl], ipw_t[oc * 128:(oc + 1) * 128, :], sg[sl])
                p.wait("gpsimd", ev, at=b5_used[sl])
                p.dma("gpsimd", bias5s[sl].rearrange("p (o j) -> p o j", o=1),
                      ipb_row[0:1, oc * 512:(oc + 1) * 512].rearrange(
                          "a (o j) -> a o j", o=1).broadcast_to([128, 1, 512]), sw[sl])
                p.wait("tensor", sg[sl])
                for si in range(8):
                    t_idx = si * 2 + pb
                    psl = si % 2
                    pp = ps_a if psl == 0 else ps_b
                    p.wait("tensor", ev, at=pp_free[psl])
                    for kc in range(KC):
                        p.emit("tensor", lambda e, pp=pp, kc=kc, t_idx=t_idx, sl=sl: e.matmul(
                            pp, xn1_fm.rearrange("p (c j) -> p c j", j=R)[:, kc, t_idx * 128:(t_idx + 1) * 128],
                            wblks[sl][:, kc * 512:(kc + 1) * 512],
                            start=(kc == 0), stop=(kc == KC - 1)), inc=(mm if kc == KC - 1 else None))
                    if si == 7:
                        wb_used[sl] = mm.n
                    p.wait("vector", mm, at=mm.n)
                    p.wait("vector", sw[sl])
                    p.emit("vector", lambda e, pp=pp, si=si, oc=oc, sl=sl: e.tensor_tensor(
                        out=qkv1.rearrange("p (s f) -> p s f", f=3072)[:, si, oc * 512:(oc + 1) * 512],
                        in0=pp, in1=bias5s[sl], op=ALU.add), inc=ev)
                    pp_free[psl] = ev.n
                    if si == 7:
                        b5_used[sl] = ev.n
            if debug & 4:
                p.wait("gpsimd", ev)
                if pb == 1:
                    p.dma("gpsimd", dbg_qkv1, qkv1, st)
                    p.wait("vector", st)
            qv = qkv1.rearrange("p (s th d) -> p s th d", th=3 * NH, d=64)
            p.wait("vector", ev)
            p.wait("gpsimd", ev)
            sc_rdy = 0
            for si in range(8):
                for tp in range(4):
                    p.emit("vector", lambda e, si=si, tp=tp, qv=qv: e.tensor_tensor(
                        out=prodv.rearrange("p (t2 h d) -> p t2 h d", t2=2, d=64),
                        in0=qv[:, si:si + 1, 0:NH, :].broadcast_to([128, 2, NH, 64]),
                        in1=qv[:, 2 * tp:2 * tp + 2, NH:2 * NH, :], op=ALU.mult))
                    p.emit("vector", lambda e, tp=tp: e.tensor_reduce(
                        out=scoresb.rearrange("p (h t) -> p h t", t=8)[:, :, 2 * tp:2 * tp + 2],
                        in_=prodv.rearrange("p (t2 h d) -> p h t2 d", t2=2, d=64), axis=AX.X, op=ALU.add))
                p.drain("vector")
                p.emit("vector", lambda e: e.tensor_reduce(
                    out=smx, in_=scoresb.rearrange("p (h t) -> p h t", t=8), axis=AX.X, op=ALU.max))
                p.drain("vector")
                p.emit("vector", lambda e: e.tensor_tensor(
                    out=scoresb.rearrange("p (h t) -> p h t", t=8), in0=scoresb.rearrange("p (h t) -> p h t", t=8),
                    in1=smx.rearrange("p (h o) -> p h o", o=1).broadcast_to([128, 16, 8]), op=ALU.subtract), inc=cp)
                p.wait("scalar", cp, at=cp.n)
                p.emit("scalar", lambda e: e.activation(out=scoresb, in_=scoresb, func=AF.Exp, scale=0.125), inc=cp)
                p.wait("vector", cp, at=cp.n)
                p.emit("vector", lambda e: e.tensor_reduce(
                    out=ssm, in_=scoresb.rearrange("p (h t) -> p h t", t=8), axis=AX.X, op=ALU.add))
                p.drain("vector")
                p.emit("vector", lambda e: e.reciprocal(out=src_, in_=ssm))
                p.drain("vector")
                p.emit("vector", lambda e: e.tensor_tensor(
                    out=scoresb.rearrange("p (h t) -> p h t", t=8), in0=scoresb.rearrange("p (h t) -> p h t", t=8),
                    in1=src_.rearrange("p (h o) -> p h o", o=1).broadcast_to([128, 16, 8]), op=ALU.mult), inc=cp)
                sc_rdy = cp.n
                ctx2 = prodv[:, 0:1024]
                prodv2 = prodv[:, 1024:2048]
                p.wait("gpsimd", cp, at=sc_rdy)
                for t in range(4):
                    p.emit("gpsimd", lambda e, t=t, qv=qv: e.tensor_tensor(
                        out=prodg[:, 0:1024].rearrange("p (h d) -> p h d", d=64),
                        in0=qv[:, t, 2 * NH:3 * NH, :],
                        in1=scoresb.rearrange("p (h t) -> p h t", t=8)[:, :, t:t + 1].broadcast_to([128, 16, 64]),
                        op=ALU.mult))
                    if t == 0:
                        p.emit("gpsimd", lambda e: e.tensor_copy(out=ctxsf, in_=prodg[:, 0:1024]))
                    elif t < 3:
                        p.emit("gpsimd", lambda e: e.tensor_tensor(out=ctxsf, in0=ctxsf, in1=prodg[:, 0:1024], op=ALU.add))
                    else:
                        p.emit("gpsimd", lambda e: e.tensor_tensor(out=ctxsf, in0=ctxsf, in1=prodg[:, 0:1024], op=ALU.add), inc=cp)
                g_done = cp.n
                for t in range(4, 8):
                    p.emit("vector", lambda e, t=t, qv=qv, prodv2=prodv2: e.tensor_tensor(
                        out=prodv2.rearrange("p (h d) -> p h d", d=64),
                        in0=qv[:, t, 2 * NH:3 * NH, :],
                        in1=scoresb.rearrange("p (h t) -> p h t", t=8)[:, :, t:t + 1].broadcast_to([128, 16, 64]),
                        op=ALU.mult))
                    p.drain("vector")
                    if t == 4:
                        p.emit("vector", lambda e, ctx2=ctx2, prodv2=prodv2: e.tensor_copy(out=ctx2, in_=prodv2))
                    else:
                        p.emit("vector", lambda e, ctx2=ctx2, prodv2=prodv2: e.tensor_tensor(
                            out=ctx2, in0=ctx2, in1=prodv2, op=ALU.add))
                    p.drain("vector")
                p.wait("vector", cp, at=g_done)
                p.emit("vector", lambda e, ctx2=ctx2: e.tensor_tensor(out=ctxsf, in0=ctxsf, in1=ctx2, op=ALU.add))
                p.drain("vector")
                p.emit("vector", lambda e: e.tensor_copy(out=xbC, in_=ctxsf), inc=cpvc)
                cast_done = cpvc.n
                p.wait("tensor", cpvc, at=cast_done)
                p.wait("tensor", ev)
                t_idx = si * 2 + pb
                for c in range(KC):
                    p.emit("tensor", lambda e, c=c: e.transpose(
                        ps_t[:, c * 128:(c + 1) * 128], xbC[:, c * 128:(c + 1) * 128], idb),
                        inc=(mm if c == KC - 1 else None))
                p.wait("scalar", mm, at=mm.n)
                p.emit("scalar", lambda e, t_idx=t_idx: e.activation(
                    out=ctx_fm.rearrange("p (c j) -> p c j", j=R)[:, :, t_idx * 128:(t_idx + 1) * 128],
                    in_=ps_t.rearrange("p (c j) -> p c j", j=128), func=AF.Copy), inc=ev)
                p.wait("gpsimd", cpvc, at=cast_done)
                p.wait("gpsimd", ev)
                p.wait("vector", ev)
                p.wait("vector", cp)

        if debug & 8:
            p.wait("gpsimd", ev)
            p.dma("gpsimd", dbg_ctx, ctx_fm, st)
            p.wait("vector", st)
        barrier()
        for oc in range(8):
            p.wait("gpsimd", mm)
            p.dma("gpsimd", oplhs, opw_t[oc * 128:(oc + 1) * 128, :], ld)
            p.wait("tensor", ld)
            for tq in range(4):
                pp = ps_a if tq % 2 == 0 else ps_b
                p.wait("tensor", ev)
                for kc in range(KC):
                    p.emit("tensor", lambda e, pp=pp, kc=kc, oc=oc, tq=tq: e.matmul(
                        pp, oplhs[:, kc * 128:(kc + 1) * 128],
                        ctx_fm[:, kc * R + tq * 512: kc * R + (tq + 1) * 512],
                        start=(kc == 0), stop=(kc == KC - 1)), inc=(mm if kc == KC - 1 else None))
                p.wait("scalar", mm)
                p.emit("scalar", lambda e, pp=pp, oc=oc: e.activation(
                    out=ev512, in_=pp, func=AF.Identity, bias=opb[:, oc:oc + 1], scale=1.0), inc=ev)
                p.wait("tensor", ev)
                for tb in range(4):
                    p.emit("tensor", lambda e, tb=tb: e.transpose(
                        ps_tf[:, tb * 128:(tb + 1) * 128], ev512[:, tb * 128:(tb + 1) * 128], idf),
                        inc=(mm if tb == 3 else None))
                p.wait("vector", mm)
                p.emit("vector", lambda e, tq=tq, oc=oc: e.tensor_tensor(
                    out=x1.rearrange("p (t j) -> p t j", j=D)[:, tq * 4:(tq + 1) * 4, oc * 128:(oc + 1) * 128],
                    in0=x1.rearrange("p (t j) -> p t j", j=D)[:, tq * 4:(tq + 1) * 4, oc * 128:(oc + 1) * 128],
                    in1=ps_tf.rearrange("p (t j) -> p t j", j=128), op=ALU.add), inc=ev)
                p.wait("tensor", ev)

        if debug & 16:
            p.wait("sync", ev)
            for t in range(NT):
                p.dma("sync", dbg_x1d1[t * 128:(t + 1) * 128, :], x1[:, t * D:(t + 1) * D], st)
            p.wait("vector", st)
        barrier()
        prod = prodv
        p.wait("gpsimd", mm)
        p.dma("gpsimd", rtb, rt_t, ld)
        p.wait("tensor", ld)
        for t in range(NT):
            pp = ps_a if t % 2 == 0 else ps_b
            p.wait("tensor", ev)
            for kc in range(KC):
                p.emit("tensor", lambda e, pp=pp, kc=kc, t=t: e.matmul(
                    pp[:, 0:32],
                    xn1_fm.rearrange("p (c j) -> p c j", j=R)[:, kc, t * 128:(t + 1) * 128],
                    rtb[:, kc * 32:(kc + 1) * 32],
                    start=(kc == 0), stop=(kc == KC - 1)), inc=(mm if kc == KC - 1 else None))
            p.wait("vector", mm)
            wsl = wall[:, 0:32]
            p.emit("vector", lambda e, pp=pp, wsl=wsl: e.tensor_tensor(out=prod[:, 0:32], in0=pp[:, 0:32], in1=rbb, op=ALU.add), inc=ev)
            p.drain("vector")
            p.emit("vector", lambda e: e.reduce_max(out=stat[:, 6:7], in_=prod[:, 0:32], axis=AX.X))
            p.drain("vector")
            p.emit("vector", lambda e: e.tensor_scalar(out=prod[:, 0:32], in0=prod[:, 0:32],
                                                       scalar1=stat[:, 6:7], scalar2=None, op0=ALU.subtract), inc=cp)
            p.wait("scalar", cp)
            p.emit("scalar", lambda e: e.activation(out=prod[:, 0:32], in_=prod[:, 0:32], func=AF.Exp), inc=cp)
            p.wait("vector", cp)
            p.emit("vector", lambda e: e.reduce_sum(out=stat[:, 6:7], in_=prod[:, 0:32], axis=AX.X))
            p.drain("vector")
            p.emit("vector", lambda e: e.reciprocal(out=stat[:, 7:8], in_=stat[:, 6:7]))
            p.drain("vector")
            p.emit("vector", lambda e: e.tensor_scalar(out=prod[:, 0:32], in0=prod[:, 0:32],
                                                       scalar1=stat[:, 7:8], scalar2=None, op0=ALU.mult))
            p.drain("vector")
            p.emit("vector", lambda e: e.tensor_copy(out=prod[:, 32:64], in_=prod[:, 0:32]))
            p.emit("vector", lambda e, wsl=wsl: e.memset(wsl, 0.0))
            p.drain("vector")
            for it in range(4):
                p.emit("vector", lambda e: e.reduce_max(out=stat[:, 6:7], in_=prod[:, 32:64], axis=AX.X))
                p.drain("vector")
                p.emit("vector", lambda e: e.tensor_scalar(out=prod[:, 64:96], in0=prod[:, 32:64],
                                                           scalar1=stat[:, 6:7], scalar2=None, op0=ALU.is_ge))
                p.drain("vector")
                p.emit("vector", lambda e: e.tensor_tensor(out=prod[:, 96:128], in0=prod[:, 64:96], in1=prod[:, 0:32], op=ALU.mult))
                p.drain("vector")
                p.emit("vector", lambda e, wsl=wsl: e.tensor_tensor(out=wsl, in0=wsl, in1=prod[:, 96:128], op=ALU.add))
                p.emit("vector", lambda e: e.tensor_scalar(out=prod[:, 96:128], in0=prod[:, 64:96],
                                                           scalar1=-1.0, scalar2=1.0, op0=ALU.mult, op1=ALU.add))
                p.drain("vector")
                p.emit("vector", lambda e: e.tensor_tensor(out=prod[:, 32:64], in0=prod[:, 32:64], in1=prod[:, 96:128], op=ALU.mult))
                p.drain("vector")
                p.emit("vector", lambda e: e.tensor_tensor(out=prod[:, 32:64], in0=prod[:, 32:64], in1=prod[:, 64:96], op=ALU.subtract))
                p.drain("vector")
            p.emit("vector", lambda e, wsl=wsl: e.reduce_sum(out=stat[:, 6:7], in_=wsl, axis=AX.X))
            p.drain("vector")
            p.emit("vector", lambda e: e.tensor_scalar(out=stat[:, 6:7], in0=stat[:, 6:7],
                                                       scalar1=1e-12, scalar2=None, op0=ALU.max))
            p.drain("vector")
            p.emit("vector", lambda e: e.reciprocal(out=stat[:, 7:8], in_=stat[:, 6:7]))
            p.drain("vector")
            p.emit("vector", lambda e, wsl=wsl: e.tensor_scalar(out=wsl, in0=wsl,
                                                                scalar1=stat[:, 7:8], scalar2=None, op0=ALU.mult), inc=cp)
            p.wait("tensor", cp)
            p.wait("tensor", aux)
            p.emit("tensor", lambda e, wsl=wsl: e.transpose(ps_tf[0:32, 0:128], wsl, idf), inc=mm)
            p.wait("vector", mm)
            p.wait("vector", st)
            p.emit("vector", lambda e: e.tensor_copy(out=wtstg, in_=ps_tf[0:32, 0:128]), inc=aux)
            p.wait("sync", aux)
            p.dma("sync", wt_dram.ap()[0:32, t * 128:(t + 1) * 128], wtstg, st)
        p.wait("vector", st)

        barrier()
        h_fm = big2.bitcast(BF16)[:, 0:32 * 512]
        sils = [big3[:, 0:512], big3[:, 512:1024]]
        h1s = [big3[:, 1024:1536], big3[:, 1536:2048]]
        gus = [wpool[:, 0:2048], wpool[:, 2048:4096]]
        wbcs = [big1[:, 0:512], big1[:, 512:1024]]
        dls = [big1[:, 1024:3072].bitcast(BF16), big1[:, 3072:5120].bitcast(BF16)]
        gu_used = [0, 0]; wbc_used = [0, 0]; gu_ld = [0, 0]; wbc_ld = [0, 0]
        pe_free = [0, 0]
        dl_used = [0, 0]; dl_ld = [0, 0]; dn_free = [0, 0]
        if debug & 32:
            p.wait("sync", st)
            p.dma("sync", dbg_wt, wt_dram.ap(), st)
        p.wait("gpsimd", st)
        for tq in range(4):
            p.wait("vector", mm)
            p.wait("gpsimd", mm)
            p.wait("tensor", ev)
            for ei in range(32):
                sl = ei % 2
                pg = ps_a if sl == 0 else ps_c
                pu = ps_b if sl == 0 else ps_d
                p.wait("gpsimd", mm, at=gu_used[sl])
                p.dma("gpsimd", gus[sl][:, 0:1024], gat[ei * 128:(ei + 1) * 128, :], sg[sl])
                p.dma("gpsimd", gus[sl][:, 1024:2048], upt[ei * 128:(ei + 1) * 128, :], sg[sl])
                p.wait("gpsimd", cp, at=wbc_used[sl])
                p.dma("gpsimd", wbcs[sl].rearrange("p (o j) -> p o j", o=1),
                      wt_dram.ap()[ei:ei + 1, tq * 512:(tq + 1) * 512].rearrange(
                          "a (o j) -> a o j", o=1).broadcast_to([128, 1, 512]), sw[sl])
                p.wait("tensor", sg[sl])
                p.wait("tensor", cp, at=pe_free[sl])
                for kc in range(KC):
                    p.emit("tensor", lambda e, pg=pg, kc=kc, tq=tq, sl=sl: e.matmul(
                        pg, gus[sl][:, kc * 128:(kc + 1) * 128],
                        xn1_fm[:, kc * R + tq * 512: kc * R + (tq + 1) * 512],
                        start=(kc == 0), stop=(kc == KC - 1)))
                    p.emit("tensor", lambda e, pu=pu, kc=kc, tq=tq, sl=sl: e.matmul(
                        pu, gus[sl][:, 1024 + kc * 128: 1024 + (kc + 1) * 128],
                        xn1_fm[:, kc * R + tq * 512: kc * R + (tq + 1) * 512],
                        start=(kc == 0), stop=(kc == KC - 1)), inc=(mm if kc == KC - 1 else None))
                gu_used[sl] = mm.n
                p.wait("scalar", mm, at=mm.n)
                p.emit("scalar", lambda e, pg=pg, sl=sl: e.activation(out=sils[sl], in_=pg, func=AF.Silu), inc=cp)
                p.wait("vector", cp, at=cp.n)
                p.wait("vector", sw[sl])
                p.emit("vector", lambda e, pu=pu, sl=sl: e.tensor_tensor(out=h1s[sl], in0=pu, in1=sils[sl], op=ALU.mult))
                p.emit("vector", lambda e, ei=ei, sl=sl: e.tensor_tensor(
                    out=h_fm[:, ei * 512:(ei + 1) * 512], in0=h1s[sl], in1=wbcs[sl], op=ALU.mult), inc=cp)
                wbc_used[sl] = cp.n
                pe_free[sl] = cp.n
            p.wait("tensor", cp)
            if (debug & 64) and tq == 0:
                p.wait("gpsimd", cp)
                p.dma("gpsimd", dbg_hfm, h_fm, st)
                p.wait("vector", st)
            for oc in range(8):
                sl = oc % 2
                pp = ps_a if sl == 0 else ps_b
                p.wait("gpsimd", mm, at=dl_used[sl])
                p.dma("gpsimd", dls[sl], dat[oc * 128:(oc + 1) * 128, :], sd[sl])
                p.wait("tensor", sd[sl])
                p.wait("tensor", ev, at=dn_free[sl])
                for kc in range(32):
                    p.emit("tensor", lambda e, pp=pp, kc=kc, sl=sl: e.matmul(
                        pp, dls[sl][:, kc * 128:(kc + 1) * 128], h_fm[:, kc * 512:(kc + 1) * 512],
                        start=(kc == 0), stop=(kc == 31)), inc=(mm if kc == 31 else None))
                dl_used[sl] = mm.n
                p.wait("scalar", mm, at=mm.n)
                p.wait("scalar", ev)
                p.emit("scalar", lambda e, pp=pp: e.activation(out=ev512, in_=pp, func=AF.Copy), inc=ev)
                p.wait("tensor", ev, at=ev.n)
                for tb in range(4):
                    p.emit("tensor", lambda e, tb=tb: e.transpose(
                        ps_tf[:, tb * 128:(tb + 1) * 128], ev512[:, tb * 128:(tb + 1) * 128], idf),
                        inc=(mm if tb == 3 else None))
                p.wait("vector", mm, at=mm.n)
                p.emit("vector", lambda e, tq=tq, oc=oc: e.tensor_tensor(
                    out=x1.rearrange("p (t j) -> p t j", j=D)[:, tq * 4:(tq + 1) * 4, oc * 128:(oc + 1) * 128],
                    in0=x1.rearrange("p (t j) -> p t j", j=D)[:, tq * 4:(tq + 1) * 4, oc * 128:(oc + 1) * 128],
                    in1=ps_tf.rearrange("p (t j) -> p t j", j=128), op=ALU.add), inc=ev)
                dn_free[sl] = ev.n
                p.wait("tensor", ev)

        if debug & 128:
            p.wait("sync", ev)
            for t in range(NT):
                p.dma("sync", dbg_x1d2[t * 128:(t + 1) * 128, :], x1[:, t * D:(t + 1) * D], st)
            p.wait("vector", st)
        barrier()
        s_h = big3.bitcast(BF16)[:, 0:4 * R]
        for oc in range(4):
            p.wait("gpsimd", mm)
            p.dma("gpsimd", sglhs, sgt[oc * 128:(oc + 1) * 128, :], ld)
            p.dma("gpsimd", sulhs, sut[oc * 128:(oc + 1) * 128, :], ld)
            p.wait("tensor", ld)
            for tq in range(4):
                pg = ps_a if tq % 2 == 0 else ps_c
                pu = ps_b if tq % 2 == 0 else ps_d
                p.wait("tensor", cp)
                for kc in range(KC):
                    p.emit("tensor", lambda e, pg=pg, kc=kc, tq=tq: e.matmul(
                        pg, sglhs[:, kc * 128:(kc + 1) * 128],
                        xn1_fm[:, kc * R + tq * 512: kc * R + (tq + 1) * 512],
                        start=(kc == 0), stop=(kc == KC - 1)))
                    p.emit("tensor", lambda e, pu=pu, kc=kc, tq=tq: e.matmul(
                        pu, sulhs[:, kc * 128:(kc + 1) * 128],
                        xn1_fm[:, kc * R + tq * 512: kc * R + (tq + 1) * 512],
                        start=(kc == 0), stop=(kc == KC - 1)), inc=(mm if kc == KC - 1 else None))
                p.wait("scalar", mm)
                p.emit("scalar", lambda e, pg=pg: e.activation(out=q512b, in_=pg, func=AF.Silu), inc=cp)
                p.wait("vector", cp)
                p.emit("vector", lambda e, pu=pu, oc=oc, tq=tq: e.tensor_tensor(
                    out=s_h[:, oc * R + tq * 512: oc * R + (tq + 1) * 512],
                    in0=pu, in1=q512b, op=ALU.mult), inc=cp)
        p.wait("tensor", cp)
        for oc in range(8):
            p.wait("gpsimd", mm)
            p.dma("gpsimd", sdlhs, sdt[oc * 128:(oc + 1) * 128, :], ld)
            p.wait("tensor", ld)
            for tq in range(4):
                pp = ps_a if tq % 2 == 0 else ps_b
                p.wait("tensor", ev)
                for kc in range(4):
                    p.emit("tensor", lambda e, pp=pp, kc=kc, tq=tq: e.matmul(
                        pp, sdlhs[:, kc * 128:(kc + 1) * 128],
                        s_h[:, kc * R + tq * 512: kc * R + (tq + 1) * 512],
                        start=(kc == 0), stop=(kc == 3)), inc=(mm if kc == 3 else None))
                p.wait("scalar", mm)
                p.emit("scalar", lambda e, pp=pp: e.activation(out=ev512, in_=pp, func=AF.Copy), inc=ev)
                p.wait("tensor", ev)
                for tb in range(4):
                    p.emit("tensor", lambda e, tb=tb: e.transpose(
                        ps_tf[:, tb * 128:(tb + 1) * 128], ev512[:, tb * 128:(tb + 1) * 128], idf),
                        inc=(mm if tb == 3 else None))
                p.wait("vector", mm)
                p.emit("vector", lambda e, tq=tq, oc=oc: e.tensor_tensor(
                    out=x1.rearrange("p (t j) -> p t j", j=D)[:, tq * 4:(tq + 1) * 4, oc * 128:(oc + 1) * 128],
                    in0=x1.rearrange("p (t j) -> p t j", j=D)[:, tq * 4:(tq + 1) * 4, oc * 128:(oc + 1) * 128],
                    in1=ps_tf.rearrange("p (t j) -> p t j", j=128), op=ALU.add), inc=ev)
                p.wait("tensor", ev)

        barrier()
        p.wait("sync", ev)
        for t in range(NT):
            p.dma("sync", out_o[t * 128:(t + 1) * 128, :], x1[:, t * D:(t + 1) * D], st)
        p.wait("gpsimd", st)
        p.wait("sync", st)
        return [ld, cp, ev, st, aux, mm] + sg + sw + sd
